# version 07 (final)
# speedup vs baseline: 1.2716x; 1.0153x over previous
import numpy as np
import ml_dtypes
from contextlib import ExitStack

import concourse.bass as bass
import concourse.mybir as mybir
from concourse.bass_utils import run_bass_kernel_spmd
from concourse.alu_op_type import AluOpType as AOP

F32 = mybir.dt.float32
BF16 = mybir.dt.bfloat16
I32 = mybir.dt.int32
U32 = mybir.dt.uint32
U8 = mybir.dt.uint8
AF = mybir.ActivationFunctionType
ALU = mybir.AluOpType
AX = mybir.AxisListType

B = 4
S = 4096
D = 2048
NCORE = 8
TOK = 2048
NCH = D // 128
BIGNEG = -1e30

ENGINES = ("sync", "vector", "scalar", "tensor", "gpsimd")


class Rec:

    def __init__(self, nc, ctx):
        self.nc = nc
        self.ctx = ctx
        self.streams = {e: [] for e in ENGINES}
        self.sems = {}
        self.cnt = {}

    def sem(self, name):
        h = self.ctx.enter_context(self.nc.semaphore(name))
        self.sems[name] = h
        self.cnt[name] = 0
        return name

    def wait(self, eng, sname, val=None):
        v = self.cnt[sname] if val is None else val
        h = self.sems[sname]
        self.streams[eng].append(lambda E, h=h, v=v: E.wait_ge(h, v))

    SELF_SYNC = {"vector": "v", "scalar": "s"}

    def op(self, eng, fn, inc=None, k=1):
        own = self.SELF_SYNC.get(eng)
        if own is not None:
            self.wait(eng, own)
            if inc is None:
                inc, k = own, 1
        if inc is None:
            self.streams[eng].append(fn)
            return self.cnt
        h = self.sems[inc]
        self.streams[eng].append(lambda E, fn=fn, h=h, k=k: fn(E).then_inc(h, k))
        self.cnt[inc] += k
        return self.cnt[inc]

    def play(self, block):
        for ename in ENGINES:
            stream = self.streams[ename]

            def body(E, stream=stream):
                for f in stream:
                    f(E)

            getattr(block, ename)(body)


def _ln_emit(r, pre, nchunks, sqscr, psum_sum, psum_sumsq, ones_col, ones_row,
             rowscr, abc, bbc, psum_bc, ntok, eps=1e-5):
    d_total = float(nchunks * 128)
    r.wait("tensor", "v")
    for c in range(nchunks):
        if c >= 2:
            r.wait("scalar", "t")
        r.op("scalar", lambda E, c=c: E.activation(sqscr[:, c % 2, :], pre[:, c, :], AF.Square), inc="s")
        r.wait("tensor", "s")
        r.op("tensor", lambda E, c=c: E.matmul(psum_sum, ones_col[:], pre[:, c, :],
                                               start=(c == 0), stop=(c == nchunks - 1)))
        r.op("tensor", lambda E, c=c: E.matmul(psum_sumsq, ones_col[:], sqscr[:, c % 2, :],
                                               start=(c == 0), stop=(c == nchunks - 1)), inc="t")
    r.wait("vector", "t")
    m_row = rowscr[:, 0:ntok]
    msq_row = rowscr[:, 512:512 + ntok]
    var_row = rowscr[:, 1024:1024 + ntok]
    b_row = rowscr[:, 1536:1536 + ntok]
    r.op("vector", lambda E: E.tensor_scalar_mul(m_row, psum_sum[:, 0:ntok], 1.0 / d_total), inc="v")
    r.op("vector", lambda E: E.tensor_scalar_mul(msq_row, psum_sumsq[:, 0:ntok], 1.0 / d_total), inc="v")
    r.op("vector", lambda E: E.tensor_mul(var_row, m_row, m_row), inc="v")
    r.op("vector", lambda E: E.tensor_sub(var_row, msq_row, var_row), inc="v")
    r.op("vector", lambda E: E.tensor_scalar_add(var_row, var_row, eps), inc="v")
    r.wait("scalar", "v")
    r.op("scalar", lambda E: E.activation(var_row, var_row, AF.Ln), inc="s")
    r.op("scalar", lambda E: E.activation(var_row, var_row, AF.Exp, scale=-0.5), inc="s")
    r.wait("vector", "s")
    r.op("vector", lambda E: E.tensor_mul(b_row, m_row, var_row), inc="v")
    r.op("vector", lambda E: E.tensor_scalar_mul(b_row, b_row, -1.0), inc="v")
    r.wait("tensor", "v")
    r.op("tensor", lambda E: E.matmul(psum_bc[0][:, 0:ntok], ones_row[:], var_row,
                                      start=True, stop=True), inc="t")
    r.op("tensor", lambda E: E.matmul(psum_bc[1][:, 0:ntok], ones_row[:], b_row,
                                      start=True, stop=True), inc="t")
    r.wait("vector", "t")
    r.op("vector", lambda E: E.tensor_copy(abc[:, 0:ntok], psum_bc[0][:, 0:ntok]), inc="v")
    r.op("vector", lambda E: E.tensor_copy(bbc[:, 0:ntok], psum_bc[1][:, 0:ntok]), inc="v")
    for c in range(nchunks):
        r.op("vector", lambda E, c=c: E.tensor_mul(pre[:, c, :], pre[:, c, :], abc[:, 0:ntok]))
        r.op("vector", lambda E, c=c: E.tensor_add(pre[:, c, :], pre[:, c, :], bbc[:, 0:ntok]),
             inc="v")


def _cplx_mul_cols(r, out, in_, are, aim, scr):
    w = in_.shape[-1]
    r.op("vector", lambda E: E.tensor_scalar_mul(scr[0:32, 0:w], in_[0:32, 0:w], are))
    r.op("vector", lambda E: E.tensor_scalar_mul(scr[32:64, 0:w], in_[32:64, 0:w], aim))
    r.op("vector", lambda E: E.tensor_sub(out[0:32, 0:w], scr[0:32, 0:w], scr[32:64, 0:w]))
    r.op("vector", lambda E: E.tensor_scalar_mul(scr[0:32, 0:w], in_[0:32, 0:w], aim))
    r.op("vector", lambda E: E.tensor_scalar_mul(scr[32:64, 0:w], in_[32:64, 0:w], are))
    r.op("vector", lambda E: E.tensor_add(out[32:64, 0:w], scr[0:32, 0:w], scr[32:64, 0:w]),
         inc="v")


def build_launch_A(dbg=False):
    nc = bass.Bass("TRN2", target_bir_lowering=False, debug=False)
    u_pref = nc.dram_tensor("u_pref", [D, S], F32, kind="ExternalInput").ap()
    lam_ur = nc.dram_tensor("lam_ur", [1, 32], F32, kind="ExternalInput").ap()
    lam_im = nc.dram_tensor("lam_im", [1, 32], F32, kind="ExternalInput").ap()
    logdt = nc.dram_tensor("logdt", [1, 64], F32, kind="ExternalInput").ap()
    bre = nc.dram_tensor("bre", [32, D], F32, kind="ExternalInput").ap()
    bim = nc.dram_tensor("bim", [32, D], F32, kind="ExternalInput").ap()
    cret = nc.dram_tensor("cret", [32, D], F32, kind="ExternalInput").ap()
    cimt = nc.dram_tensor("cimt", [32, D], F32, kind="ExternalInput").ap()
    dcol = nc.dram_tensor("dcol", [128, NCH], F32, kind="ExternalInput").ap()
    x1 = nc.dram_tensor("x1", [D, TOK], F32, kind="ExternalOutput").ap()
    if dbg:
        dbg_rows = nc.dram_tensor("dbg_rows", [1, 32 * 24], F32, kind="ExternalOutput").ap()
        dbg_scal6 = nc.dram_tensor("dbg_scal6", [32, 6], F32, kind="ExternalOutput").ap()
        dbg_tp = nc.dram_tensor("dbg_tp", [32, 2, 128], F32, kind="ExternalOutput").ap()
        dbg_tm = nc.dram_tensor("dbg_tm", [32, 2, 128], F32, kind="ExternalOutput").ap()
        dbg_bu = nc.dram_tensor("dbg_bu", [32, 2, S], F32, kind="ExternalOutput").ap()
        dbg_ubf = nc.dram_tensor("dbg_ubf", [128, NCH, 512], mybir.dt.bfloat16, kind="ExternalOutput").ap()
        dbg_cum = nc.dram_tensor("dbg_cum", [32, 2, S], F32, kind="ExternalOutput").ap()
        dbg_xs = nc.dram_tensor("dbg_xs", [32, 2, S], mybir.dt.bfloat16, kind="ExternalOutput").ap()
        dbg_pre = nc.dram_tensor("dbg_pre", [128, NCH, 512], F32, kind="ExternalOutput").ap()

    u3 = u_pref.rearrange("(c p) t -> c p t", p=128)

    with ExitStack() as ctx:
        sb = lambda n, s, d=F32: ctx.enter_context(nc.sbuf_tensor(n, s, d))
        r = Rec(nc, ctx)
        for s_ in ("qs", "qg", "v", "s", "t", "so"):
            r.sem(s_)

        row_par = sb("row_par", [1, 32 * 24])
        logdt_t = sb("logdt_t", [1, 64])
        lam_ur_row = sb("lam_ur_t", [1, 32])
        lam_im_row = sb("lam_im_t", [1, 32])
        ki32 = sb("ki32", [1, 64], I32)
        dcol_t = sb("dcol_t", [128, NCH])
        d1col = sb("d1col", [128, NCH])
        idv = sb("idv", [128, 128])
        ident = sb("ident", [128, 128])
        ones_col = sb("ones_col", [128, 1])
        ones_row = sb("ones_row", [1, 128])
        scal6 = sb("scal6", [32, 6])
        a128c = sb("a128c", [32, 2])
        tp = sb("tp", [32, 2, 128])
        tm = sb("tm", [32, 2, 128])
        tcol = sb("tcol", [32, 2])
        wcol = sb("wcol", [32, 2])
        ccol = sb("ccol", [32, 4])
        ee = sb("ee", [32, 2, 32])
        bpack = sb("bpack", [32, 4, D])
        bre_t = bpack[:, 0, :]
        bim_t = bpack[:, 1, :]
        cret_t = bpack[:, 2, :]
        cimt_t = bpack[:, 3, :]
        d0s = bpack[:, 0:2, :].rearrange("n a b -> n (a b)")
        sca = bpack[:, 2, :]
        scb = bpack[:, 3, :]
        bbarp = sb("bbarp", [32, 2, D])
        bbre = bbarp[:, 0, :]
        bbim = bbarp[:, 1, :]
        scr2 = sb("scr2", [32, D])
        xs_bf = bbarp[:].rearrange("n a b -> n (a b)").bitcast(BF16).rearrange("n (a b) -> n a b", a=2)
        bbar_Tre = sb("bbar_Tre", [128, NCH, 32], BF16)
        bbar_Tim = sb("bbar_Tim", [128, NCH, 32], BF16)
        cpack_re = sb("cpack_re", [32, NCH, 128], BF16)
        cpack_im = sb("cpack_im", [32, NCH, 128], BF16)
        u_bf = sb("u_bf", [128, NCH, 512], BF16)
        bu = sb("bu", [32, 2, S])
        cum = sb("cum", [32, 2, S])
        u32b = sb("u32b", [128, 2, 512])
        pre = sb("pre", [128, NCH, 512])
        sqscr = sb("sqscr", [128, 2, 512])
        rowscr = sb("rowscr", [1, 2048])
        abc = sb("abc", [128, 512])
        bbc = sb("bbc", [128, 512])

        PS = ctx.enter_context(nc.psum_tensor("PS", [128, 4096], F32))
        ps_bu_re = PS[0:32, 0:512]
        ps_bu_im = PS[0:32, 512:1024]
        ps_y = [PS[:, 0:512], PS[:, 512:1024]]
        ps_tr = PS[:, 1024:1056]
        ps_6 = PS[0:32, 1536:1542]
        ps_sum = PS[0:1, 2048:2560]
        ps_sumsq = PS[0:1, 2560:3072]
        ps_bc = [PS[:, 3072:3584], PS[:, 3584:4096]]

        ROWS = {}
        for i, nm in enumerate(
            ["delta", "ab", "en", "ln1", "mx", "sp", "lamre", "rr", "th", "y",
             "kf", "fr", "ang", "yc", "frc", "angc", "sinth", "costh", "er",
             "are", "aim", "asq", "den", "tmp"]):
            ROWS[nm] = row_par[:, 32 * i:32 * i + 32]

        block = ctx.enter_context(nc.Block())

        for (dst, src) in [(lam_ur_row[:], lam_ur), (lam_im_row[:], lam_im),
                           (logdt_t[:], logdt), (bre_t, bre), (bim_t, bim),
                           (cret_t, cret), (cimt_t, cimt), (dcol_t[:], dcol)]:
            r.op("sync", lambda E, dst=dst, src=src: E.dma_start(out=dst, in_=src), inc="qs", k=16)
        r.op("gpsimd", lambda E: E.iota(idv[:], pattern=[[-1, 128]], channel_multiplier=1,
                                        allow_small_or_imprecise_dtypes=True), inc="qg", k=1)

        r.wait("vector", "qs")
        r.wait("vector", "qg")
        r.wait("scalar", "qs")
        V, Sc = "vector", "scalar"
        R = ROWS
        r.op(V, lambda E: E.tensor_single_scalar(ident[:], idv[:], 0.0, ALU.is_equal), inc="v")
        r.op(V, lambda E: E.memset(ones_col[:], 1.0), inc="v")
        r.op(V, lambda E: E.memset(ones_row[:], 1.0), inc="v")
        r.op(V, lambda E: E.tensor_scalar_add(d1col[:], dcol_t[:], 1.0), inc="v")
        r.op(Sc, lambda E: E.activation(R["delta"], logdt_t[:, 0:32], AF.Exp), inc="s")
        r.op(Sc, lambda E: E.activation(R["ab"], lam_ur_row[:], AF.Abs), inc="s")
        r.op(Sc, lambda E: E.activation(R["en"], R["ab"], AF.Exp, scale=-1.0), inc="s")
        r.op(Sc, lambda E: E.activation(R["ln1"], R["en"], AF.Ln, bias=1.0), inc="s")
        r.wait(V, "s")
        r.op(V, lambda E: E.tensor_scalar_max(R["mx"], lam_ur_row[:], 0.0), inc="v")
        r.op(V, lambda E: E.tensor_add(R["sp"], R["mx"], R["ln1"]), inc="v")
        r.wait(Sc, "v")
        r.op(Sc, lambda E: E.activation(R["lamre"], R["sp"], AF.Copy, scale=-1.0), inc="s")
        r.wait(V, "s")
        r.op(V, lambda E: E.tensor_scalar_add(R["lamre"], R["lamre"], -0.0101), inc="v")
        r.op(V, lambda E: E.tensor_mul(R["rr"], R["lamre"], R["delta"]), inc="v")
        r.op(V, lambda E: E.tensor_mul(R["th"], lam_im_row[:], R["delta"]), inc="v")
        inv2pi = 1.0 / (2.0 * np.pi)
        twopi = 2.0 * np.pi
        r.op(V, lambda E: E.tensor_scalar_mul(R["y"], R["th"], inv2pi), inc="v")
        r.op(V, lambda E: E.tensor_copy(ki32[:, 0:32], R["y"]), inc="v")
        r.wait(V, "v")
        r.op(V, lambda E: E.tensor_copy(R["kf"], ki32[:, 0:32]), inc="v")
        r.wait(V, "v")
        r.op(V, lambda E: E.tensor_sub(R["fr"], R["y"], R["kf"]), inc="v")
        r.op(V, lambda E: E.tensor_scalar_mul(R["ang"], R["fr"], twopi), inc="v")
        r.op(V, lambda E: E.tensor_scalar_add(R["yc"], R["y"], 0.25), inc="v")
        r.op(V, lambda E: E.tensor_copy(ki32[:, 32:64], R["yc"]), inc="v")
        r.wait(V, "v")
        r.op(V, lambda E: E.tensor_copy(R["kf"], ki32[:, 32:64]), inc="v")
        r.wait(V, "v")
        r.op(V, lambda E: E.tensor_sub(R["frc"], R["yc"], R["kf"]), inc="v")
        r.op(V, lambda E: E.tensor_scalar_mul(R["angc"], R["frc"], twopi), inc="v")
        r.wait(Sc, "v")
        r.op(Sc, lambda E: E.activation(R["sinth"], R["ang"], AF.Sin), inc="s")
        r.op(Sc, lambda E: E.activation(R["costh"], R["angc"], AF.Sin), inc="s")
        r.op(Sc, lambda E: E.activation(R["er"], R["rr"], AF.Exp), inc="s")
        r.wait(V, "s")
        r.op(V, lambda E: E.tensor_mul(R["are"], R["er"], R["costh"]), inc="v")
        r.op(V, lambda E: E.tensor_mul(R["aim"], R["er"], R["sinth"]), inc="v")
        r.op(V, lambda E: E.tensor_mul(R["asq"], R["are"], R["are"]), inc="v")
        r.op(V, lambda E: E.tensor_mul(R["tmp"], R["aim"], R["aim"]), inc="v")
        r.op(V, lambda E: E.tensor_add(R["asq"], R["asq"], R["tmp"]), inc="v")
        r.op(V, lambda E: E.reciprocal(R["asq"], R["asq"]), inc="v")
        r.wait(V, "v")
        r.op(V, lambda E: E.tensor_mul(R["en"], R["are"], R["asq"]), inc="v")
        r.op(V, lambda E: E.tensor_mul(R["ln1"], R["aim"], R["asq"]), inc="v")
        r.op(V, lambda E: E.tensor_scalar_mul(R["ln1"], R["ln1"], -1.0), inc="v")
        r.op(V, lambda E: E.tensor_mul(R["den"], R["lamre"], R["lamre"]), inc="v")
        r.op(V, lambda E: E.tensor_mul(R["tmp"], lam_im_row[:], lam_im_row[:]), inc="v")
        r.op(V, lambda E: E.tensor_add(R["den"], R["den"], R["tmp"]), inc="v")
        r.op(V, lambda E: E.reciprocal(R["den"], R["den"]), inc="v")
        r.wait(V, "v")
        r.op(V, lambda E: E.tensor_scalar_add(R["mx"], R["are"], -1.0), inc="v")
        r.op(V, lambda E: E.tensor_mul(R["sp"], R["mx"], R["lamre"]), inc="v")
        r.op(V, lambda E: E.tensor_mul(R["tmp"], R["aim"], lam_im_row[:]), inc="v")
        r.op(V, lambda E: E.tensor_add(R["sp"], R["sp"], R["tmp"]), inc="v")
        r.op(V, lambda E: E.tensor_mul(R["sp"], R["sp"], R["den"]), inc="v")
        r.op(V, lambda E: E.tensor_mul(R["ab"], R["aim"], R["lamre"]), inc="v")
        r.op(V, lambda E: E.tensor_mul(R["tmp"], R["mx"], lam_im_row[:]), inc="v")
        r.op(V, lambda E: E.tensor_sub(R["ab"], R["ab"], R["tmp"]), inc="v")
        r.op(V, lambda E: E.tensor_mul(R["ab"], R["ab"], R["den"]), inc="v")
        ones11 = ones_row[:, 0:1]
        r.wait("tensor", "v")
        for j, nm in enumerate(["sp", "ab", "are", "aim", "en", "ln1"]):
            r.op("tensor", lambda E, j=j, nm=nm: E.matmul(
                ps_6[:, j:j + 1], R[nm], ones11, start=True, stop=True),
                inc=("t" if j == 5 else None))
        r.wait(V, "t")
        r.op(V, lambda E: E.tensor_copy(scal6[:], ps_6), inc="v")
        s_re_c = scal6[:, 0:1]
        s_im_c = scal6[:, 1:2]
        a_re_c = scal6[:, 2:3]
        a_im_c = scal6[:, 3:4]
        ainv_re_c = scal6[:, 4:5]
        ainv_im_c = scal6[:, 5:6]

        def build_table(T, base_re_c, base_im_c):
            Tre = T[:, 0, :]
            Tim = T[:, 1, :]
            r.op(V, lambda E: E.memset(T[:, :, 0:1], 0.0), inc="v")
            r.op(V, lambda E: E.memset(Tre[:, 0:1], 1.0), inc="v")
            r.op(V, lambda E: E.tensor_copy(Tre[:, 1:2], base_re_c), inc="v")
            r.op(V, lambda E: E.tensor_copy(Tim[:, 1:2], base_im_c), inc="v")
            k = 2
            while k <= 64:
                r.op(V, lambda E, k=k: E.tensor_mul(tcol[:, 0:1], Tre[:, k - 1:k], base_re_c))
                r.op(V, lambda E, k=k: E.tensor_mul(wcol[:, 0:1], Tim[:, k - 1:k], base_im_c))
                r.op(V, lambda E: E.tensor_sub(tcol[:, 0:1], tcol[:, 0:1], wcol[:, 0:1]))
                r.op(V, lambda E, k=k: E.tensor_mul(tcol[:, 1:2], Tre[:, k - 1:k], base_im_c))
                r.op(V, lambda E, k=k: E.tensor_mul(wcol[:, 1:2], Tim[:, k - 1:k], base_re_c))
                r.op(V, lambda E: E.tensor_add(tcol[:, 1:2], tcol[:, 1:2], wcol[:, 1:2]))
                r.op(V, lambda E, k=k: E.tensor_scalar_mul(Tre[:, k:2 * k], Tre[:, 0:k], tcol[:, 0:1]))
                r.op(V, lambda E, k=k: E.tensor_scalar_mul(sca[:, 0:k], Tim[:, 0:k], tcol[:, 1:2]))
                r.op(V, lambda E, k=k: E.tensor_sub(Tre[:, k:2 * k], Tre[:, k:2 * k], sca[:, 0:k]))
                r.op(V, lambda E, k=k: E.tensor_scalar_mul(Tim[:, k:2 * k], Tim[:, 0:k], tcol[:, 0:1]))
                r.op(V, lambda E, k=k: E.tensor_scalar_mul(sca[:, 0:k], Tre[:, 0:k], tcol[:, 1:2]))
                r.op(V, lambda E, k=k: E.tensor_add(Tim[:, k:2 * k], Tim[:, k:2 * k], sca[:, 0:k]),
                     inc="v")
                k *= 2

        build_table(tp, a_re_c, a_im_c)
        build_table(tm, ainv_re_c, ainv_im_c)
        r.op(V, lambda E: E.tensor_mul(a128c[:, 0:1], tp[:, 0, 127:128], a_re_c))
        r.op(V, lambda E: E.tensor_mul(wcol[:, 0:1], tp[:, 1, 127:128], a_im_c))
        r.op(V, lambda E: E.tensor_sub(a128c[:, 0:1], a128c[:, 0:1], wcol[:, 0:1]))
        r.op(V, lambda E: E.tensor_mul(a128c[:, 1:2], tp[:, 0, 127:128], a_im_c))
        r.op(V, lambda E: E.tensor_mul(wcol[:, 1:2], tp[:, 1, 127:128], a_re_c))
        r.op(V, lambda E: E.tensor_add(a128c[:, 1:2], a128c[:, 1:2], wcol[:, 1:2]), inc="v")

        r.op(V, lambda E: E.tensor_scalar_mul(bbre, bre_t, s_re_c))
        r.op(V, lambda E: E.tensor_scalar_mul(scr2[:], bim_t, s_im_c))
        r.op(V, lambda E: E.tensor_sub(bbre, bbre, scr2[:]))
        r.op(V, lambda E: E.tensor_scalar_mul(bbim, bre_t, s_im_c))
        r.op(V, lambda E: E.tensor_scalar_mul(scr2[:], bim_t, s_re_c))
        r.op(V, lambda E: E.tensor_add(bbim, bbim, scr2[:]), inc="v")
        id32 = ident[0:32, 0:32]
        for c in range(NCH):
            for srcb, dstb in ((bbre, bbar_Tre), (bbim, bbar_Tim)):
                r.wait("tensor", "v")
                r.op("tensor", lambda E, c=c, srcb=srcb: E.transpose(
                    ps_tr, srcb[:, c * 128:(c + 1) * 128], id32), inc="t")
                r.wait(V, "t")
                r.op(V, lambda E, c=c, dstb=dstb: E.tensor_copy(
                    dstb[:, c, :], ps_tr[:, 0:32]), inc="v")
        for c in range(NCH):
            r.op(Sc, lambda E, c=c: E.activation(cpack_re[:, c, :], cret_t[:, c * 128:(c + 1) * 128],
                                                 AF.Copy, scale=2.0), inc="s")
            r.op(Sc, lambda E, c=c: E.activation(cpack_im[:, c, :], cimt_t[:, c * 128:(c + 1) * 128],
                                                 AF.Copy, scale=-2.0), inc="s")

        NG = S // 512
        t_group_end = {}
        for g in range(NG):
            if g >= 1:
                r.wait("gpsimd", "t", t_group_end[g - 1])
            for c in range(NCH):
                r.op("gpsimd", lambda E, c=c, g=g: E.dma_start(
                    out=u_bf[:, c, :],
                    in_=u3[c, :, g * 512:(g + 1) * 512]), inc="qg", k=16)
            r.wait("tensor", "qg")
            r.wait("tensor", "v")
            for c in range(NCH):
                r.op("tensor", lambda E, c=c: E.matmul(
                    ps_bu_re, bbar_Tre[:, c, :], u_bf[:, c, :],
                    start=(c == 0), stop=(c == NCH - 1)))
                r.op("tensor", lambda E, c=c: E.matmul(
                    ps_bu_im, bbar_Tim[:, c, :], u_bf[:, c, :],
                    start=(c == 0), stop=(c == NCH - 1)), inc="t")
            t_group_end[g] = r.cnt["t"]
            r.wait(V, "t")
            r.op(V, lambda E, g=g: E.tensor_copy(bu[:, 0, g * 512:(g + 1) * 512], ps_bu_re))
            r.op(V, lambda E, g=g: E.tensor_copy(bu[:, 1, g * 512:(g + 1) * 512], ps_bu_im), inc="v")

        if dbg:
            r.wait("sync", "v")
            r.op("sync", lambda E: E.dma_start(out=dbg_bu, in_=bu[:]), inc="so", k=16)
            r.op("sync", lambda E: E.dma_start(out=dbg_ubf, in_=u_bf[:]), inc="so", k=16)
            r.wait("vector", "so")

        d03 = d0s.rearrange("n (c t) -> n c t", t=128)
        r.wait("gpsimd", "s")
        r.wait("gpsimd", "v")
        r.op("gpsimd", lambda E: E.iota(d03, pattern=[[0, S // 128], [1, 128]],
                                        channel_multiplier=0,
                                        allow_small_or_imprecise_dtypes=True), inc="qg", k=1)
        r.wait(V, "qg")
        r.wait(V, "s")
        r.op(V, lambda E: E.tensor_single_scalar(d0s, d0s, 0.5, ALU.is_ge), inc="v")
        NCH_S = S // 128
        HCH = NCH_S // 2

        def cplx_combine(dst_sel, in_re, in_im, tab, ta, tb, alu):
            for half in range(2):
                lo = half * (S // 2)
                in_a = in_re[:, lo:lo + S // 2].rearrange("n (c t) -> n c t", t=128)
                in_b = in_im[:, lo:lo + S // 2].rearrange("n (c t) -> n c t", t=128)
                tb_a = tab[:, ta, :].unsqueeze(1).broadcast_to([32, HCH, 128])
                tb_b = tab[:, tb, :].unsqueeze(1).broadcast_to([32, HCH, 128])
                sca3 = sca.rearrange("n (c t) -> n c t", t=128)
                scb3 = scb.rearrange("n (c t) -> n c t", t=128)
                r.op(V, lambda E, in_a=in_a, tb_a=tb_a, sca3=sca3: E.tensor_tensor(
                    sca3, in_a, tb_a, ALU.mult))
                r.op(V, lambda E, in_b=in_b, tb_b=tb_b, scb3=scb3: E.tensor_tensor(
                    scb3, in_b, tb_b, ALU.mult))
                dst = dst_sel(lo).rearrange("n (c t) -> n c t", t=128)
                r.op(V, lambda E, dst=dst, sca3=sca3, scb3=scb3, alu=alu: E.tensor_tensor(
                    dst, sca3, scb3, alu), inc="v")

        cplx_combine(lambda lo: cum[:, 0, lo:lo + S // 2], bu[:, 0, :], bu[:, 1, :],
                     tm, 1, 0, ALU.add)
        cplx_combine(lambda lo: bu[:, 0, lo:lo + S // 2], bu[:, 0, :], bu[:, 1, :],
                     tm, 0, 1, ALU.subtract)
        r.op(V, lambda E: E.tensor_copy(bu[:, 1, :], cum[:, 0, :]), inc="v")

        r.op(V, lambda E: E.tensor_tensor_scan(cum[:, 0, :], d0s, bu[:, 0, :], 0.0,
                                               ALU.mult, ALU.add), inc="v")
        r.op(V, lambda E: E.tensor_tensor_scan(cum[:, 1, :], d0s, bu[:, 1, :], 0.0,
                                               ALU.mult, ALU.add), inc="v")
        cum3r = cum[:, 0, :].rearrange("n (c t) -> n c t", t=128)
        cum3i = cum[:, 1, :].rearrange("n (c t) -> n c t", t=128)
        r.op(V, lambda E: E.tensor_copy(ee[:, 0, :], cum3r[:, :, 127]))
        r.op(V, lambda E: E.tensor_copy(ee[:, 1, :], cum3i[:, :, 127]), inc="v")
        r.op(V, lambda E: E.memset(ccol[:, 0:2], 0.0), inc="v")
        for p in range(1, NCH_S):
            ccr = ccol[:, 0:1]
            cci = ccol[:, 1:2]
            ctr = ccol[:, 2:3]
            cti = ccol[:, 3:4]
            r.op(V, lambda E, p=p: E.tensor_add(ccol[:, 2:4], ccol[:, 0:2], ee[:, :, p - 1]))
            r.op(V, lambda E: E.tensor_mul(ccr, ctr, a128c[:, 0:1]))
            r.op(V, lambda E: E.tensor_mul(wcol[:, 0:1], cti, a128c[:, 1:2]))
            r.op(V, lambda E: E.tensor_sub(ccr, ccr, wcol[:, 0:1]))
            r.op(V, lambda E: E.tensor_mul(cci, ctr, a128c[:, 1:2]))
            r.op(V, lambda E: E.tensor_mul(wcol[:, 1:2], cti, a128c[:, 0:1]))
            r.op(V, lambda E: E.tensor_add(cci, cci, wcol[:, 1:2]))
            r.op(V, lambda E, p=p: E.tensor_scalar_add(
                cum[:, 0, p * 128:(p + 1) * 128], cum[:, 0, p * 128:(p + 1) * 128], ccr))
            r.op(V, lambda E, p=p: E.tensor_scalar_add(
                cum[:, 1, p * 128:(p + 1) * 128], cum[:, 1, p * 128:(p + 1) * 128], cci),
                inc="v")
        cplx_combine(lambda lo: xs_bf[:, 0, lo:lo + S // 2], cum[:, 0, :], cum[:, 1, :],
                     tp, 0, 1, ALU.subtract)
        cplx_combine(lambda lo: xs_bf[:, 1, lo:lo + S // 2], cum[:, 0, :], cum[:, 1, :],
                     tp, 1, 0, ALU.add)

        x13 = x1.rearrange("(c p) t -> c p t", p=128)
        OWN0 = TOK
        r.wait("tensor", "s")
        for qg in range(4):
            for c in range(NCH):
                if qg * NCH + c >= 2:
                    r.wait("sync", "v")
                r.op("sync", lambda E, qg=qg, c=c: E.dma_start(
                    out=u32b[:, (qg * NCH + c) % 2, :],
                    in_=u3[c, :, OWN0 + qg * 512:OWN0 + (qg + 1) * 512]), inc="qs", k=16)
                r.wait("vector", "qs")
                r.wait("tensor", "v")
                if qg > 0 and c == 0:
                    r.wait("vector", "so")
                r.op("tensor", lambda E, qg=qg, c=c: E.matmul(
                    ps_y[c % 2], cpack_re[:, c, :],
                    xs_bf[:, 0, OWN0 + qg * 512:OWN0 + (qg + 1) * 512],
                    start=True, stop=False))
                r.op("tensor", lambda E, qg=qg, c=c: E.matmul(
                    ps_y[c % 2], cpack_im[:, c, :],
                    xs_bf[:, 1, OWN0 + qg * 512:OWN0 + (qg + 1) * 512],
                    start=False, stop=True), inc="t")
                r.wait(V, "t")
                r.op(V, lambda E, qg=qg, c=c: E.scalar_tensor_tensor(
                    pre[:, c, :], u32b[:, (qg * NCH + c) % 2, :], d1col[:, c:c + 1],
                    ps_y[c % 2], ALU.mult, ALU.add), inc="v")
            r.wait("scalar", "v")
            _ln_emit(r, pre, NCH, sqscr, ps_sum, ps_sumsq, ones_col, ones_row,
                     rowscr, abc, bbc, ps_bc, 512)
            r.wait("sync", "v")
            for c in range(NCH):
                r.op("sync", lambda E, qg=qg, c=c: E.dma_start(
                    out=x13[c, :, qg * 512:(qg + 1) * 512], in_=pre[:, c, :]), inc="so", k=16)
        if dbg:
            r.wait("sync", "v")
            r.wait("sync", "s")
            for (dst, srcx) in [(dbg_rows, row_par[:]), (dbg_scal6, scal6[:]),
                                (dbg_tp, tp[:]), (dbg_tm, tm[:]),
                                (dbg_cum, cum[:]), (dbg_xs, xs_bf),
                                (dbg_pre, pre[:])]:
                r.op("sync", lambda E, dst=dst, srcx=srcx: E.dma_start(out=dst, in_=srcx),
                     inc="so", k=16)
        r.wait("sync", "so")
        r.play(block)
    return nc


def build_launch_B(dbg=False):
    nc = bass.Bass("TRN2", target_bir_lowering=False, debug=False)
    x1 = nc.dram_tensor("x1", [D, TOK], F32, kind="ExternalInput").ap()
    rwt = nc.dram_tensor("rwt", [D, 8], F32, kind="ExternalInput").ap()
    w1 = nc.dram_tensor("w1", [8, D, 1024], F32, kind="ExternalInput").ap()
    w2 = nc.dram_tensor("w2", [8, D, 1024], F32, kind="ExternalInput").ap()
    w3 = nc.dram_tensor("w3", [8, 1024, D], F32, kind="ExternalInput").ap()
    x2 = nc.dram_tensor("x2", [D, TOK], F32, kind="ExternalOutput").ap()
    hsp = nc.dram_tensor("hsp", [8, 1024, TOK], BF16, kind="Internal").ap()
    x2pre = nc.dram_tensor("x2pre", [D, TOK], F32, kind="Internal").ap()
    if dbg:
        dbg_lg = nc.dram_tensor("dbg_lg", [128, NCH, 8], F32, kind="ExternalOutput").ap()
        dbg_wrow = nc.dram_tensor("dbg_wrow", [64, TOK], F32, kind="ExternalOutput").ap()
        dbg_hsp = nc.dram_tensor("dbg_hsp", [8, 1024, TOK], BF16, kind="ExternalOutput").ap()
        dbg_x2p = nc.dram_tensor("dbg_x2p", [D, TOK], F32, kind="ExternalOutput").ap()
        dbg_hp7 = nc.dram_tensor("dbg_hp7", [128, 8, TOK], mybir.dt.bfloat16, kind="ExternalOutput").ap()

    x13 = x1.rearrange("(c p) t -> c p t", p=128)
    x23 = x2.rearrange("(c p) t -> c p t", p=128)
    x2p3 = x2pre.rearrange("(c p) t -> c p t", p=128)
    rwt3 = rwt.rearrange("(c p) e -> c p e", p=128)

    NT = TOK // 512
    NFB = 8

    with ExitStack() as ctx:
        sb = lambda n, s, d=F32: ctx.enter_context(nc.sbuf_tensor(n, s, d))
        r = Rec(nc, ctx)
        for s_ in ("qs", "qg", "v", "s", "t", "so"):
            r.sem(s_)

        xbf = sb("xbf", [128, NCH, TOK], BF16)
        w3sl4 = xbf[:].rearrange("p c t -> p (c t)").rearrange(
            "p (e fc dq) -> p e fc dq", e=8, fc=8)
        x32 = sb("x32", [128, 4, 512])
        rwt_t = sb("rwt_t", [128, NCH, 8])
        pack8 = sb("pack8", [64, TOK])
        lg_fm = pack8[0:8, :]
        wrow_fm = pack8[32:40, :]
        lg_tm = sb("lg_tm", [128, NCH, 8])
        mx_t = sb("mx_t", [128, 8])
        idx_t = sb("idx_t", [128, 8], U32)
        idxf = sb("idxf", [128, 2])
        wcol2 = sb("wcol2", [128, 4])
        iota8 = sb("iota8", [128, 8])
        wmap = sb("wmap", [128, 8])
        wrow_e = sb("wrow_e", [1, TOK])
        wbc = sb("wbc", [128, TOK])
        w12s = sb("w12s", [128, 2, NCH, 512], BF16)
        pre = w12s[:].rearrange("p a c t -> p (a c t)").bitcast(F32).rearrange(
            "p (c t) -> p c t", c=NCH)
        gsil = sb("gsil", [128, 512])
        vw = sb("vw", [128, 512])
        hprime = sb("hprime", [128, NFB, TOK], BF16)
        hb3 = hprime[:].rearrange("p a t -> p (a t)").rearrange(
            "p (b fc t) -> p b fc t", b=4, fc=8)
        sqscr = sb("sqscr", [128, 2, 512])
        rowscr = sb("rowscr", [1, 2048])
        abc = sb("abc", [128, 512])
        bbc = sb("bbc", [128, 512])
        idv = sb("idv", [128, 128])
        ident = sb("ident", [128, 128])
        ones_col = sb("ones_col", [128, 1])
        ones_row = sb("ones_row", [1, 128])

        PS = ctx.enter_context(nc.psum_tensor("PS", [128, 4096], F32))
        bank = lambda i: PS[:, 512 * i:512 * (i + 1)]
        ps_lg = PS[0:8, 0:512]
        ps_tr = PS[:, 512:1024]

        block = ctx.enter_context(nc.Block())
        V, Sc = "vector", "scalar"

        r.op("sync", lambda E: E.dma_start(out=rwt_t[:], in_=rwt3.rearrange("c p e -> p c e")),
             inc="qs", k=16)
        r.op("gpsimd", lambda E: E.iota(idv[:], pattern=[[-1, 128]], channel_multiplier=1,
                                        allow_small_or_imprecise_dtypes=True), inc="qg", k=1)
        r.op("gpsimd", lambda E: E.iota(iota8[:], pattern=[[1, 8]], channel_multiplier=0,
                                        allow_small_or_imprecise_dtypes=True), inc="qg", k=1)
        r.wait(V, "qg")
        r.op(V, lambda E: E.tensor_single_scalar(ident[:], idv[:], 0.0, ALU.is_equal))
        r.op(V, lambda E: E.memset(ones_col[:], 1.0))
        r.op(V, lambda E: E.memset(ones_row[:], 1.0))

        r.wait("tensor", "qs")
        ld_done = {}
        for tg in range(NT):
            for c in range(NCH):
                i = tg * NCH + c
                if i >= 2:
                    r.wait("sync", "t", ld_done[i - 2][0])
                    r.wait("sync", "v", ld_done[i - 2][1])
                r.op("sync", lambda E, tg=tg, c=c, i=i: E.dma_start(
                    out=x32[:, i % 4, :], in_=x13[c, :, tg * 512:(tg + 1) * 512]), inc="qs", k=16)
                r.wait("tensor", "qs")
                r.op("tensor", lambda E, tg=tg, c=c, i=i: E.matmul(
                    ps_lg, rwt_t[:, c, :], x32[:, i % 4, :],
                    start=(c == 0), stop=(c == NCH - 1)), inc="t")
                r.wait(V, "qs")
                r.op(V, lambda E, tg=tg, c=c, i=i: E.tensor_copy(
                    xbf[:, c, tg * 512:(tg + 1) * 512], x32[:, i % 4, :]))
                ld_done[i] = (r.cnt["t"], r.cnt["v"])
            r.wait(V, "t")
            r.op(V, lambda E, tg=tg: E.tensor_copy(lg_fm[:, tg * 512:(tg + 1) * 512], ps_lg))
            r.wait("tensor", "v")

        ident8 = ident[0:8, 0:8]
        for i in range(NCH):
            r.wait("tensor", "v")
            r.op("tensor", lambda E, i=i: E.transpose(
                ps_tr[0:128, 0:8], lg_fm[:, i * 128:(i + 1) * 128], ident8), inc="t")
            r.wait(V, "t")
            r.op(V, lambda E, i=i: E.tensor_copy(lg_tm[:, i, :], ps_tr[0:128, 0:8]))
            r.op(V, lambda E, i=i: E.max(mx_t[:], lg_tm[:, i, :]))
            r.op(V, lambda E, i=i: E.max_index(idx_t[:], mx_t[:], lg_tm[:, i, :]))
            r.op(V, lambda E: E.tensor_copy(idxf[:], idx_t[:, 0:2]))
            r.op(V, lambda E: E.tensor_sub(wcol2[:, 0:1], mx_t[:, 1:2], mx_t[:, 0:1]))
            r.wait(Sc, "v")
            r.op(Sc, lambda E: E.activation(wcol2[:, 1:2], wcol2[:, 0:1], AF.Exp))
            r.wait(V, "s")
            r.op(V, lambda E: E.tensor_scalar_add(wcol2[:, 2:3], wcol2[:, 1:2], 1.0))
            r.op(V, lambda E: E.reciprocal(wcol2[:, 2:3], wcol2[:, 2:3]))
            r.wait(V, "v")
            r.op(V, lambda E: E.tensor_mul(wcol2[:, 3:4], wcol2[:, 1:2], wcol2[:, 2:3]))
            r.op(V, lambda E: E.tensor_scalar(wmap[:], iota8[:], idxf[:, 0:1], wcol2[:, 2:3],
                                              ALU.is_equal, ALU.mult))
            r.op(V, lambda E: E.tensor_scalar(gsil[:, 0:8], iota8[:], idxf[:, 1:2], wcol2[:, 3:4],
                                              ALU.is_equal, ALU.mult))
            r.op(V, lambda E: E.tensor_add(wmap[:], wmap[:], gsil[:, 0:8]))
            r.wait("tensor", "v")
            r.op("tensor", lambda E, i=i: E.transpose(
                ps_tr[0:8, 0:128], wmap[:], ident[:]), inc="t")
            r.wait(V, "t")
            r.op(V, lambda E, i=i: E.tensor_copy(wrow_fm[:, i * 128:(i + 1) * 128],
                                                 ps_tr[0:8, 0:128]))

        for e in range(8):
            r.wait("gpsimd", "v")
            r.op("gpsimd", lambda E, e=e: E.dma_start(out=wrow_e[:], in_=pack8[32 + e:33 + e, :]),
                 inc="qg", k=16)
            r.wait("tensor", "qg")
            for tg in range(NT):
                r.op("tensor", lambda E, tg=tg: E.matmul(
                    bank(2 + tg % 2), ones_row[:], wrow_e[:, tg * 512:(tg + 1) * 512],
                    start=True, stop=True), inc="t")
                r.wait(V, "t")
                r.op(V, lambda E, tg=tg: E.tensor_copy(wbc[:, tg * 512:(tg + 1) * 512],
                                                       bank(2 + tg % 2)))
                r.wait("tensor", "v")
            for fh in range(2):
                r.wait("gpsimd", "t")
                for c in range(NCH):
                    r.op("gpsimd", lambda E, e=e, fh=fh, c=c: E.dma_start(
                        out=w12s[:, 0, c, :], in_=w1[e, :, :].rearrange(
                            "(c p) f -> c p f", p=128)[c, :, fh * 512:(fh + 1) * 512]),
                        inc="qg", k=16)
                    r.op("gpsimd", lambda E, e=e, fh=fh, c=c: E.dma_start(
                        out=w12s[:, 1, c, :], in_=w2[e, :, :].rearrange(
                            "(c p) f -> c p f", p=128)[c, :, fh * 512:(fh + 1) * 512]),
                        inc="qg", k=16)
                r.wait("tensor", "qg")
                for fb in range(4):
                    fblk = fh * 4 + fb
                    for c in range(NCH):
                        for tg in range(NT):
                            r.op("tensor", lambda E, c=c, tg=tg, fb=fb: E.matmul(
                                bank(tg), w12s[:, 0, c, fb * 128:(fb + 1) * 128],
                                xbf[:, c, tg * 512:(tg + 1) * 512],
                                start=(c == 0), stop=(c == NCH - 1)))
                        for tg in range(NT):
                            r.op("tensor", lambda E, c=c, tg=tg, fb=fb: E.matmul(
                                bank(4 + tg), w12s[:, 1, c, fb * 128:(fb + 1) * 128],
                                xbf[:, c, tg * 512:(tg + 1) * 512],
                                start=(c == 0), stop=(c == NCH - 1)),
                                inc=("t" if c == NCH - 1 and tg == NT - 1 else None))
                    for tg in range(NT):
                        r.wait(Sc, "t")
                        r.wait(Sc, "v")
                        r.op(Sc, lambda E, tg=tg: E.activation(gsil[:], bank(tg), AF.Silu))
                        r.wait(V, "s")
                        r.op(V, lambda E, tg=tg: E.tensor_tensor(
                            vw[:], bank(4 + tg), wbc[:, tg * 512:(tg + 1) * 512], ALU.mult))
                        r.op(V, lambda E, tg=tg, fblk=fblk: E.tensor_tensor(
                            hprime[:, fblk, tg * 512:(tg + 1) * 512], gsil[:], vw[:], ALU.mult))
                    r.wait("tensor", "v")
            r.wait("sync", "v")
            for fblk in range(NFB):
                r.op("sync", lambda E, e=e, fblk=fblk: E.dma_start(
                    out=hsp[e, fblk * 128:(fblk + 1) * 128, :], in_=hprime[:, fblk, :]),
                    inc="so", k=16)
            r.wait("gpsimd", "so")
            r.wait("vector", "so")

        if dbg:
            r.wait("sync", "v")
            r.op("sync", lambda E: E.dma_start(out=dbg_hp7, in_=hprime[:]), inc="so", k=16)
            r.wait("gpsimd", "so")

        t_b3 = {}
        r.wait("gpsimd", "so")
        for dq in range(4):
            r.wait("gpsimd", "t")
            for e in range(8):
                for fc in range(8):
                    r.op("gpsimd", lambda E, dq=dq, e=e, fc=fc: E.dma_start(
                        out=w3sl4[:, e, fc, :],
                        in_=w3[e, :, :].rearrange("(fc p) d -> fc p d", p=128)[
                            fc, :, dq * 512:(dq + 1) * 512]), inc="qg", k=16)
            r.wait("tensor", "qg")
            for tg in range(NT):
                for e in range(8):
                    i = tg * 8 + e
                    if i >= 2:
                        r.wait("gpsimd", "t", t_b3[(dq, i - 2)])
                    for fc in range(8):
                        r.op("gpsimd", lambda E, e=e, fc=fc, tg=tg, i=i: E.dma_start(
                            out=hb3[:, i % 4, fc, :],
                            in_=hsp[e, fc * 128:(fc + 1) * 128, tg * 512:(tg + 1) * 512]),
                            inc="qg", k=16)
                    r.wait("tensor", "qg")
                    if e == 0:
                        r.wait("tensor", "v")
                    for dblk in range(4):
                        for fc in range(8):
                            r.op("tensor", lambda E, e=e, fc=fc, dblk=dblk, i=i: E.matmul(
                                bank(dblk), w3sl4[:, e, fc, dblk * 128:(dblk + 1) * 128],
                                hb3[:, i % 4, fc, :],
                                start=(e == 0 and fc == 0), stop=(e == 7 and fc == 7)),
                                inc=("t" if fc == 7 and dblk == 3 else None))
                    t_b3[(dq, i)] = r.cnt["t"]
                for dblk in range(4):
                    c = dq * 4 + dblk
                    r.op("sync", lambda E, c=c, tg=tg, dblk=dblk: E.dma_start(
                        out=x32[:, dblk % 4, :], in_=x13[c, :, tg * 512:(tg + 1) * 512]),
                        inc="qs", k=16)
                    r.wait(V, "qs")
                    r.wait(V, "t")
                    r.op(V, lambda E, c=c, tg=tg, dblk=dblk: E.scalar_tensor_tensor(
                        gsil[:], x32[:, dblk % 4, :], 1.0, bank(dblk), ALU.mult, ALU.add))
                    r.wait("sync", "v")
                    r.op("sync", lambda E, c=c, tg=tg: E.dma_start(
                        out=x2p3[c, :, tg * 512:(tg + 1) * 512], in_=gsil[:]), inc="so", k=16)
                    r.wait(V, "so")

        if dbg:
            r.wait("sync", "v")
            r.op("sync", lambda E: E.dma_start(out=dbg_lg, in_=lg_tm[:]), inc="so", k=16)
            r.op("sync", lambda E: E.dma_start(out=dbg_wrow, in_=pack8[:]), inc="so", k=16)
            r.op("sync", lambda E: E.dma_start(out=dbg_hsp, in_=hsp), inc="so", k=16)
            r.op("sync", lambda E: E.dma_start(out=dbg_x2p, in_=x2pre), inc="so", k=16)

        for tg in range(NT):
            r.wait("sync", "so")
            r.wait("sync", "v")
            for c in range(NCH):
                r.op("sync", lambda E, c=c, tg=tg: E.dma_start(
                    out=pre[:, c, :], in_=x2p3[c, :, tg * 512:(tg + 1) * 512]), inc="qs", k=16)
            r.wait("vector", "qs")
            r.wait("scalar", "qs")
            r.wait("tensor", "qs")
            _ln_emit(r, pre, NCH, sqscr, PS[0:1, 2048:2560], PS[0:1, 2560:3072],
                     ones_col, ones_row, rowscr, abc, bbc,
                     [PS[:, 3072:3584], PS[:, 3584:4096]], 512)
            r.wait("sync", "v")
            for c in range(NCH):
                r.op("sync", lambda E, c=c, tg=tg: E.dma_start(
                    out=x23[c, :, tg * 512:(tg + 1) * 512], in_=pre[:, c, :]), inc="so", k=16)
        r.wait("sync", "so")
        r.play(block)
    return nc


def build_launch_C(dbg=False):
    nc = bass.Bass("TRN2", target_bir_lowering=False, debug=False)
    xkv = nc.dram_tensor("xkv", [D, S], F32, kind="ExternalInput").ap()
    pos = nc.dram_tensor("pos", [1, S], F32, kind="ExternalInput").ap()
    maskrow_d = nc.dram_tensor("maskrow", [1, 128], F32, kind="ExternalInput").ap()
    wqdT = nc.dram_tensor("wqdT", [D, 1024], F32, kind="ExternalInput").ap()
    qnw = nc.dram_tensor("qnw", [128, 8], F32, kind="ExternalInput").ap()
    wquT = nc.dram_tensor("wquT", [1024, 2048], F32, kind="ExternalInput").ap()
    wqrT = nc.dram_tensor("wqrT", [1024, 1024], F32, kind="ExternalInput").ap()
    wkvdT = nc.dram_tensor("wkvdT", [D, 512], F32, kind="ExternalInput").ap()
    kvnw = nc.dram_tensor("kvnw", [128, 4], F32, kind="ExternalInput").ap()
    wkvuT = nc.dram_tensor("wkvuT", [512, 4096], F32, kind="ExternalInput").ap()
    wkrT = nc.dram_tensor("wkrT", [D, 64], F32, kind="ExternalInput").ap()
    woT = nc.dram_tensor("woT", [D, D], F32, kind="ExternalInput").ap()
    x3 = nc.dram_tensor("x3", [D, TOK], F32, kind="ExternalOutput").ap()
    aspill = nc.dram_tensor("aspill", [D, TOK], BF16, kind="Internal").ap()
    if dbg:
        dbg_qc = nc.dram_tensor("dbg_qc", [128, 8, TOK], BF16, kind="ExternalOutput").ap()
        dbg_kvc = nc.dram_tensor("dbg_kvc", [128, 4, S], BF16, kind="ExternalOutput").ap()
        dbg_kr = nc.dram_tensor("dbg_kr", [64, S], BF16, kind="ExternalOutput").ap()
        dbg_cos = nc.dram_tensor("dbg_cos", [64, S], BF16, kind="ExternalOutput").ap()
        dbg_asp = nc.dram_tensor("dbg_asp", [D, TOK], BF16, kind="ExternalOutput").ap()
        dbg_kn15 = nc.dram_tensor("dbg_kn15", [128, S], BF16, kind="ExternalOutput").ap()
        dbg_qn15 = nc.dram_tensor("dbg_qn15", [128, TOK], BF16, kind="ExternalOutput").ap()
        dbg_qp15 = nc.dram_tensor("dbg_qp15", [64, TOK], BF16, kind="ExternalOutput").ap()
        dbg_ftab = nc.dram_tensor("dbg_ftab", [32, S], F32, kind="ExternalOutput").ap()
        dbg_sin0 = nc.dram_tensor("dbg_sin0", [64, S], BF16, kind="ExternalOutput").ap()
        dbg_kvc2 = nc.dram_tensor("dbg_kvc2", [128, 4, S], BF16, kind="ExternalOutput").ap()
        dbg_ang = nc.dram_tensor("dbg_ang", [32, 1024], F32, kind="ExternalOutput").ap()
        dbg_i = nc.dram_tensor("dbg_i", [32, 1024], I32, kind="ExternalOutput").ap()
        dbg_s2 = nc.dram_tensor("dbg_s2", [32, 1024], F32, kind="ExternalOutput").ap()
        dbg_bias = nc.dram_tensor("dbg_bias", [128, 128], F32, kind="ExternalOutput").ap()
        dbg_msc = nc.dram_tensor("dbg_msc", [1, 8], F32, kind="ExternalOutput").ap()
        dbg_pt = nc.dram_tensor("dbg_pt", [128, 2, 512], BF16, kind="ExternalOutput").ap()

    xkv3 = xkv.rearrange("(c p) t -> c p t", p=128)
    x33 = x3.rearrange("(c p) t -> c p t", p=128)
    asp3 = aspill.rearrange("(c p) t -> c p t", p=128)
    NKT = S // 128
    NH = 16
    SCALE = 1.0 / float(np.sqrt(192.0))
    inv2pi = 1.0 / (2.0 * np.pi)
    twopi = 2.0 * np.pi

    with ExitStack() as ctx:
        sb = lambda n, s, d=F32: ctx.enter_context(nc.sbuf_tensor(n, s, d))
        r = Rec(nc, ctx)
        for s_ in ("qs", "qg", "v", "s", "t", "so"):
            r.sem(s_)

        x32k = sb("x32k", [128, 2, 512])
        xbf_s = sb("xbf_s", [128, 2, 512], BF16)
        qc_bf = sb("qc_bf", [128, 8, TOK], BF16)
        pre = qc_bf[:].rearrange("p a t -> p (a t)").bitcast(F32).rearrange(
            "p (c t) -> p c t", c=NCH)
        kvc_bf = sb("kvc_bf", [128, 4, S], BF16)
        wo_s = kvc_bf[:].rearrange("p a t -> p (a t)").rearrange(
            "p (c t) -> p c t", c=NCH)
        kro_raw = sb("kro_raw", [64, S], BF16)
        krope2 = sb("krope2", [64, S], BF16)
        cosk2 = sb("cosk2", [64, S], BF16)
        sink2 = sb("sink2", [64, S], BF16)
        wkr_bf = sb("wkr_bf", [128, NCH, 64], BF16)
        wkvd_s = sb("wkvd_s", [128, 2, 512], BF16)
        wqd_s = sb("wqd_s", [128, 2, 1024], BF16)
        qnw_t = sb("qnw_t", [128, 8])
        kvnw_t = sb("kvnw_t", [128, 4])
        prodscr = sb("prodscr", [128, 8, 512])
        prodflat = prodscr[:].rearrange("p a t -> p (a t)")
        ftab = prodflat[0:32, 0:4096]
        rowscr_f = prodflat[0:1, 0:2048]
        sqscr = sb("sqscr", [128, 2, 512])
        trig_s1 = sb("trig_s1", [32, 1024])
        trig_s2 = sqscr[:].rearrange("p a t -> p (a t)")[0:32, 0:1024]
        trig_i = sb("trig_i", [32, 1024], I32)
        rows_rms = sb("rows_rms", [1, 1024])
        rbref = sb("rbref", [128, 512])
        wup_h = sb("wup_h", [128, 2, 8, 128], BF16)
        wqr_h = sb("wqr_h", [128, 2, 8, 64], BF16)
        wkvu_h = sb("wkvu_h", [128, 2, 4, 256], BF16)
        knope = sb("knope", [128, S], BF16)
        vext = sb("vext", [128, NKT, 130], BF16)
        qnope = sb("qnope", [128, TOK], BF16)
        qpe = sb("qpe", [64, TOK], BF16)
        qpe_f = prodflat[0:64, 2048:3072].rearrange("p (a t) -> p a t", a=2)
        ptile = sb("ptile", [128, 2048], BF16)
        att_t = sb("att_t", [128, 1, 512], BF16)
        att_qg = sb("att_qg", [128, NCH, 512], BF16)
        m01 = sb("m01", [128, 4, 512], BF16)
        ivt = None
        biasr = rows_rms[0:1, 0:128]
        bias_h = sb("bias_h", [128, 128])
        maskrow = sb("maskrow_t", [1, 128])
        pos_sl = rows_rms[0:1, 512:1024]
        invf = sb("invf", [1, 32])
        r64 = sb("r64", [64, 64])
        r64b = sb("r64b", [64, 64], BF16)
        d64 = sb("d64", [64, 64])
        pcol = sb("pcol", [64, 4])
        mxq = sb("mxq", [128, 4])
        msc = sb("msc", [1, 8])
        ident = sb("identc", [128, 128])
        idv = ident
        ones_col = sb("ones_colc", [128, 1])
        ones_row = sb("ones_rowc", [1, 128])
        eps11 = sb("eps11", [1, 1])
        abc = prodflat[:, 3072:3584]
        bbc = prodflat[:, 3584:4096]

        PS = ctx.enter_context(nc.psum_tensor("PS", [128, 4096], F32))
        bank = lambda i: PS[:, 512 * i:512 * (i + 1)]

        block = ctx.enter_context(nc.Block())
        V, Sc = "vector", "scalar"

        r.op("sync", lambda E: E.dma_start(out=maskrow[:], in_=maskrow_d), inc="qs", k=16)
        r.op("sync", lambda E: E.dma_start(out=qnw_t[:], in_=qnw), inc="qs", k=16)
        r.op("sync", lambda E: E.dma_start(out=kvnw_t[:], in_=kvnw), inc="qs", k=16)
        r.op("gpsimd", lambda E: E.dma_start(
            out=wkr_bf[:], in_=wkrT.rearrange("(c p) f -> p c f", p=128)), inc="qg", k=16)
        r.op("gpsimd", lambda E: E.iota(idv[:], pattern=[[-1, 128]], channel_multiplier=1,
                                        allow_small_or_imprecise_dtypes=True), inc="qg", k=1)

        xkvbf = nc.dram_tensor("xkvbf", [D, S], BF16, kind="Internal").ap()
        xkvbf3 = xkvbf.rearrange("(c p) t -> c p t", p=128)
        p0_st = {}
        for i0 in range(NCH * 8):
            c0, tg0 = i0 // 8, i0 % 8
            if i0 >= 2:
                r.wait("gpsimd", "qs", p0_st[i0 - 2])
            r.op("gpsimd", lambda E, c0=c0, tg0=tg0, i0=i0: E.dma_start(
                out=xbf_s[:, i0 % 2, :], in_=xkv3[c0, :, tg0 * 512:(tg0 + 1) * 512]),
                inc="qg", k=16)
            r.wait("sync", "qg")
            r.op("sync", lambda E, c0=c0, tg0=tg0, i0=i0: E.dma_start(
                out=xkvbf3[c0, :, tg0 * 512:(tg0 + 1) * 512], in_=xbf_s[:, i0 % 2, :]),
                inc="qs", k=16)
            p0_st[i0] = r.cnt["qs"]
        p0_done = r.cnt["qs"]

        r.op("gpsimd", lambda E: E.iota(sqscr[:, 0, :], pattern=[[1, 512]], channel_multiplier=-1,
                                        allow_small_or_imprecise_dtypes=True), inc="qg", k=1)
        r.op("gpsimd", lambda E: E.iota(d64[:], pattern=[[-1, 64]], channel_multiplier=1,
                                        allow_small_or_imprecise_dtypes=True), inc="qg", k=1)
        r.op("gpsimd", lambda E: E.iota(pcol[:, 0:1], pattern=[[0, 1]], channel_multiplier=1,
                                        allow_small_or_imprecise_dtypes=True), inc="qg", k=1)
        r.op("gpsimd", lambda E: E.iota(invf[:], pattern=[[1, 32]], channel_multiplier=0,
                                        allow_small_or_imprecise_dtypes=True), inc="qg", k=1)
        r.wait(V, "qg")
        r.wait(V, "qs")
        r.op(V, lambda E: E.tensor_single_scalar(ident[:], idv[:], 0.0, ALU.is_equal))
        r.op(V, lambda E: E.memset(ones_col[:], 1.0))
        r.op(V, lambda E: E.memset(ones_row[:], 1.0))
        r.op(V, lambda E: E.memset(eps11[:], 1e-6))
        for dd in range(4):
            r.op(V, lambda E, dd=dd: E.tensor_single_scalar(
                m01[:, dd, :], sqscr[:, 0, :], dd * 128 - 0.5, ALU.is_ge))
        r.op(V, lambda E: E.tensor_single_scalar(pcol[:, 1:2], pcol[:, 0:1], 31.5, ALU.is_ge))
        r.op(V, lambda E: E.tensor_single_scalar(pcol[:, 2:3], pcol[:, 1:2], 0.5, ALU.is_lt))
        r.op(V, lambda E: E.tensor_single_scalar(r64[:], d64[:], 32.0, ALU.is_equal))
        r.op(V, lambda E: E.tensor_tensor(r64[:], r64[:],
                                          pcol[:, 1:2].broadcast_to([64, 64]), ALU.mult))
        r.op(V, lambda E: E.tensor_single_scalar(d64[:], d64[:], -32.0, ALU.is_equal))
        r.op(V, lambda E: E.tensor_tensor(d64[:], d64[:],
                                          pcol[:, 2:3].broadcast_to([64, 64]), ALU.mult))
        r.op(V, lambda E: E.tensor_sub(r64[:], d64[:], r64[:]))
        r.op(V, lambda E: E.tensor_copy(r64b[:], r64[:]))
        r.op(V, lambda E: E.tensor_scalar_mul(invf[:], invf[:], -float(np.log(10000.0)) / 32.0))
        r.wait(Sc, "v")
        r.op(Sc, lambda E: E.activation(invf[:], invf[:], AF.Exp))
        r.wait(V, "s")
        r.op(V, lambda E: E.tensor_scalar_mul(qnw_t[:], qnw_t[:], SCALE))

        dsc = nc.dram_tensor("dsc", [1, 512], F32, kind="Internal").ap()
        NTK = S // 512
        p1_ld = {}
        r.wait("sync", "qs", p0_done)
        for tgk in range(NTK):
            for c in range(NCH):
                i = tgk * NCH + c
                if i >= 2:
                    r.wait("gpsimd", "t", p1_ld[i - 2])
                    r.wait("sync", "t", p1_ld[i - 2])
                r.op("sync", lambda E, c=c, tgk=tgk, i=i: E.dma_start(
                    out=xbf_s[:, i % 2, :], in_=xkvbf3[c, :, tgk * 512:(tgk + 1) * 512]),
                    inc="qs", k=16)
                r.op("gpsimd", lambda E, c=c, i=i: E.dma_start(
                    out=wkvd_s[:, i % 2, :], in_=wkvdT.rearrange(
                        "(c p) f -> c p f", p=128)[c, :, :]), inc="qg", k=16)
                r.wait("tensor", "qg")
                r.wait("tensor", "qs")
                if c == 0:
                    r.wait("tensor", "v")
                for rb in range(4):
                    r.op("tensor", lambda E, c=c, rb=rb, i=i: E.matmul(
                        bank(rb), wkvd_s[:, i % 2, rb * 128:(rb + 1) * 128],
                        xbf_s[:, i % 2, :], start=(c == 0), stop=(c == NCH - 1)))
                r.op("tensor", lambda E, c=c, i=i: E.matmul(
                    bank(4)[0:64, :], wkr_bf[:, c, :], xbf_s[:, i % 2, :],
                    start=(c == 0), stop=(c == NCH - 1)), inc="t")
                p1_ld[i] = r.cnt["t"]
            r.wait(V, "t")
            for rb in range(4):
                r.op(V, lambda E, rb=rb: E.tensor_copy(prodscr[:, rb, :], bank(rb)))
            r.op(V, lambda E, tgk=tgk: E.tensor_copy(
                kro_raw[:, tgk * 512:(tgk + 1) * 512], bank(4)[0:64, :]), inc="v")
            r.wait(Sc, "v")
            for rb in range(4):
                if rb >= 2:
                    r.wait(Sc, "t")
                r.op(Sc, lambda E, rb=rb: E.activation(sqscr[:, rb % 2, :], prodscr[:, rb, :],
                                                       AF.Square), inc="s")
                r.wait("tensor", "s")
                r.op("tensor", lambda E, rb=rb: E.matmul(
                    bank(6)[0:1, :], ones_col[:], sqscr[:, rb % 2, :],
                    start=(rb == 0), stop=(rb == 3)), inc="t")
            r.wait(Sc, "t")
            r.op(Sc, lambda E: E.activation(rows_rms[:, 0:512], bank(6)[0:1, :], AF.Ln,
                                            scale=1.0 / 512.0, bias=eps11[:]), inc="s")
            r.op(Sc, lambda E: E.activation(rows_rms[:, 0:512], rows_rms[:, 0:512], AF.Exp,
                                            scale=-0.5), inc="s")
            r.wait("tensor", "s")
            r.op("tensor", lambda E: E.matmul(bank(7), ones_row[:], rows_rms[:, 0:512],
                                              start=True, stop=True), inc="t")
            r.wait(V, "t")
            r.op(V, lambda E: E.tensor_copy(rbref[:], bank(7)))
            for rb in range(4):
                r.op(V, lambda E, rb=rb: E.tensor_mul(prodscr[:, rb, :], prodscr[:, rb, :],
                                                      rbref[:]))
                r.op(V, lambda E, rb=rb, tgk=tgk: E.tensor_scalar_mul(
                    kvc_bf[:, rb, tgk * 512:(tgk + 1) * 512], prodscr[:, rb, :],
                    kvnw_t[:, rb:rb + 1]), inc="v")

        NTQ = TOK // 512
        p2_ld = {}
        for tgq in range(NTQ):
            for c in range(NCH):
                i = tgq * NCH + c
                if i >= 2:
                    r.wait("gpsimd", "t", p2_ld[i - 2])
                    r.wait("sync", "t", p2_ld[i - 2])
                else:
                    r.wait("sync", "t", p1_ld[NTK * NCH - 2 + i])
                r.op("sync", lambda E, c=c, tgq=tgq, i=i: E.dma_start(
                    out=xbf_s[:, i % 2, :], in_=xkvbf3[c, :, TOK + tgq * 512:TOK + (tgq + 1) * 512]),
                    inc="qs", k=16)
                r.op("gpsimd", lambda E, c=c, i=i: E.dma_start(
                    out=wqd_s[:, i % 2, :], in_=wqdT.rearrange(
                        "(c p) f -> c p f", p=128)[c, :, :]), inc="qg", k=16)
                r.wait("tensor", "qg")
                r.wait("tensor", "qs")
                if c == 0:
                    r.wait("tensor", "v")
                for rb in range(8):
                    r.op("tensor", lambda E, c=c, rb=rb, i=i: E.matmul(
                        bank(rb), wqd_s[:, i % 2, rb * 128:(rb + 1) * 128],
                        xbf_s[:, i % 2, :], start=(c == 0), stop=(c == NCH - 1)),
                        inc=("t" if rb == 7 else None))
                p2_ld[i] = r.cnt["t"]
            r.wait(V, "t")
            for rb in range(8):
                r.op(V, lambda E, rb=rb: E.tensor_copy(prodscr[:, rb, :], bank(rb)), inc="v")
            r.wait(Sc, "v")
            for rb in range(8):
                if rb >= 2:
                    r.wait(Sc, "t")
                r.op(Sc, lambda E, rb=rb: E.activation(sqscr[:, rb % 2, :], prodscr[:, rb, :],
                                                       AF.Square), inc="s")
                r.wait("tensor", "s")
                r.wait("tensor", "v")
                r.op("tensor", lambda E, rb=rb: E.matmul(
                    bank(0)[0:1, :], ones_col[:], sqscr[:, rb % 2, :],
                    start=(rb == 0), stop=(rb == 7)), inc="t")
            r.wait(Sc, "t")
            r.op(Sc, lambda E: E.activation(rows_rms[:, 0:512], bank(0)[0:1, :], AF.Ln,
                                            scale=1.0 / 1024.0, bias=eps11[:]), inc="s")
            r.op(Sc, lambda E: E.activation(rows_rms[:, 0:512], rows_rms[:, 0:512], AF.Exp,
                                            scale=-0.5), inc="s")
            r.wait("tensor", "s")
            r.op("tensor", lambda E: E.matmul(bank(1), ones_row[:], rows_rms[:, 0:512],
                                              start=True, stop=True), inc="t")
            r.wait(V, "t")
            r.op(V, lambda E: E.tensor_copy(rbref[:], bank(1)))
            for rb in range(8):
                r.op(V, lambda E, rb=rb: E.tensor_mul(prodscr[:, rb, :], prodscr[:, rb, :],
                                                      rbref[:]))
                r.op(V, lambda E, rb=rb, tgq=tgq: E.tensor_scalar_mul(
                    qc_bf[:, rb, tgq * 512:(tgq + 1) * 512], prodscr[:, rb, :],
                    qnw_t[:, rb:rb + 1]), inc="v")

        r.wait("tensor", "v")
        for g in range(8):
            r.op("sync", lambda E, g=g: E.dma_start(
                out=pos_sl[:], in_=pos[:, g * 512:(g + 1) * 512]), inc="qs", k=16)
            r.wait("tensor", "qs")
            r.op("tensor", lambda E, g=g: E.matmul(
                bank(2 + g % 2)[0:32, :], invf[:], pos_sl[:], start=True, stop=True), inc="t")
            r.wait(V, "t")
            r.op(V, lambda E, g=g: E.tensor_copy(ftab[:, g * 512:(g + 1) * 512],
                                                 bank(2 + g % 2)[0:32, :]), inc="v")
            r.wait("sync", "v")
            r.wait("tensor", "v")
        for qtr, dst in ((0.0, sink2), (0.25, cosk2)):
            for q4 in range(4):
                sl = slice(q4 * 1024, (q4 + 1) * 1024)
                r.wait(V, "s")
                r.op(V, lambda E, sl=sl: E.tensor_scalar_mul(trig_s1[:], ftab[:, sl], inv2pi))
                if qtr:
                    r.op(V, lambda E: E.tensor_scalar_add(trig_s1[:], trig_s1[:], 0.25))
                r.op(V, lambda E: E.tensor_copy(trig_i[:], trig_s1[:]))
                r.op(V, lambda E: E.tensor_copy(trig_s2[:], trig_i[:]))
                r.op(V, lambda E: E.tensor_sub(trig_s1[:], trig_s1[:], trig_s2[:]))
                r.op(V, lambda E: E.tensor_scalar_mul(trig_s1[:], trig_s1[:], twopi), inc="v")
                r.wait(Sc, "v")
                r.op(Sc, lambda E, dst=dst, sl=sl: E.activation(
                    dst[0:32, sl], trig_s1[:], AF.Sin), inc="s")
                if dbg and dst is cosk2 and sl.start == 3072:
                    r.wait("sync", "v")
                    r.wait("sync", "s")
                    r.op("sync", lambda E: E.dma_start(out=dbg_ang, in_=trig_s1[:]), inc="so", k=16)
                    r.op("sync", lambda E: E.dma_start(out=dbg_i, in_=trig_i[:]), inc="so", k=16)
                    r.op("sync", lambda E: E.dma_start(out=dbg_s2, in_=trig_s2[:]), inc="so", k=16)
                    r.wait("vector", "so")
        if dbg:
            r.wait("sync", "s")
            r.wait("sync", "v")
            r.op("sync", lambda E: E.dma_start(out=dbg_ftab, in_=ftab), inc="so", k=16)
            r.op("sync", lambda E: E.dma_start(out=dbg_sin0, in_=sink2[:]), inc="so", k=16)
            r.op("sync", lambda E: E.dma_start(out=dbg_kvc2, in_=kvc_bf[:]), inc="so", k=16)
            r.wait("gpsimd", "so")
            r.wait("vector", "so")
        r.wait("gpsimd", "s")
        r.op("gpsimd", lambda E: E.dma_start(out=sink2[32:64, :], in_=sink2[0:32, :]),
             inc="qg", k=16)
        r.op("gpsimd", lambda E: E.dma_start(out=cosk2[32:64, :], in_=cosk2[0:32, :]),
             inc="qg", k=16)

        r.wait("tensor", "qg")
        r.wait(V, "qg")
        r.wait("tensor", "v")
        for g in range(8):
            sl = slice(g * 512, (g + 1) * 512)
            r.op("tensor", lambda E, sl=sl: E.matmul(
                bank(2)[0:64, :], r64b[:], kro_raw[:, sl], start=True, stop=True), inc="t")
            r.wait(V, "t")
            r.op(V, lambda E, sl=sl: E.tensor_tensor(qpe_f[:, 0, :], kro_raw[:, sl],
                                                     cosk2[:, sl], ALU.mult))
            r.op(V, lambda E, sl=sl: E.tensor_tensor(qpe_f[:, 1, :], bank(2)[0:64, :],
                                                     sink2[:, sl], ALU.mult))
            r.op(V, lambda E, sl=sl: E.tensor_add(krope2[:, sl], qpe_f[:, 0, :],
                                                  qpe_f[:, 1, :]), inc="v")
            r.wait("tensor", "v")

        wq3 = wquT.rearrange("(c p) f -> c p f", p=128)
        wqr3 = wqrT.rearrange("(c p) f -> c p f", p=128)
        wkvu3 = wkvuT.rearrange("(c p) f -> c p f", p=128)
        head_ms = {}
        for h in range(NH):
            if h >= 1:
                r.wait("gpsimd", "t", head_ms.get(h - 1, 0))
            for c in range(8):
                r.op("gpsimd", lambda E, h=h, c=c: E.dma_start(
                    out=wup_h[:, h % 2, c, :], in_=wq3[c, :, h * 128:(h + 1) * 128]),
                    inc="qg", k=16)
                r.op("gpsimd", lambda E, h=h, c=c: E.dma_start(
                    out=wqr_h[:, h % 2, c, :], in_=wqr3[c, :, h * 64:(h + 1) * 64]),
                    inc="qg", k=16)
            for c in range(4):
                r.op("gpsimd", lambda E, h=h, c=c: E.dma_start(
                    out=wkvu_h[:, h % 2, c, :], in_=wkvu3[c, :, h * 256:(h + 1) * 256]),
                    inc="qg", k=16)
            r.wait("tensor", "qg")
            r.wait("tensor", "v")
            r.wait("tensor", "so")
            kn_cp = {}
            for g in range(8):
                if g >= 2:
                    r.wait("tensor", "v", kn_cp[g - 2])
                for rc in range(4):
                    r.op("tensor", lambda E, h=h, g=g, rc=rc: E.matmul(
                        bank(g % 2), wkvu_h[:, h % 2, rc, 0:128],
                        kvc_bf[:, rc, g * 512:(g + 1) * 512],
                        start=(rc == 0), stop=(rc == 3)), inc=("t" if rc == 3 else None))
                r.wait(V, "t")
                r.op(V, lambda E, g=g: E.tensor_copy(knope[:, g * 512:(g + 1) * 512],
                                                     bank(g % 2)), inc="v")
                kn_cp[g] = r.cnt["v"]
            r.op(V, lambda E: E.memset(vext[:, :, 64:65], 1.0))
            r.op(V, lambda E: E.memset(vext[:, :, 129:130], 1.0), inc="v")
            vx_cp = {}
            for kt in range(NKT):
                if kt >= 2:
                    r.wait("tensor", "v", vx_cp[kt - 2])
                for rc in range(4):
                    r.op("tensor", lambda E, h=h, kt=kt, rc=rc: E.matmul(
                        bank(2 + kt % 2)[:, 0:128], kvc_bf[:, rc, kt * 128:(kt + 1) * 128],
                        wkvu_h[:, h % 2, rc, 128:256],
                        start=(rc == 0), stop=(rc == 3)), inc=("t" if rc == 3 else None))
                r.wait(V, "t")
                r.op(V, lambda E, kt=kt: E.tensor_copy(vext[:, kt, 0:64],
                                                       bank(2 + kt % 2)[:, 0:64]))
                r.op(V, lambda E, kt=kt: E.tensor_copy(vext[:, kt, 65:129],
                                                       bank(2 + kt % 2)[:, 64:128]), inc="v")
                vx_cp[kt] = r.cnt["v"]
            qn_cp = {}
            for g in range(4):
                if g >= 2:
                    r.wait("tensor", "v", qn_cp[g - 2])
                for rc in range(8):
                    r.op("tensor", lambda E, h=h, g=g, rc=rc: E.matmul(
                        bank(4 + g % 2), wup_h[:, h % 2, rc, :],
                        qc_bf[:, rc, g * 512:(g + 1) * 512],
                        start=(rc == 0), stop=(rc == 7)), inc=("t" if rc == 7 else None))
                r.wait(V, "t")
                r.op(V, lambda E, g=g: E.tensor_copy(qnope[:, g * 512:(g + 1) * 512],
                                                     bank(4 + g % 2)), inc="v")
                qn_cp[g] = r.cnt["v"]
            for g in range(4):
                for rc in range(8):
                    r.op("tensor", lambda E, h=h, g=g, rc=rc: E.matmul(
                        bank(6)[0:64, :], wqr_h[:, h % 2, rc, :],
                        qc_bf[:, rc, g * 512:(g + 1) * 512],
                        start=(rc == 0), stop=(rc == 7)), inc=("t" if rc == 7 else None))
                r.wait(V, "t")
                r.op(V, lambda E, g=g: E.tensor_copy(qpe_f[:, 0, :], bank(6)[0:64, :]), inc="v")
                r.wait("tensor", "v")
                r.op("tensor", lambda E, g=g: E.matmul(
                    bank(7)[0:64, :], r64[:], qpe_f[:, 0, :], start=True, stop=True), inc="t")
                qsl = slice(TOK + g * 512, TOK + (g + 1) * 512)
                r.wait(V, "t")
                r.op(V, lambda E, g=g, qsl=qsl: E.tensor_tensor(
                    qpe_f[:, 1, :], qpe_f[:, 0, :], cosk2[:, qsl], ALU.mult))
                r.op(V, lambda E, g=g, qsl=qsl: E.tensor_tensor(
                    qpe_f[:, 0, :], bank(7)[0:64, :], sink2[:, qsl], ALU.mult))
                r.op(V, lambda E, g=g: E.tensor_add(
                    qpe[:, g * 512:(g + 1) * 512], qpe_f[:, 0, :], qpe_f[:, 1, :]), inc="v")
                r.wait("tensor", "v")
            r.op(V, lambda E: E.reduce_max(mxq[:, 0:1], knope[:], axis=AX.X,
                                           apply_absolute_value=True))
            r.op(V, lambda E: E.reduce_max(mxq[0:64, 2:3], krope2[:], axis=AX.X,
                                           apply_absolute_value=True), inc="v")
            for j in (0, 2):
                r.wait("tensor", "v")
                r.op("tensor", lambda E, j=j: E.transpose(
                    bank(0)[0:1, 0:128] if j == 0 else bank(0)[0:1, 0:64],
                    mxq[:, j:j + 1] if j == 0 else mxq[0:64, j:j + 1],
                    ident[:] if j == 0 else ident[0:64, 0:64]), inc="t")
                r.wait(V, "t")
                r.op(V, lambda E, j=j: E.reduce_max(
                    msc[:, j:j + 1], bank(0)[0:1, 0:128] if j == 0 else bank(0)[0:1, 0:64],
                    axis=AX.X), inc="v")
            r.op(V, lambda E: E.memset(msc[:, 7:8], 0.0), inc="v")
            for g in range(4):
                gsl = slice(g * 512, (g + 1) * 512)
                r.wait(Sc, "v")
                r.wait(Sc, "t")
                r.op(Sc, lambda E, gsl=gsl: E.activation(sqscr[:, 0, :], qnope[:, gsl],
                                                         AF.Square), inc="s")
                r.op(Sc, lambda E, gsl=gsl: E.activation(sqscr[0:64, 1, :], qpe[:, gsl],
                                                         AF.Square), inc="s")
                r.wait("tensor", "s")
                r.wait("tensor", "v")
                r.op("tensor", lambda E: E.matmul(bank(1)[0:1, :], ones_col[:],
                                                  sqscr[:, 0, :], start=True, stop=False))
                r.op("tensor", lambda E: E.matmul(bank(1)[0:1, :], ones_col[0:64, :],
                                                  sqscr[0:64, 1, :], start=False, stop=True),
                     inc="t")
                r.wait(V, "t")
                r.op(V, lambda E: E.reduce_max(msc[:, 6:7], bank(1)[0:1, :], axis=AX.X))
                r.op(V, lambda E: E.tensor_max(msc[:, 7:8], msc[:, 7:8], msc[:, 6:7]), inc="v")
            r.op(V, lambda E: E.tensor_mul(msc[:, 4:5], msc[:, 0:1], msc[:, 0:1]))
            r.op(V, lambda E: E.tensor_scalar_mul(msc[:, 4:5], msc[:, 4:5], 128.0))
            r.op(V, lambda E: E.tensor_mul(msc[:, 5:6], msc[:, 2:3], msc[:, 2:3]))
            r.op(V, lambda E: E.tensor_scalar_mul(msc[:, 5:6], msc[:, 5:6], 64.0))
            r.op(V, lambda E: E.tensor_add(msc[:, 4:5], msc[:, 4:5], msc[:, 5:6]))
            r.op(V, lambda E: E.tensor_mul(msc[:, 4:5], msc[:, 4:5], msc[:, 7:8]), inc="v")
            r.wait(Sc, "v")
            r.op(Sc, lambda E: E.activation(msc[:, 4:5], msc[:, 4:5], AF.Sqrt), inc="s")
            r.wait(V, "s")
            r.op(V, lambda E: E.tensor_single_scalar(biasr[:], maskrow[:], 0.0, ALU.add))
            r.op(V, lambda E: E.tensor_scalar_mul(msc[:, 4:5], msc[:, 4:5], -1.0))
            r.op(V, lambda E: E.tensor_scalar_add(biasr[:], biasr[:], msc[:, 4:5]), inc="v")
            r.wait("tensor", "v")
            r.op("tensor", lambda E: E.matmul(bank(0)[:, 0:128], ones_row[:], biasr[:],
                                              start=True, stop=True), inc="t")
            r.wait(V, "t")
            r.op(V, lambda E: E.tensor_copy(bias_h[:], bank(0)[:, 0:128]), inc="v")

            r.wait(Sc, "v")
            r.wait("tensor", "v")
            exp_gp = {}

            av_gp = {}
            P = 0
            ev_v = r.cnt["v"]
            for j in range(4):
                nkt = 20 + 4 * j
                npair = nkt // 2
                qsl = slice(j * 512, (j + 1) * 512)
                mask_v = {}

                def emit_av(Pa, pa):
                    pb_ = Pa % 2
                    kt0_ = 2 * pa
                    if pa == 0:
                        r.wait("tensor", "v", ev_v)
                    if Pa in mask_v:
                        r.wait("tensor", "v", mask_v[Pa])
                    else:
                        r.wait("tensor", "s", exp_gp[Pa])
                    for q in (0, 1):
                        kt = kt0_ + q
                        psl = slice(pb_ * 1024 + q * 512, pb_ * 1024 + (q + 1) * 512)
                        r.op("tensor", lambda E, kt=kt, nkt=nkt, psl=psl: E.matmul(
                            bank(4)[0:65, :], vext[:, kt, 0:65], ptile[:, psl],
                            start=(kt == 0), stop=(kt == nkt - 1)))
                        r.op("tensor", lambda E, kt=kt, nkt=nkt, psl=psl: E.matmul(
                            bank(5)[0:65, :], vext[:, kt, 65:130], ptile[:, psl],
                            start=(kt == 0), stop=(kt == nkt - 1)),
                            inc=("t" if q == 1 else None))
                    av_gp[Pa] = r.cnt["t"]

                for p in range(npair):
                    kt0 = 2 * p
                    pb = P % 2
                    if P >= 2:
                        r.wait("tensor", "s", exp_gp[P - 2])
                    for q in (0, 1):
                        kt = kt0 + q
                        sb_ = bank(2 * pb + q)
                        r.op("tensor", lambda E, kt=kt, qsl=qsl, sb_=sb_: E.matmul(
                            sb_, knope[:, kt * 128:(kt + 1) * 128], qnope[:, qsl],
                            start=True, stop=False))
                        r.op("tensor", lambda E, kt=kt, qsl=qsl, sb_=sb_: E.matmul(
                            sb_, krope2[:, kt * 128:(kt + 1) * 128], qpe[:, qsl],
                            start=False, stop=True), inc=("t" if q == 1 else None))
                    smm = r.cnt["t"]
                    r.wait(Sc, "t", smm)
                    if P >= 2:
                        r.wait(Sc, "t", av_gp[P - 2])
                    bcol = j * 32 + kt0
                    r.op(Sc, lambda E, pb=pb, bcol=bcol: E.activation(
                        ptile[:, pb * 1024:(pb + 1) * 1024], PS[:, pb * 1024:(pb + 1) * 1024],
                        AF.Exp, bias=bias_h[:, bcol:bcol + 1]), inc="s")
                    exp_gp[P] = r.cnt["s"]
                    dd0 = kt0 - (16 + 4 * j)
                    if dd0 >= 0:
                        r.wait(V, "s", exp_gp[P])
                        for q in (0, 1):
                            r.op(V, lambda E, pb=pb, q=q, dd0=dd0: E.tensor_tensor(
                                ptile[:, pb * 1024 + q * 512:pb * 1024 + (q + 1) * 512],
                                ptile[:, pb * 1024 + q * 512:pb * 1024 + (q + 1) * 512],
                                m01[:, dd0 + q, :], ALU.mult),
                                inc=("v" if q == 1 else None))
                        mask_v[P] = r.cnt["v"]
                    if p > 0:
                        emit_av(P - 1, p - 1)
                    P += 1
                emit_av(P - 1, npair - 1)
                r.wait(V, "t", av_gp[P - 1])
                r.op(V, lambda E: E.tensor_copy(rows_rms[:, 512:1024], bank(4)[64:65, :]))
                r.op(V, lambda E: E.tensor_scalar_add(rows_rms[:, 512:1024],
                                                      rows_rms[:, 512:1024], 1e-38))
                r.wait(V, "v")
                r.op(V, lambda E: E.reciprocal(rows_rms[:, 512:1024], rows_rms[:, 512:1024]),
                     inc="v")
                r.wait("sync", "v")
                r.op("sync", lambda E: E.dma_start(out=dsc, in_=rows_rms[0:1, 512:1024]),
                     inc="qs", k=16)
                r.wait("sync", "qs")
                r.op("sync", lambda E: E.dma_start(out=rbref[:],
                                                   in_=dsc.broadcast_to((128, 512))),
                     inc="qs", k=16)
                r.wait(V, "qs")
                r.wait(V, "so")
                r.op(V, lambda E, j=j: E.scalar_tensor_tensor(
                    att_t[0:64, 0, :], rbref[0:64, :], 1.0, bank(4)[0:64, :],
                    ALU.mult, ALU.mult))
                r.op(V, lambda E, j=j: E.scalar_tensor_tensor(
                    att_t[64:128, 0, :], rbref[64:128, :], 1.0, bank(5)[0:64, :],
                    ALU.mult, ALU.mult), inc="v")
                ev_v = r.cnt["v"]
                r.wait("sync", "v")
                r.op("sync", lambda E, h=h, j=j: E.dma_start(
                    out=asp3[h, :, j * 512:(j + 1) * 512], in_=att_t[:, 0, :]),
                    inc="so", k=16)
            head_ms[h] = r.cnt["t"]

        woT3 = woT.rearrange("(c p) d -> c p d", p=128)
        for qg in range(4):
            r.wait("gpsimd", "so")
            r.wait("gpsimd", "t")
            for hc in range(NCH):
                r.op("gpsimd", lambda E, qg=qg, hc=hc: E.dma_start(
                    out=att_qg[:, hc, :], in_=asp3[hc, :, qg * 512:(qg + 1) * 512]),
                    inc="qg", k=16)
            for dh in range(2):
                r.wait("gpsimd", "t")
                for hc in range(NCH):
                    r.op("gpsimd", lambda E, qg=qg, dh=dh, hc=hc: E.dma_start(
                        out=wo_s[:, hc, :], in_=woT3[hc, :, dh * 1024:(dh + 1) * 1024]),
                        inc="qg", k=16)
                r.wait("tensor", "qg")
                if qg > 0 or dh > 0:
                    r.wait("tensor", "v")
                for dblk in range(8):
                    for hc in range(NCH):
                        r.op("tensor", lambda E, dblk=dblk, hc=hc: E.matmul(
                            bank(dblk), wo_s[:, hc, dblk * 128:(dblk + 1) * 128],
                            att_qg[:, hc, :], start=(hc == 0), stop=(hc == NCH - 1)),
                            inc=("t" if hc == NCH - 1 else None))
                for dblk in range(8):
                    c = dh * 8 + dblk
                    if dblk >= 2 or dh > 0 or qg > 0:
                        r.wait("sync", "v")
                    r.op("sync", lambda E, c=c, qg=qg, dblk=dblk: E.dma_start(
                        out=x32k[:, dblk % 2, :],
                        in_=xkv3[c, :, TOK + qg * 512:TOK + (qg + 1) * 512]), inc="qs", k=16)
                    r.wait(V, "qs")
                    r.wait(V, "t")
                    r.op(V, lambda E, c=c, dblk=dblk: E.scalar_tensor_tensor(
                        pre[:, c, :], x32k[:, dblk % 2, :], 1.0, bank(dblk),
                        ALU.mult, ALU.add), inc="v")
            r.wait("scalar", "v")
            _ln_emit(r, pre, NCH, sqscr, PS[0:1, 2048:2560], PS[0:1, 2560:3072],
                     ones_col, ones_row, rowscr_f, abc, bbc,
                     [PS[:, 3072:3584], PS[:, 3584:4096]], 512)
            r.wait("sync", "v")
            for c in range(NCH):
                r.op("sync", lambda E, c=c, qg=qg: E.dma_start(
                    out=x33[c, :, qg * 512:(qg + 1) * 512], in_=pre[:, c, :]), inc="so", k=16)
        if dbg:
            r.wait("sync", "v")
            r.wait("sync", "s")
            r.op("sync", lambda E: E.dma_start(out=dbg_qc, in_=qc_bf[:]), inc="so", k=16)
            r.op("sync", lambda E: E.dma_start(out=dbg_kvc, in_=kvc_bf[:]), inc="so", k=16)
            r.op("sync", lambda E: E.dma_start(out=dbg_kr, in_=krope2[:]), inc="so", k=16)
            r.op("sync", lambda E: E.dma_start(out=dbg_cos, in_=cosk2[:]), inc="so", k=16)
            r.op("sync", lambda E: E.dma_start(out=dbg_asp, in_=aspill), inc="so", k=16)
            r.op("sync", lambda E: E.dma_start(out=dbg_kn15, in_=knope[:]), inc="so", k=16)
            r.op("sync", lambda E: E.dma_start(out=dbg_bias, in_=bias_h[:]), inc="so", k=16)
            r.op("sync", lambda E: E.dma_start(out=dbg_msc, in_=msc[:]), inc="so", k=16)
            r.op("sync", lambda E: E.dma_start(out=dbg_pt, in_=ptile[:]), inc="so", k=16)
            r.op("sync", lambda E: E.dma_start(out=dbg_qn15, in_=qnope[:]), inc="so", k=16)
            r.op("sync", lambda E: E.dma_start(out=dbg_qp15, in_=qpe[:]), inc="so", k=16)
        r.wait("sync", "so")
        r.play(block)
    return nc


def build_launch_D():
    nc = bass.Bass("TRN2", target_bir_lowering=False, debug=False)
    x3 = nc.dram_tensor("x3", [D, TOK], F32, kind="ExternalInput").ap()
    w_o = nc.dram_tensor("w_o", [D, D], F32, kind="ExternalInput").ap()
    wT = nc.dram_tensor("wT", [D, D], F32, kind="ExternalInput").ap()
    bcol = nc.dram_tensor("bcol", [128, NCH], F32, kind="ExternalInput").ap()
    x4 = nc.dram_tensor("x4", [D, TOK], F32, kind="ExternalOutput").ap()

    x33 = x3.rearrange("(c p) t -> c p t", p=128)
    x43 = x4.rearrange("(c p) t -> c p t", p=128)
    wo3 = w_o.rearrange("(c p) i -> c p i", p=128)
    wT3 = wT.rearrange("(c p) o -> c p o", p=128)

    with ExitStack() as ctx:
        sb = lambda n, s, d=F32: ctx.enter_context(nc.sbuf_tensor(n, s, d))
        r = Rec(nc, ctx)
        for s_ in ("qs", "qg", "v", "s", "t", "so"):
            r.sem(s_)

        xbf = sb("xbf", [128, NCH, TOK], BF16)
        sgn = sb("sgn", [128, NCH, D], BF16)
        wld = sb("wld", [128, 2, D])
        alpha = sb("alpha", [128, NCH])
        bias_t = sb("bias_t", [128, NCH])
        x32 = sb("x32", [128, 2, 512])
        scr = sb("scr", [128, 2048])
        pre = sb("pre", [128, NCH, 512])
        sqscr = sb("sqscr", [128, 2, 512])
        rowscr = sb("rowscr", [1, 2048])
        abc = sb("abc", [128, 512])
        bbc = sb("bbc", [128, 512])
        ones_col = sb("ones_col", [128, 1])
        ones_row = sb("ones_row", [1, 128])

        PS = ctx.enter_context(nc.psum_tensor("PS", [128, 4096], F32))
        bank = lambda i: PS[:, 512 * i:512 * (i + 1)]

        block = ctx.enter_context(nc.Block())
        V, Sc = "vector", "scalar"

        r.op("sync", lambda E: E.dma_start(out=bias_t[:], in_=bcol), inc="qs", k=16)
        r.wait(V, "qs")
        r.op(V, lambda E: E.memset(ones_col[:], 1.0))
        r.op(V, lambda E: E.memset(ones_row[:], 1.0))

        ld_ms = {}
        for c in range(NCH):
            if c >= 2:
                r.wait("sync", "v", ld_ms[c - 2])
                r.wait("gpsimd", "v", ld_ms[c - 2])
            r.op("sync", lambda E, c=c: E.dma_start(out=wld[:, c % 2, :], in_=wo3[c]),
                 inc="qs", k=16)
            r.op("gpsimd", lambda E, c=c: E.dma_start(out=xbf[:, c, :], in_=x33[c]),
                 inc="qg", k=16)
            r.wait(V, "qs")
            r.op(V, lambda E, c=c: E.tensor_reduce(alpha[:, c:c + 1], wld[:, c % 2, :],
                                                   AX.X, ALU.add, apply_absolute_value=True),
                 inc="v")
            ld_ms[c] = r.cnt["v"]
        for c in range(NCH):
            r.wait("sync", "v")
            r.op("sync", lambda E, c=c: E.dma_start(out=wld[:, c % 2, :], in_=wT3[c]),
                 inc="qs", k=16)
            r.wait(V, "qs")
            r.op(V, lambda E, c=c: E.tensor_single_scalar(scr[:], wld[:, c % 2, :],
                                                          0.0, ALU.is_ge))
            r.op(V, lambda E, c=c: E.tensor_scalar(sgn[:, c, :], scr[:], 2.0, -1.0,
                                                   ALU.mult, ALU.add), inc="v")
        r.op(V, lambda E: E.tensor_scalar_mul(alpha[:], alpha[:], 1.0 / float(D)), inc="v")

        r.wait("tensor", "qg")
        r.wait("tensor", "v")
        for tg in range(4):
            for dblk in range(NCH):
                if tg + dblk > 0:
                    r.wait("tensor", "v")
                for c in range(NCH):
                    r.op("tensor", lambda E, c=c, dblk=dblk, tg=tg: E.matmul(
                        bank(dblk % 2), sgn[:, c, dblk * 128:(dblk + 1) * 128],
                        xbf[:, c, tg * 512:(tg + 1) * 512],
                        start=(c == 0), stop=(c == NCH - 1)), inc=("t" if c == NCH - 1 else None))
                if dblk >= 2:
                    r.wait("sync", "v")
                r.op("sync", lambda E, dblk=dblk, tg=tg: E.dma_start(
                    out=x32[:, dblk % 2, :], in_=x33[dblk, :, tg * 512:(tg + 1) * 512]),
                    inc="qs", k=16)
                r.wait(V, "qs")
                r.wait(V, "t")
                r.op(V, lambda E, dblk=dblk: E.tensor_scalar(
                    pre[:, dblk, :], bank(dblk % 2), alpha[:, dblk:dblk + 1],
                    bias_t[:, dblk:dblk + 1], ALU.mult, ALU.add))
                r.op(V, lambda E, dblk=dblk: E.tensor_add(
                    pre[:, dblk, :], pre[:, dblk, :], x32[:, dblk % 2, :]), inc="v")
            r.wait("scalar", "v")
            _ln_emit(r, pre, NCH, sqscr, PS[0:1, 2048:2560], PS[0:1, 2560:3072],
                     ones_col, ones_row, rowscr, abc, bbc,
                     [PS[:, 3072:3584], PS[:, 3584:4096]], 512)
            r.wait("sync", "v")
            if tg > 0:
                pass
            for c in range(NCH):
                r.op("sync", lambda E, c=c, tg=tg: E.dma_start(
                    out=x43[c, :, tg * 512:(tg + 1) * 512], in_=pre[:, c, :]), inc="so", k=16)
            r.wait("vector", "so")
        r.wait("sync", "so")
        r.play(block)
    return nc


def _prep_launch_D(inputs, x3_shards):
    w = np.asarray(inputs["L1_ffn_weight"], np.float32)
    base = {
        "w_o": np.ascontiguousarray(w),
        "wT": np.ascontiguousarray(w.T),
        "bcol": np.ascontiguousarray(np.asarray(inputs["L1_ffn_bias"], np.float32)
                                     .reshape(NCH, 128).T),
    }
    in_maps = []
    for cid in range(NCORE):
        m = dict(base)
        m["x3"] = np.ascontiguousarray(x3_shards[cid])
        in_maps.append(m)
    return in_maps


def run_launch_D(inputs, x3_shards):
    nc = build_launch_D()
    in_maps = _prep_launch_D(inputs, x3_shards)
    res = run_bass_kernel_spmd(nc, in_maps, core_ids=list(range(NCORE)))
    return [res.results[cid]["x4"] for cid in range(NCORE)]


def build_fused_AB():
    nc = bass.Bass("TRN2", target_bir_lowering=False, debug=False)
    dbg = False
    u_pref = nc.dram_tensor("u_pref", [D, S], F32, kind="ExternalInput").ap()
    lam_ur = nc.dram_tensor("lam_ur", [1, 32], F32, kind="ExternalInput").ap()
    lam_im = nc.dram_tensor("lam_im", [1, 32], F32, kind="ExternalInput").ap()
    logdt = nc.dram_tensor("logdt", [1, 64], F32, kind="ExternalInput").ap()
    bre = nc.dram_tensor("bre", [32, D], F32, kind="ExternalInput").ap()
    bim = nc.dram_tensor("bim", [32, D], F32, kind="ExternalInput").ap()
    cret = nc.dram_tensor("cret", [32, D], F32, kind="ExternalInput").ap()
    cimt = nc.dram_tensor("cimt", [32, D], F32, kind="ExternalInput").ap()
    dcol = nc.dram_tensor("dcol", [128, NCH], F32, kind="ExternalInput").ap()
    x1 = nc.dram_tensor("x1", [D, TOK], F32, kind="Internal").ap()
    rwt = nc.dram_tensor("rwt", [D, 8], F32, kind="ExternalInput").ap()
    w1 = nc.dram_tensor("w1", [8, D, 1024], F32, kind="ExternalInput").ap()
    w2 = nc.dram_tensor("w2", [8, D, 1024], F32, kind="ExternalInput").ap()
    w3 = nc.dram_tensor("w3", [8, 1024, D], F32, kind="ExternalInput").ap()
    x2 = nc.dram_tensor("x2", [D, TOK], F32, kind="ExternalOutput").ap()
    hsp = nc.dram_tensor("hsp", [8, 1024, TOK], BF16, kind="Internal").ap()
    x2pre = nc.dram_tensor("x2pre", [D, TOK], F32, kind="Internal").ap()

    NT = TOK // 512
    NFB = 8
    u3 = u_pref.rearrange("(c p) t -> c p t", p=128)
    x13 = x1.rearrange("(c p) t -> c p t", p=128)
    x23 = x2.rearrange("(c p) t -> c p t", p=128)
    x2p3 = x2pre.rearrange("(c p) t -> c p t", p=128)
    rwt3 = rwt.rearrange("(c p) e -> c p e", p=128)

    with ExitStack() as topctx:
        r = Rec(nc, topctx)
        for s_ in ("qs", "qg", "v", "s", "t", "so"):
            r.sem(s_)
        PS = topctx.enter_context(nc.psum_tensor("PS", [128, 4096], F32))
        block = topctx.enter_context(nc.Block())

        def barrier():
            for eng in ENGINES:
                for snm in ("qs", "qg", "v", "s", "t", "so"):
                    r.wait(eng, snm)

        def _phase_A():
            with ExitStack() as ctx:
                sb = lambda n, s, d=F32: ctx.enter_context(nc.sbuf_tensor("A_" + n, s, d))
                row_par = sb("row_par", [1, 32 * 24])
                logdt_t = sb("logdt_t", [1, 64])
                lam_ur_row = sb("lam_ur_t", [1, 32])
                lam_im_row = sb("lam_im_t", [1, 32])
                ki32 = sb("ki32", [1, 64], I32)
                dcol_t = sb("dcol_t", [128, NCH])
                d1col = sb("d1col", [128, NCH])
                idv = sb("idv", [128, 128])
                ident = sb("ident", [128, 128])
                ones_col = sb("ones_col", [128, 1])
                ones_row = sb("ones_row", [1, 128])
                scal6 = sb("scal6", [32, 6])
                a128c = sb("a128c", [32, 2])
                tp = sb("tp", [32, 2, 128])
                tm = sb("tm", [32, 2, 128])
                tcol = sb("tcol", [32, 2])
                wcol = sb("wcol", [32, 2])
                ccol = sb("ccol", [32, 4])
                ee = sb("ee", [32, 2, 32])
                bpack = sb("bpack", [32, 4, D])
                bre_t = bpack[:, 0, :]
                bim_t = bpack[:, 1, :]
                cret_t = bpack[:, 2, :]
                cimt_t = bpack[:, 3, :]
                d0s = bpack[:, 0:2, :].rearrange("n a b -> n (a b)")
                sca = bpack[:, 2, :]
                scb = bpack[:, 3, :]
                bbarp = sb("bbarp", [32, 2, D])
                bbre = bbarp[:, 0, :]
                bbim = bbarp[:, 1, :]
                scr2 = sb("scr2", [32, D])
                xs_bf = bbarp[:].rearrange("n a b -> n (a b)").bitcast(BF16).rearrange("n (a b) -> n a b", a=2)
                bbar_Tre = sb("bbar_Tre", [128, NCH, 32], BF16)
                bbar_Tim = sb("bbar_Tim", [128, NCH, 32], BF16)
                cpack_re = sb("cpack_re", [32, NCH, 128], BF16)
                cpack_im = sb("cpack_im", [32, NCH, 128], BF16)
                u_bf = sb("u_bf", [128, NCH, 512], BF16)
                bu = sb("bu", [32, 2, S])
                cum = sb("cum", [32, 2, S])
                u32b = sb("u32b", [128, 2, 512])
                pre = sb("pre", [128, NCH, 512])
                sqscr = sb("sqscr", [128, 2, 512])
                rowscr = sb("rowscr", [1, 2048])
                abc = sb("abc", [128, 512])
                bbc = sb("bbc", [128, 512])

                ps_bu_re = PS[0:32, 0:512]
                ps_bu_im = PS[0:32, 512:1024]
                ps_y = [PS[:, 0:512], PS[:, 512:1024]]
                ps_tr = PS[:, 1024:1056]
                ps_6 = PS[0:32, 1536:1542]
                ps_sum = PS[0:1, 2048:2560]
                ps_sumsq = PS[0:1, 2560:3072]
                ps_bc = [PS[:, 3072:3584], PS[:, 3584:4096]]

                ROWS = {}
                for i, nm in enumerate(
                    ["delta", "ab", "en", "ln1", "mx", "sp", "lamre", "rr", "th", "y",
                     "kf", "fr", "ang", "yc", "frc", "angc", "sinth", "costh", "er",
                     "are", "aim", "asq", "den", "tmp"]):
                    ROWS[nm] = row_par[:, 32 * i:32 * i + 32]


                for (dst, src) in [(lam_ur_row[:], lam_ur), (lam_im_row[:], lam_im),
                                   (logdt_t[:], logdt), (bre_t, bre), (bim_t, bim),
                                   (cret_t, cret), (cimt_t, cimt), (dcol_t[:], dcol)]:
                    r.op("sync", lambda E, dst=dst, src=src: E.dma_start(out=dst, in_=src), inc="qs", k=16)
                r.op("gpsimd", lambda E: E.iota(idv[:], pattern=[[-1, 128]], channel_multiplier=1,
                                                allow_small_or_imprecise_dtypes=True), inc="qg", k=1)

                r.wait("vector", "qs")
                r.wait("vector", "qg")
                r.wait("scalar", "qs")
                V, Sc = "vector", "scalar"
                R = ROWS
                r.op(V, lambda E: E.tensor_single_scalar(ident[:], idv[:], 0.0, ALU.is_equal), inc="v")
                r.op(V, lambda E: E.memset(ones_col[:], 1.0), inc="v")
                r.op(V, lambda E: E.memset(ones_row[:], 1.0), inc="v")
                r.op(V, lambda E: E.tensor_scalar_add(d1col[:], dcol_t[:], 1.0), inc="v")
                r.op(Sc, lambda E: E.activation(R["delta"], logdt_t[:, 0:32], AF.Exp), inc="s")
                r.op(Sc, lambda E: E.activation(R["ab"], lam_ur_row[:], AF.Abs), inc="s")
                r.op(Sc, lambda E: E.activation(R["en"], R["ab"], AF.Exp, scale=-1.0), inc="s")
                r.op(Sc, lambda E: E.activation(R["ln1"], R["en"], AF.Ln, bias=1.0), inc="s")
                r.wait(V, "s")
                r.op(V, lambda E: E.tensor_scalar_max(R["mx"], lam_ur_row[:], 0.0), inc="v")
                r.op(V, lambda E: E.tensor_add(R["sp"], R["mx"], R["ln1"]), inc="v")
                r.wait(Sc, "v")
                r.op(Sc, lambda E: E.activation(R["lamre"], R["sp"], AF.Copy, scale=-1.0), inc="s")
                r.wait(V, "s")
                r.op(V, lambda E: E.tensor_scalar_add(R["lamre"], R["lamre"], -0.0101), inc="v")
                r.op(V, lambda E: E.tensor_mul(R["rr"], R["lamre"], R["delta"]), inc="v")
                r.op(V, lambda E: E.tensor_mul(R["th"], lam_im_row[:], R["delta"]), inc="v")
                inv2pi = 1.0 / (2.0 * np.pi)
                twopi = 2.0 * np.pi
                r.op(V, lambda E: E.tensor_scalar_mul(R["y"], R["th"], inv2pi), inc="v")
                r.op(V, lambda E: E.tensor_copy(ki32[:, 0:32], R["y"]), inc="v")
                r.wait(V, "v")
                r.op(V, lambda E: E.tensor_copy(R["kf"], ki32[:, 0:32]), inc="v")
                r.wait(V, "v")
                r.op(V, lambda E: E.tensor_sub(R["fr"], R["y"], R["kf"]), inc="v")
                r.op(V, lambda E: E.tensor_scalar_mul(R["ang"], R["fr"], twopi), inc="v")
                r.op(V, lambda E: E.tensor_scalar_add(R["yc"], R["y"], 0.25), inc="v")
                r.op(V, lambda E: E.tensor_copy(ki32[:, 32:64], R["yc"]), inc="v")
                r.wait(V, "v")
                r.op(V, lambda E: E.tensor_copy(R["kf"], ki32[:, 32:64]), inc="v")
                r.wait(V, "v")
                r.op(V, lambda E: E.tensor_sub(R["frc"], R["yc"], R["kf"]), inc="v")
                r.op(V, lambda E: E.tensor_scalar_mul(R["angc"], R["frc"], twopi), inc="v")
                r.wait(Sc, "v")
                r.op(Sc, lambda E: E.activation(R["sinth"], R["ang"], AF.Sin), inc="s")
                r.op(Sc, lambda E: E.activation(R["costh"], R["angc"], AF.Sin), inc="s")
                r.op(Sc, lambda E: E.activation(R["er"], R["rr"], AF.Exp), inc="s")
                r.wait(V, "s")
                r.op(V, lambda E: E.tensor_mul(R["are"], R["er"], R["costh"]), inc="v")
                r.op(V, lambda E: E.tensor_mul(R["aim"], R["er"], R["sinth"]), inc="v")
                r.op(V, lambda E: E.tensor_mul(R["asq"], R["are"], R["are"]), inc="v")
                r.op(V, lambda E: E.tensor_mul(R["tmp"], R["aim"], R["aim"]), inc="v")
                r.op(V, lambda E: E.tensor_add(R["asq"], R["asq"], R["tmp"]), inc="v")
                r.op(V, lambda E: E.reciprocal(R["asq"], R["asq"]), inc="v")
                r.wait(V, "v")
                r.op(V, lambda E: E.tensor_mul(R["en"], R["are"], R["asq"]), inc="v")
                r.op(V, lambda E: E.tensor_mul(R["ln1"], R["aim"], R["asq"]), inc="v")
                r.op(V, lambda E: E.tensor_scalar_mul(R["ln1"], R["ln1"], -1.0), inc="v")
                r.op(V, lambda E: E.tensor_mul(R["den"], R["lamre"], R["lamre"]), inc="v")
                r.op(V, lambda E: E.tensor_mul(R["tmp"], lam_im_row[:], lam_im_row[:]), inc="v")
                r.op(V, lambda E: E.tensor_add(R["den"], R["den"], R["tmp"]), inc="v")
                r.op(V, lambda E: E.reciprocal(R["den"], R["den"]), inc="v")
                r.wait(V, "v")
                r.op(V, lambda E: E.tensor_scalar_add(R["mx"], R["are"], -1.0), inc="v")
                r.op(V, lambda E: E.tensor_mul(R["sp"], R["mx"], R["lamre"]), inc="v")
                r.op(V, lambda E: E.tensor_mul(R["tmp"], R["aim"], lam_im_row[:]), inc="v")
                r.op(V, lambda E: E.tensor_add(R["sp"], R["sp"], R["tmp"]), inc="v")
                r.op(V, lambda E: E.tensor_mul(R["sp"], R["sp"], R["den"]), inc="v")
                r.op(V, lambda E: E.tensor_mul(R["ab"], R["aim"], R["lamre"]), inc="v")
                r.op(V, lambda E: E.tensor_mul(R["tmp"], R["mx"], lam_im_row[:]), inc="v")
                r.op(V, lambda E: E.tensor_sub(R["ab"], R["ab"], R["tmp"]), inc="v")
                r.op(V, lambda E: E.tensor_mul(R["ab"], R["ab"], R["den"]), inc="v")
                ones11 = ones_row[:, 0:1]
                r.wait("tensor", "v")
                for j, nm in enumerate(["sp", "ab", "are", "aim", "en", "ln1"]):
                    r.op("tensor", lambda E, j=j, nm=nm: E.matmul(
                        ps_6[:, j:j + 1], R[nm], ones11, start=True, stop=True),
                        inc=("t" if j == 5 else None))
                r.wait(V, "t")
                r.op(V, lambda E: E.tensor_copy(scal6[:], ps_6), inc="v")
                s_re_c = scal6[:, 0:1]
                s_im_c = scal6[:, 1:2]
                a_re_c = scal6[:, 2:3]
                a_im_c = scal6[:, 3:4]
                ainv_re_c = scal6[:, 4:5]
                ainv_im_c = scal6[:, 5:6]

                def build_table(T, base_re_c, base_im_c):
                    Tre = T[:, 0, :]
                    Tim = T[:, 1, :]
                    r.op(V, lambda E: E.memset(T[:, :, 0:1], 0.0), inc="v")
                    r.op(V, lambda E: E.memset(Tre[:, 0:1], 1.0), inc="v")
                    r.op(V, lambda E: E.tensor_copy(Tre[:, 1:2], base_re_c), inc="v")
                    r.op(V, lambda E: E.tensor_copy(Tim[:, 1:2], base_im_c), inc="v")
                    k = 2
                    while k <= 64:
                        r.op(V, lambda E, k=k: E.tensor_mul(tcol[:, 0:1], Tre[:, k - 1:k], base_re_c))
                        r.op(V, lambda E, k=k: E.tensor_mul(wcol[:, 0:1], Tim[:, k - 1:k], base_im_c))
                        r.op(V, lambda E: E.tensor_sub(tcol[:, 0:1], tcol[:, 0:1], wcol[:, 0:1]))
                        r.op(V, lambda E, k=k: E.tensor_mul(tcol[:, 1:2], Tre[:, k - 1:k], base_im_c))
                        r.op(V, lambda E, k=k: E.tensor_mul(wcol[:, 1:2], Tim[:, k - 1:k], base_re_c))
                        r.op(V, lambda E: E.tensor_add(tcol[:, 1:2], tcol[:, 1:2], wcol[:, 1:2]))
                        r.op(V, lambda E, k=k: E.tensor_scalar_mul(Tre[:, k:2 * k], Tre[:, 0:k], tcol[:, 0:1]))
                        r.op(V, lambda E, k=k: E.tensor_scalar_mul(sca[:, 0:k], Tim[:, 0:k], tcol[:, 1:2]))
                        r.op(V, lambda E, k=k: E.tensor_sub(Tre[:, k:2 * k], Tre[:, k:2 * k], sca[:, 0:k]))
                        r.op(V, lambda E, k=k: E.tensor_scalar_mul(Tim[:, k:2 * k], Tim[:, 0:k], tcol[:, 0:1]))
                        r.op(V, lambda E, k=k: E.tensor_scalar_mul(sca[:, 0:k], Tre[:, 0:k], tcol[:, 1:2]))
                        r.op(V, lambda E, k=k: E.tensor_add(Tim[:, k:2 * k], Tim[:, k:2 * k], sca[:, 0:k]),
                             inc="v")
                        k *= 2

                build_table(tp, a_re_c, a_im_c)
                build_table(tm, ainv_re_c, ainv_im_c)
                r.op(V, lambda E: E.tensor_mul(a128c[:, 0:1], tp[:, 0, 127:128], a_re_c))
                r.op(V, lambda E: E.tensor_mul(wcol[:, 0:1], tp[:, 1, 127:128], a_im_c))
                r.op(V, lambda E: E.tensor_sub(a128c[:, 0:1], a128c[:, 0:1], wcol[:, 0:1]))
                r.op(V, lambda E: E.tensor_mul(a128c[:, 1:2], tp[:, 0, 127:128], a_im_c))
                r.op(V, lambda E: E.tensor_mul(wcol[:, 1:2], tp[:, 1, 127:128], a_re_c))
                r.op(V, lambda E: E.tensor_add(a128c[:, 1:2], a128c[:, 1:2], wcol[:, 1:2]), inc="v")

                r.op(V, lambda E: E.tensor_scalar_mul(bbre, bre_t, s_re_c))
                r.op(V, lambda E: E.tensor_scalar_mul(scr2[:], bim_t, s_im_c))
                r.op(V, lambda E: E.tensor_sub(bbre, bbre, scr2[:]))
                r.op(V, lambda E: E.tensor_scalar_mul(bbim, bre_t, s_im_c))
                r.op(V, lambda E: E.tensor_scalar_mul(scr2[:], bim_t, s_re_c))
                r.op(V, lambda E: E.tensor_add(bbim, bbim, scr2[:]), inc="v")
                id32 = ident[0:32, 0:32]
                for c in range(NCH):
                    for srcb, dstb in ((bbre, bbar_Tre), (bbim, bbar_Tim)):
                        r.wait("tensor", "v")
                        r.op("tensor", lambda E, c=c, srcb=srcb: E.transpose(
                            ps_tr, srcb[:, c * 128:(c + 1) * 128], id32), inc="t")
                        r.wait(V, "t")
                        r.op(V, lambda E, c=c, dstb=dstb: E.tensor_copy(
                            dstb[:, c, :], ps_tr[:, 0:32]), inc="v")
                for c in range(NCH):
                    r.op(Sc, lambda E, c=c: E.activation(cpack_re[:, c, :], cret_t[:, c * 128:(c + 1) * 128],
                                                         AF.Copy, scale=2.0), inc="s")
                    r.op(Sc, lambda E, c=c: E.activation(cpack_im[:, c, :], cimt_t[:, c * 128:(c + 1) * 128],
                                                         AF.Copy, scale=-2.0), inc="s")

                NG = S // 512
                t_group_end = {}
                for g in range(NG):
                    if g >= 1:
                        r.wait("gpsimd", "t", t_group_end[g - 1])
                    for c in range(NCH):
                        r.op("gpsimd", lambda E, c=c, g=g: E.dma_start(
                            out=u_bf[:, c, :],
                            in_=u3[c, :, g * 512:(g + 1) * 512]), inc="qg", k=16)
                    r.wait("tensor", "qg")
                    r.wait("tensor", "v")
                    for c in range(NCH):
                        r.op("tensor", lambda E, c=c: E.matmul(
                            ps_bu_re, bbar_Tre[:, c, :], u_bf[:, c, :],
                            start=(c == 0), stop=(c == NCH - 1)))
                        r.op("tensor", lambda E, c=c: E.matmul(
                            ps_bu_im, bbar_Tim[:, c, :], u_bf[:, c, :],
                            start=(c == 0), stop=(c == NCH - 1)), inc="t")
                    t_group_end[g] = r.cnt["t"]
                    r.wait(V, "t")
                    r.op(V, lambda E, g=g: E.tensor_copy(bu[:, 0, g * 512:(g + 1) * 512], ps_bu_re))
                    r.op(V, lambda E, g=g: E.tensor_copy(bu[:, 1, g * 512:(g + 1) * 512], ps_bu_im), inc="v")

                d03 = d0s.rearrange("n (c t) -> n c t", t=128)
                r.wait("gpsimd", "s")
                r.wait("gpsimd", "v")
                r.op("gpsimd", lambda E: E.iota(d03, pattern=[[0, S // 128], [1, 128]],
                                                channel_multiplier=0,
                                                allow_small_or_imprecise_dtypes=True), inc="qg", k=1)
                r.wait(V, "qg")
                r.wait(V, "s")
                r.op(V, lambda E: E.tensor_single_scalar(d0s, d0s, 0.5, ALU.is_ge), inc="v")
                NCH_S = S // 128
                HCH = NCH_S // 2

                def cplx_combine(dst_sel, in_re, in_im, tab, ta, tb, alu):
                    for half in range(2):
                        lo = half * (S // 2)
                        in_a = in_re[:, lo:lo + S // 2].rearrange("n (c t) -> n c t", t=128)
                        in_b = in_im[:, lo:lo + S // 2].rearrange("n (c t) -> n c t", t=128)
                        tb_a = tab[:, ta, :].unsqueeze(1).broadcast_to([32, HCH, 128])
                        tb_b = tab[:, tb, :].unsqueeze(1).broadcast_to([32, HCH, 128])
                        sca3 = sca.rearrange("n (c t) -> n c t", t=128)
                        scb3 = scb.rearrange("n (c t) -> n c t", t=128)
                        r.op(V, lambda E, in_a=in_a, tb_a=tb_a, sca3=sca3: E.tensor_tensor(
                            sca3, in_a, tb_a, ALU.mult))
                        r.op(V, lambda E, in_b=in_b, tb_b=tb_b, scb3=scb3: E.tensor_tensor(
                            scb3, in_b, tb_b, ALU.mult))
                        dst = dst_sel(lo).rearrange("n (c t) -> n c t", t=128)
                        r.op(V, lambda E, dst=dst, sca3=sca3, scb3=scb3, alu=alu: E.tensor_tensor(
                            dst, sca3, scb3, alu), inc="v")

                cplx_combine(lambda lo: cum[:, 0, lo:lo + S // 2], bu[:, 0, :], bu[:, 1, :],
                             tm, 1, 0, ALU.add)
                cplx_combine(lambda lo: bu[:, 0, lo:lo + S // 2], bu[:, 0, :], bu[:, 1, :],
                             tm, 0, 1, ALU.subtract)
                r.op(V, lambda E: E.tensor_copy(bu[:, 1, :], cum[:, 0, :]), inc="v")

                r.op(V, lambda E: E.tensor_tensor_scan(cum[:, 0, :], d0s, bu[:, 0, :], 0.0,
                                                       ALU.mult, ALU.add), inc="v")
                r.op(V, lambda E: E.tensor_tensor_scan(cum[:, 1, :], d0s, bu[:, 1, :], 0.0,
                                                       ALU.mult, ALU.add), inc="v")
                cum3r = cum[:, 0, :].rearrange("n (c t) -> n c t", t=128)
                cum3i = cum[:, 1, :].rearrange("n (c t) -> n c t", t=128)
                r.op(V, lambda E: E.tensor_copy(ee[:, 0, :], cum3r[:, :, 127]))
                r.op(V, lambda E: E.tensor_copy(ee[:, 1, :], cum3i[:, :, 127]), inc="v")
                r.op(V, lambda E: E.memset(ccol[:, 0:2], 0.0), inc="v")
                for p in range(1, NCH_S):
                    ccr = ccol[:, 0:1]
                    cci = ccol[:, 1:2]
                    ctr = ccol[:, 2:3]
                    cti = ccol[:, 3:4]
                    r.op(V, lambda E, p=p: E.tensor_add(ccol[:, 2:4], ccol[:, 0:2], ee[:, :, p - 1]))
                    r.op(V, lambda E: E.tensor_mul(ccr, ctr, a128c[:, 0:1]))
                    r.op(V, lambda E: E.tensor_mul(wcol[:, 0:1], cti, a128c[:, 1:2]))
                    r.op(V, lambda E: E.tensor_sub(ccr, ccr, wcol[:, 0:1]))
                    r.op(V, lambda E: E.tensor_mul(cci, ctr, a128c[:, 1:2]))
                    r.op(V, lambda E: E.tensor_mul(wcol[:, 1:2], cti, a128c[:, 0:1]))
                    r.op(V, lambda E: E.tensor_add(cci, cci, wcol[:, 1:2]))
                    r.op(V, lambda E, p=p: E.tensor_scalar_add(
                        cum[:, 0, p * 128:(p + 1) * 128], cum[:, 0, p * 128:(p + 1) * 128], ccr))
                    r.op(V, lambda E, p=p: E.tensor_scalar_add(
                        cum[:, 1, p * 128:(p + 1) * 128], cum[:, 1, p * 128:(p + 1) * 128], cci),
                        inc="v")
                cplx_combine(lambda lo: xs_bf[:, 0, lo:lo + S // 2], cum[:, 0, :], cum[:, 1, :],
                             tp, 0, 1, ALU.subtract)
                cplx_combine(lambda lo: xs_bf[:, 1, lo:lo + S // 2], cum[:, 0, :], cum[:, 1, :],
                             tp, 1, 0, ALU.add)

                x13 = x1.rearrange("(c p) t -> c p t", p=128)
                OWN0 = TOK
                r.wait("tensor", "s")
                for qg in range(4):
                    for c in range(NCH):
                        if qg * NCH + c >= 2:
                            r.wait("sync", "v")
                        r.op("sync", lambda E, qg=qg, c=c: E.dma_start(
                            out=u32b[:, (qg * NCH + c) % 2, :],
                            in_=u3[c, :, OWN0 + qg * 512:OWN0 + (qg + 1) * 512]), inc="qs", k=16)
                        r.wait("vector", "qs")
                        r.wait("tensor", "v")
                        if qg > 0 and c == 0:
                            r.wait("vector", "so")
                        r.op("tensor", lambda E, qg=qg, c=c: E.matmul(
                            ps_y[c % 2], cpack_re[:, c, :],
                            xs_bf[:, 0, OWN0 + qg * 512:OWN0 + (qg + 1) * 512],
                            start=True, stop=False))
                        r.op("tensor", lambda E, qg=qg, c=c: E.matmul(
                            ps_y[c % 2], cpack_im[:, c, :],
                            xs_bf[:, 1, OWN0 + qg * 512:OWN0 + (qg + 1) * 512],
                            start=False, stop=True), inc="t")
                        r.wait(V, "t")
                        r.op(V, lambda E, qg=qg, c=c: E.scalar_tensor_tensor(
                            pre[:, c, :], u32b[:, (qg * NCH + c) % 2, :], d1col[:, c:c + 1],
                            ps_y[c % 2], ALU.mult, ALU.add), inc="v")
                    r.wait("scalar", "v")
                    _ln_emit(r, pre, NCH, sqscr, ps_sum, ps_sumsq, ones_col, ones_row,
                             rowscr, abc, bbc, ps_bc, 512)
                    r.wait("sync", "v")
                    for c in range(NCH):
                        r.op("sync", lambda E, qg=qg, c=c: E.dma_start(
                            out=x13[c, :, qg * 512:(qg + 1) * 512], in_=pre[:, c, :]), inc="so", k=16)
                r.wait("sync", "so")

        def _phase_B():
            with ExitStack() as ctx:
                sb = lambda n, s, d=F32: ctx.enter_context(nc.sbuf_tensor("B_" + n, s, d))
                xbf = sb("xbf", [128, NCH, TOK], BF16)
                w3sl4 = xbf[:].rearrange("p c t -> p (c t)").rearrange(
                    "p (e fc dq) -> p e fc dq", e=8, fc=8)
                x32 = sb("x32", [128, 4, 512])
                rwt_t = sb("rwt_t", [128, NCH, 8])
                pack8 = sb("pack8", [64, TOK])
                lg_fm = pack8[0:8, :]
                wrow_fm = pack8[32:40, :]
                lg_tm = sb("lg_tm", [128, NCH, 8])
                mx_t = sb("mx_t", [128, 8])
                idx_t = sb("idx_t", [128, 8], U32)
                idxf = sb("idxf", [128, 2])
                wcol2 = sb("wcol2", [128, 4])
                iota8 = sb("iota8", [128, 8])
                wmap = sb("wmap", [128, 8])
                wrow_e = sb("wrow_e", [1, TOK])
                wbc = sb("wbc", [128, TOK])
                w12s = sb("w12s", [128, 2, NCH, 512], BF16)
                pre = w12s[:].rearrange("p a c t -> p (a c t)").bitcast(F32).rearrange(
                    "p (c t) -> p c t", c=NCH)
                gsil = sb("gsil", [128, 512])
                vw = sb("vw", [128, 512])
                hprime = sb("hprime", [128, NFB, TOK], BF16)
                hb3 = hprime[:].rearrange("p a t -> p (a t)").rearrange(
                    "p (b fc t) -> p b fc t", b=4, fc=8)
                sqscr = sb("sqscr", [128, 2, 512])
                rowscr = sb("rowscr", [1, 2048])
                abc = sb("abc", [128, 512])
                bbc = sb("bbc", [128, 512])
                idv = sb("idv", [128, 128])
                ident = sb("ident", [128, 128])
                ones_col = sb("ones_col", [128, 1])
                ones_row = sb("ones_row", [1, 128])

                bank = lambda i: PS[:, 512 * i:512 * (i + 1)]
                ps_lg = PS[0:8, 0:512]
                ps_tr = PS[:, 512:1024]

                V, Sc = "vector", "scalar"

                r.op("sync", lambda E: E.dma_start(out=rwt_t[:], in_=rwt3.rearrange("c p e -> p c e")),
                     inc="qs", k=16)
                r.op("gpsimd", lambda E: E.iota(idv[:], pattern=[[-1, 128]], channel_multiplier=1,
                                                allow_small_or_imprecise_dtypes=True), inc="qg", k=1)
                r.op("gpsimd", lambda E: E.iota(iota8[:], pattern=[[1, 8]], channel_multiplier=0,
                                                allow_small_or_imprecise_dtypes=True), inc="qg", k=1)
                r.wait(V, "qg")
                r.op(V, lambda E: E.tensor_single_scalar(ident[:], idv[:], 0.0, ALU.is_equal))
                r.op(V, lambda E: E.memset(ones_col[:], 1.0))
                r.op(V, lambda E: E.memset(ones_row[:], 1.0))

                r.wait("tensor", "qs")
                ld_done = {}
                for tg in range(NT):
                    for c in range(NCH):
                        i = tg * NCH + c
                        if i >= 2:
                            r.wait("sync", "t", ld_done[i - 2][0])
                            r.wait("sync", "v", ld_done[i - 2][1])
                        r.op("sync", lambda E, tg=tg, c=c, i=i: E.dma_start(
                            out=x32[:, i % 4, :], in_=x13[c, :, tg * 512:(tg + 1) * 512]), inc="qs", k=16)
                        r.wait("tensor", "qs")
                        r.op("tensor", lambda E, tg=tg, c=c, i=i: E.matmul(
                            ps_lg, rwt_t[:, c, :], x32[:, i % 4, :],
                            start=(c == 0), stop=(c == NCH - 1)), inc="t")
                        r.wait(V, "qs")
                        r.op(V, lambda E, tg=tg, c=c, i=i: E.tensor_copy(
                            xbf[:, c, tg * 512:(tg + 1) * 512], x32[:, i % 4, :]))
                        ld_done[i] = (r.cnt["t"], r.cnt["v"])
                    r.wait(V, "t")
                    r.op(V, lambda E, tg=tg: E.tensor_copy(lg_fm[:, tg * 512:(tg + 1) * 512], ps_lg))
                    r.wait("tensor", "v")

                ident8 = ident[0:8, 0:8]
                for i in range(NCH):
                    r.wait("tensor", "v")
                    r.op("tensor", lambda E, i=i: E.transpose(
                        ps_tr[0:128, 0:8], lg_fm[:, i * 128:(i + 1) * 128], ident8), inc="t")
                    r.wait(V, "t")
                    r.op(V, lambda E, i=i: E.tensor_copy(lg_tm[:, i, :], ps_tr[0:128, 0:8]))
                    r.op(V, lambda E, i=i: E.max(mx_t[:], lg_tm[:, i, :]))
                    r.op(V, lambda E, i=i: E.max_index(idx_t[:], mx_t[:], lg_tm[:, i, :]))
                    r.op(V, lambda E: E.tensor_copy(idxf[:], idx_t[:, 0:2]))
                    r.op(V, lambda E: E.tensor_sub(wcol2[:, 0:1], mx_t[:, 1:2], mx_t[:, 0:1]))
                    r.wait(Sc, "v")
                    r.op(Sc, lambda E: E.activation(wcol2[:, 1:2], wcol2[:, 0:1], AF.Exp))
                    r.wait(V, "s")
                    r.op(V, lambda E: E.tensor_scalar_add(wcol2[:, 2:3], wcol2[:, 1:2], 1.0))
                    r.op(V, lambda E: E.reciprocal(wcol2[:, 2:3], wcol2[:, 2:3]))
                    r.wait(V, "v")
                    r.op(V, lambda E: E.tensor_mul(wcol2[:, 3:4], wcol2[:, 1:2], wcol2[:, 2:3]))
                    r.op(V, lambda E: E.tensor_scalar(wmap[:], iota8[:], idxf[:, 0:1], wcol2[:, 2:3],
                                                      ALU.is_equal, ALU.mult))
                    r.op(V, lambda E: E.tensor_scalar(gsil[:, 0:8], iota8[:], idxf[:, 1:2], wcol2[:, 3:4],
                                                      ALU.is_equal, ALU.mult))
                    r.op(V, lambda E: E.tensor_add(wmap[:], wmap[:], gsil[:, 0:8]))
                    r.wait("tensor", "v")
                    r.op("tensor", lambda E, i=i: E.transpose(
                        ps_tr[0:8, 0:128], wmap[:], ident[:]), inc="t")
                    r.wait(V, "t")
                    r.op(V, lambda E, i=i: E.tensor_copy(wrow_fm[:, i * 128:(i + 1) * 128],
                                                         ps_tr[0:8, 0:128]))

                for e in range(8):
                    r.wait("gpsimd", "v")
                    r.op("gpsimd", lambda E, e=e: E.dma_start(out=wrow_e[:], in_=pack8[32 + e:33 + e, :]),
                         inc="qg", k=16)
                    r.wait("tensor", "qg")
                    for tg in range(NT):
                        r.op("tensor", lambda E, tg=tg: E.matmul(
                            bank(2 + tg % 2), ones_row[:], wrow_e[:, tg * 512:(tg + 1) * 512],
                            start=True, stop=True), inc="t")
                        r.wait(V, "t")
                        r.op(V, lambda E, tg=tg: E.tensor_copy(wbc[:, tg * 512:(tg + 1) * 512],
                                                               bank(2 + tg % 2)))
                        r.wait("tensor", "v")
                    for fh in range(2):
                        r.wait("gpsimd", "t")
                        for c in range(NCH):
                            r.op("gpsimd", lambda E, e=e, fh=fh, c=c: E.dma_start(
                                out=w12s[:, 0, c, :], in_=w1[e, :, :].rearrange(
                                    "(c p) f -> c p f", p=128)[c, :, fh * 512:(fh + 1) * 512]),
                                inc="qg", k=16)
                            r.op("gpsimd", lambda E, e=e, fh=fh, c=c: E.dma_start(
                                out=w12s[:, 1, c, :], in_=w2[e, :, :].rearrange(
                                    "(c p) f -> c p f", p=128)[c, :, fh * 512:(fh + 1) * 512]),
                                inc="qg", k=16)
                        r.wait("tensor", "qg")
                        for fb in range(4):
                            fblk = fh * 4 + fb
                            for c in range(NCH):
                                for tg in range(NT):
                                    r.op("tensor", lambda E, c=c, tg=tg, fb=fb: E.matmul(
                                        bank(tg), w12s[:, 0, c, fb * 128:(fb + 1) * 128],
                                        xbf[:, c, tg * 512:(tg + 1) * 512],
                                        start=(c == 0), stop=(c == NCH - 1)))
                                for tg in range(NT):
                                    r.op("tensor", lambda E, c=c, tg=tg, fb=fb: E.matmul(
                                        bank(4 + tg), w12s[:, 1, c, fb * 128:(fb + 1) * 128],
                                        xbf[:, c, tg * 512:(tg + 1) * 512],
                                        start=(c == 0), stop=(c == NCH - 1)),
                                        inc=("t" if c == NCH - 1 and tg == NT - 1 else None))
                            for tg in range(NT):
                                r.wait(Sc, "t")
                                r.wait(Sc, "v")
                                r.op(Sc, lambda E, tg=tg: E.activation(gsil[:], bank(tg), AF.Silu))
                                r.wait(V, "s")
                                r.op(V, lambda E, tg=tg: E.tensor_tensor(
                                    vw[:], bank(4 + tg), wbc[:, tg * 512:(tg + 1) * 512], ALU.mult))
                                r.op(V, lambda E, tg=tg, fblk=fblk: E.tensor_tensor(
                                    hprime[:, fblk, tg * 512:(tg + 1) * 512], gsil[:], vw[:], ALU.mult))
                            r.wait("tensor", "v")
                    r.wait("sync", "v")
                    for fblk in range(NFB):
                        r.op("sync", lambda E, e=e, fblk=fblk: E.dma_start(
                            out=hsp[e, fblk * 128:(fblk + 1) * 128, :], in_=hprime[:, fblk, :]),
                            inc="so", k=16)
                    r.wait("gpsimd", "so")
                    r.wait("vector", "so")

                t_b3 = {}
                r.wait("gpsimd", "so")
                for dq in range(4):
                    r.wait("gpsimd", "t")
                    for e in range(8):
                        for fc in range(8):
                            r.op("gpsimd", lambda E, dq=dq, e=e, fc=fc: E.dma_start(
                                out=w3sl4[:, e, fc, :],
                                in_=w3[e, :, :].rearrange("(fc p) d -> fc p d", p=128)[
                                    fc, :, dq * 512:(dq + 1) * 512]), inc="qg", k=16)
                    r.wait("tensor", "qg")
                    for tg in range(NT):
                        for e in range(8):
                            i = tg * 8 + e
                            if i >= 2:
                                r.wait("gpsimd", "t", t_b3[(dq, i - 2)])
                            for fc in range(8):
                                r.op("gpsimd", lambda E, e=e, fc=fc, tg=tg, i=i: E.dma_start(
                                    out=hb3[:, i % 4, fc, :],
                                    in_=hsp[e, fc * 128:(fc + 1) * 128, tg * 512:(tg + 1) * 512]),
                                    inc="qg", k=16)
                            r.wait("tensor", "qg")
                            if e == 0:
                                r.wait("tensor", "v")
                            for dblk in range(4):
                                for fc in range(8):
                                    r.op("tensor", lambda E, e=e, fc=fc, dblk=dblk, i=i: E.matmul(
                                        bank(dblk), w3sl4[:, e, fc, dblk * 128:(dblk + 1) * 128],
                                        hb3[:, i % 4, fc, :],
                                        start=(e == 0 and fc == 0), stop=(e == 7 and fc == 7)),
                                        inc=("t" if fc == 7 and dblk == 3 else None))
                            t_b3[(dq, i)] = r.cnt["t"]
                        for dblk in range(4):
                            c = dq * 4 + dblk
                            r.op("sync", lambda E, c=c, tg=tg, dblk=dblk: E.dma_start(
                                out=x32[:, dblk % 4, :], in_=x13[c, :, tg * 512:(tg + 1) * 512]),
                                inc="qs", k=16)
                            r.wait(V, "qs")
                            r.wait(V, "t")
                            r.op(V, lambda E, c=c, tg=tg, dblk=dblk: E.scalar_tensor_tensor(
                                gsil[:], x32[:, dblk % 4, :], 1.0, bank(dblk), ALU.mult, ALU.add))
                            r.wait("sync", "v")
                            r.op("sync", lambda E, c=c, tg=tg: E.dma_start(
                                out=x2p3[c, :, tg * 512:(tg + 1) * 512], in_=gsil[:]), inc="so", k=16)
                            r.wait(V, "so")

                for tg in range(NT):
                    r.wait("sync", "so")
                    r.wait("sync", "v")
                    for c in range(NCH):
                        r.op("sync", lambda E, c=c, tg=tg: E.dma_start(
                            out=pre[:, c, :], in_=x2p3[c, :, tg * 512:(tg + 1) * 512]), inc="qs", k=16)
                    r.wait("vector", "qs")
                    r.wait("scalar", "qs")
                    r.wait("tensor", "qs")
                    _ln_emit(r, pre, NCH, sqscr, PS[0:1, 2048:2560], PS[0:1, 2560:3072],
                             ones_col, ones_row, rowscr, abc, bbc,
                             [PS[:, 3072:3584], PS[:, 3584:4096]], 512)
                    r.wait("sync", "v")
                    for c in range(NCH):
                        r.op("sync", lambda E, c=c, tg=tg: E.dma_start(
                            out=x23[c, :, tg * 512:(tg + 1) * 512], in_=pre[:, c, :]), inc="so", k=16)
                r.wait("sync", "so")

        _phase_A()
        barrier()
        _phase_B()
        r.wait("sync", "so")
        r.play(block)
    return nc


def build_fused_CD():
    nc = bass.Bass("TRN2", target_bir_lowering=False, debug=False)
    dbg = False
    xkv = nc.dram_tensor("xkv", [D, S], F32, kind="ExternalInput").ap()
    pos = nc.dram_tensor("pos", [1, S], F32, kind="ExternalInput").ap()
    maskrow_d = nc.dram_tensor("maskrow", [1, 128], F32, kind="ExternalInput").ap()
    wqdT = nc.dram_tensor("wqdT", [D, 1024], F32, kind="ExternalInput").ap()
    qnw = nc.dram_tensor("qnw", [128, 8], F32, kind="ExternalInput").ap()
    wquT = nc.dram_tensor("wquT", [1024, 2048], F32, kind="ExternalInput").ap()
    wqrT = nc.dram_tensor("wqrT", [1024, 1024], F32, kind="ExternalInput").ap()
    wkvdT = nc.dram_tensor("wkvdT", [D, 512], F32, kind="ExternalInput").ap()
    kvnw = nc.dram_tensor("kvnw", [128, 4], F32, kind="ExternalInput").ap()
    wkvuT = nc.dram_tensor("wkvuT", [512, 4096], F32, kind="ExternalInput").ap()
    wkrT = nc.dram_tensor("wkrT", [D, 64], F32, kind="ExternalInput").ap()
    woT = nc.dram_tensor("woT", [D, D], F32, kind="ExternalInput").ap()
    x3 = nc.dram_tensor("x3", [D, TOK], F32, kind="Internal").ap()
    aspill = nc.dram_tensor("aspill", [D, TOK], BF16, kind="Internal").ap()
    w_o = nc.dram_tensor("w_o", [D, D], F32, kind="ExternalInput").ap()
    wT = nc.dram_tensor("wT", [D, D], F32, kind="ExternalInput").ap()
    bcol = nc.dram_tensor("bcol", [128, NCH], F32, kind="ExternalInput").ap()
    x4 = nc.dram_tensor("x4", [D, TOK], F32, kind="ExternalOutput").ap()

    NH = 16
    NKT = S // 128
    SCALE = 1.0 / float(np.sqrt(192.0))
    inv2pi = 1.0 / (2.0 * np.pi)
    twopi = 2.0 * np.pi
    xkv3 = xkv.rearrange("(c p) t -> c p t", p=128)
    x33 = x3.rearrange("(c p) t -> c p t", p=128)
    asp3 = aspill.rearrange("(c p) t -> c p t", p=128)
    x43 = x4.rearrange("(c p) t -> c p t", p=128)
    wo3 = w_o.rearrange("(c p) i -> c p i", p=128)
    wT3 = wT.rearrange("(c p) o -> c p o", p=128)

    with ExitStack() as topctx:
        r = Rec(nc, topctx)
        for s_ in ("qs", "qg", "v", "s", "t", "so"):
            r.sem(s_)
        PS = topctx.enter_context(nc.psum_tensor("PS", [128, 4096], F32))
        block = topctx.enter_context(nc.Block())

        def barrier():
            for eng in ENGINES:
                for snm in ("qs", "qg", "v", "s", "t", "so"):
                    r.wait(eng, snm)

        def _phase_C():
            with ExitStack() as ctx:
                sb = lambda n, s, d=F32: ctx.enter_context(nc.sbuf_tensor("C_" + n, s, d))
                x32k = sb("x32k", [128, 2, 512])
                xbf_s = sb("xbf_s", [128, 2, 512], BF16)
                qc_bf = sb("qc_bf", [128, 8, TOK], BF16)
                pre = qc_bf[:].rearrange("p a t -> p (a t)").bitcast(F32).rearrange(
                    "p (c t) -> p c t", c=NCH)
                kvc_bf = sb("kvc_bf", [128, 4, S], BF16)
                wo_s = kvc_bf[:].rearrange("p a t -> p (a t)").rearrange(
                    "p (c t) -> p c t", c=NCH)
                kro_raw = sb("kro_raw", [64, S], BF16)
                krope2 = sb("krope2", [64, S], BF16)
                cosk2 = sb("cosk2", [64, S], BF16)
                sink2 = sb("sink2", [64, S], BF16)
                wkr_bf = sb("wkr_bf", [128, NCH, 64], BF16)
                wkvd_s = sb("wkvd_s", [128, 2, 512], BF16)
                wqd_s = sb("wqd_s", [128, 2, 1024], BF16)
                qnw_t = sb("qnw_t", [128, 8])
                kvnw_t = sb("kvnw_t", [128, 4])
                prodscr = sb("prodscr", [128, 8, 512])
                prodflat = prodscr[:].rearrange("p a t -> p (a t)")
                ftab = prodflat[0:32, 0:4096]
                rowscr_f = prodflat[0:1, 0:2048]
                sqscr = sb("sqscr", [128, 2, 512])
                trig_s1 = sb("trig_s1", [32, 1024])
                trig_s2 = sqscr[:].rearrange("p a t -> p (a t)")[0:32, 0:1024]
                trig_i = sb("trig_i", [32, 1024], I32)
                rows_rms = sb("rows_rms", [1, 1024])
                rbref = sb("rbref", [128, 512])
                wup_h = sb("wup_h", [128, 2, 8, 128], BF16)
                wqr_h = sb("wqr_h", [128, 2, 8, 64], BF16)
                wkvu_h = sb("wkvu_h", [128, 2, 4, 256], BF16)
                knope = sb("knope", [128, S], BF16)
                vext = sb("vext", [128, NKT, 130], BF16)
                qnope = sb("qnope", [128, TOK], BF16)
                qpe = sb("qpe", [64, TOK], BF16)
                qpe_f = prodflat[0:64, 2048:3072].rearrange("p (a t) -> p a t", a=2)
                ptile = sb("ptile", [128, 2048], BF16)
                att_t = sb("att_t", [128, 1, 512], BF16)
                att_qg = sb("att_qg", [128, NCH, 512], BF16)
                m01 = sb("m01", [128, 4, 512], BF16)
                ivt = None
                biasr = rows_rms[0:1, 0:128]
                bias_h = sb("bias_h", [128, 128])
                maskrow = sb("maskrow_t", [1, 128])
                pos_sl = rows_rms[0:1, 512:1024]
                invf = sb("invf", [1, 32])
                r64 = sb("r64", [64, 64])
                r64b = sb("r64b", [64, 64], BF16)
                d64 = sb("d64", [64, 64])
                pcol = sb("pcol", [64, 4])
                mxq = sb("mxq", [128, 4])
                msc = sb("msc", [1, 8])
                ident = sb("identc", [128, 128])
                idv = ident
                ones_col = sb("ones_colc", [128, 1])
                ones_row = sb("ones_rowc", [1, 128])
                eps11 = sb("eps11", [1, 1])
                abc = prodflat[:, 3072:3584]
                bbc = prodflat[:, 3584:4096]

                bank = lambda i: PS[:, 512 * i:512 * (i + 1)]

                V, Sc = "vector", "scalar"

                r.op("sync", lambda E: E.dma_start(out=maskrow[:], in_=maskrow_d), inc="qs", k=16)
                r.op("sync", lambda E: E.dma_start(out=qnw_t[:], in_=qnw), inc="qs", k=16)
                r.op("sync", lambda E: E.dma_start(out=kvnw_t[:], in_=kvnw), inc="qs", k=16)
                r.op("gpsimd", lambda E: E.dma_start(
                    out=wkr_bf[:], in_=wkrT.rearrange("(c p) f -> p c f", p=128)), inc="qg", k=16)
                r.op("gpsimd", lambda E: E.iota(idv[:], pattern=[[-1, 128]], channel_multiplier=1,
                                                allow_small_or_imprecise_dtypes=True), inc="qg", k=1)

                xkvbf = nc.dram_tensor("xkvbf", [D, S], BF16, kind="Internal").ap()
                xkvbf3 = xkvbf.rearrange("(c p) t -> c p t", p=128)
                p0_st = {}
                for i0 in range(NCH * 8):
                    c0, tg0 = i0 // 8, i0 % 8
                    if i0 >= 2:
                        r.wait("gpsimd", "qs", p0_st[i0 - 2])
                    r.op("gpsimd", lambda E, c0=c0, tg0=tg0, i0=i0: E.dma_start(
                        out=xbf_s[:, i0 % 2, :], in_=xkv3[c0, :, tg0 * 512:(tg0 + 1) * 512]),
                        inc="qg", k=16)
                    r.wait("sync", "qg")
                    r.op("sync", lambda E, c0=c0, tg0=tg0, i0=i0: E.dma_start(
                        out=xkvbf3[c0, :, tg0 * 512:(tg0 + 1) * 512], in_=xbf_s[:, i0 % 2, :]),
                        inc="qs", k=16)
                    p0_st[i0] = r.cnt["qs"]
                p0_done = r.cnt["qs"]

                r.op("gpsimd", lambda E: E.iota(sqscr[:, 0, :], pattern=[[1, 512]], channel_multiplier=-1,
                                                allow_small_or_imprecise_dtypes=True), inc="qg", k=1)
                r.op("gpsimd", lambda E: E.iota(d64[:], pattern=[[-1, 64]], channel_multiplier=1,
                                                allow_small_or_imprecise_dtypes=True), inc="qg", k=1)
                r.op("gpsimd", lambda E: E.iota(pcol[:, 0:1], pattern=[[0, 1]], channel_multiplier=1,
                                                allow_small_or_imprecise_dtypes=True), inc="qg", k=1)
                r.op("gpsimd", lambda E: E.iota(invf[:], pattern=[[1, 32]], channel_multiplier=0,
                                                allow_small_or_imprecise_dtypes=True), inc="qg", k=1)
                r.wait(V, "qg")
                r.wait(V, "qs")
                r.op(V, lambda E: E.tensor_single_scalar(ident[:], idv[:], 0.0, ALU.is_equal))
                r.op(V, lambda E: E.memset(ones_col[:], 1.0))
                r.op(V, lambda E: E.memset(ones_row[:], 1.0))
                r.op(V, lambda E: E.memset(eps11[:], 1e-6))
                for dd in range(4):
                    r.op(V, lambda E, dd=dd: E.tensor_single_scalar(
                        m01[:, dd, :], sqscr[:, 0, :], dd * 128 - 0.5, ALU.is_ge))
                r.op(V, lambda E: E.tensor_single_scalar(pcol[:, 1:2], pcol[:, 0:1], 31.5, ALU.is_ge))
                r.op(V, lambda E: E.tensor_single_scalar(pcol[:, 2:3], pcol[:, 1:2], 0.5, ALU.is_lt))
                r.op(V, lambda E: E.tensor_single_scalar(r64[:], d64[:], 32.0, ALU.is_equal))
                r.op(V, lambda E: E.tensor_tensor(r64[:], r64[:],
                                                  pcol[:, 1:2].broadcast_to([64, 64]), ALU.mult))
                r.op(V, lambda E: E.tensor_single_scalar(d64[:], d64[:], -32.0, ALU.is_equal))
                r.op(V, lambda E: E.tensor_tensor(d64[:], d64[:],
                                                  pcol[:, 2:3].broadcast_to([64, 64]), ALU.mult))
                r.op(V, lambda E: E.tensor_sub(r64[:], d64[:], r64[:]))
                r.op(V, lambda E: E.tensor_copy(r64b[:], r64[:]))
                r.op(V, lambda E: E.tensor_scalar_mul(invf[:], invf[:], -float(np.log(10000.0)) / 32.0))
                r.wait(Sc, "v")
                r.op(Sc, lambda E: E.activation(invf[:], invf[:], AF.Exp))
                r.wait(V, "s")
                r.op(V, lambda E: E.tensor_scalar_mul(qnw_t[:], qnw_t[:], SCALE))

                dsc = nc.dram_tensor("dsc", [1, 512], F32, kind="Internal").ap()
                NTK = S // 512
                p1_ld = {}
                r.wait("sync", "qs", p0_done)
                for tgk in range(NTK):
                    for c in range(NCH):
                        i = tgk * NCH + c
                        if i >= 2:
                            r.wait("gpsimd", "t", p1_ld[i - 2])
                            r.wait("sync", "t", p1_ld[i - 2])
                        r.op("sync", lambda E, c=c, tgk=tgk, i=i: E.dma_start(
                            out=xbf_s[:, i % 2, :], in_=xkvbf3[c, :, tgk * 512:(tgk + 1) * 512]),
                            inc="qs", k=16)
                        r.op("gpsimd", lambda E, c=c, i=i: E.dma_start(
                            out=wkvd_s[:, i % 2, :], in_=wkvdT.rearrange(
                                "(c p) f -> c p f", p=128)[c, :, :]), inc="qg", k=16)
                        r.wait("tensor", "qg")
                        r.wait("tensor", "qs")
                        if c == 0:
                            r.wait("tensor", "v")
                        for rb in range(4):
                            r.op("tensor", lambda E, c=c, rb=rb, i=i: E.matmul(
                                bank(rb), wkvd_s[:, i % 2, rb * 128:(rb + 1) * 128],
                                xbf_s[:, i % 2, :], start=(c == 0), stop=(c == NCH - 1)))
                        r.op("tensor", lambda E, c=c, i=i: E.matmul(
                            bank(4)[0:64, :], wkr_bf[:, c, :], xbf_s[:, i % 2, :],
                            start=(c == 0), stop=(c == NCH - 1)), inc="t")
                        p1_ld[i] = r.cnt["t"]
                    r.wait(V, "t")
                    for rb in range(4):
                        r.op(V, lambda E, rb=rb: E.tensor_copy(prodscr[:, rb, :], bank(rb)))
                    r.op(V, lambda E, tgk=tgk: E.tensor_copy(
                        kro_raw[:, tgk * 512:(tgk + 1) * 512], bank(4)[0:64, :]), inc="v")
                    r.wait(Sc, "v")
                    for rb in range(4):
                        if rb >= 2:
                            r.wait(Sc, "t")
                        r.op(Sc, lambda E, rb=rb: E.activation(sqscr[:, rb % 2, :], prodscr[:, rb, :],
                                                               AF.Square), inc="s")
                        r.wait("tensor", "s")
                        r.op("tensor", lambda E, rb=rb: E.matmul(
                            bank(6)[0:1, :], ones_col[:], sqscr[:, rb % 2, :],
                            start=(rb == 0), stop=(rb == 3)), inc="t")
                    r.wait(Sc, "t")
                    r.op(Sc, lambda E: E.activation(rows_rms[:, 0:512], bank(6)[0:1, :], AF.Ln,
                                                    scale=1.0 / 512.0, bias=eps11[:]), inc="s")
                    r.op(Sc, lambda E: E.activation(rows_rms[:, 0:512], rows_rms[:, 0:512], AF.Exp,
                                                    scale=-0.5), inc="s")
                    r.wait("tensor", "s")
                    r.op("tensor", lambda E: E.matmul(bank(7), ones_row[:], rows_rms[:, 0:512],
                                                      start=True, stop=True), inc="t")
                    r.wait(V, "t")
                    r.op(V, lambda E: E.tensor_copy(rbref[:], bank(7)))
                    for rb in range(4):
                        r.op(V, lambda E, rb=rb: E.tensor_mul(prodscr[:, rb, :], prodscr[:, rb, :],
                                                              rbref[:]))
                        r.op(V, lambda E, rb=rb, tgk=tgk: E.tensor_scalar_mul(
                            kvc_bf[:, rb, tgk * 512:(tgk + 1) * 512], prodscr[:, rb, :],
                            kvnw_t[:, rb:rb + 1]), inc="v")

                NTQ = TOK // 512
                p2_ld = {}
                for tgq in range(NTQ):
                    for c in range(NCH):
                        i = tgq * NCH + c
                        if i >= 2:
                            r.wait("gpsimd", "t", p2_ld[i - 2])
                            r.wait("sync", "t", p2_ld[i - 2])
                        else:
                            r.wait("sync", "t", p1_ld[NTK * NCH - 2 + i])
                        r.op("sync", lambda E, c=c, tgq=tgq, i=i: E.dma_start(
                            out=xbf_s[:, i % 2, :], in_=xkvbf3[c, :, TOK + tgq * 512:TOK + (tgq + 1) * 512]),
                            inc="qs", k=16)
                        r.op("gpsimd", lambda E, c=c, i=i: E.dma_start(
                            out=wqd_s[:, i % 2, :], in_=wqdT.rearrange(
                                "(c p) f -> c p f", p=128)[c, :, :]), inc="qg", k=16)
                        r.wait("tensor", "qg")
                        r.wait("tensor", "qs")
                        if c == 0:
                            r.wait("tensor", "v")
                        for rb in range(8):
                            r.op("tensor", lambda E, c=c, rb=rb, i=i: E.matmul(
                                bank(rb), wqd_s[:, i % 2, rb * 128:(rb + 1) * 128],
                                xbf_s[:, i % 2, :], start=(c == 0), stop=(c == NCH - 1)),
                                inc=("t" if rb == 7 else None))
                        p2_ld[i] = r.cnt["t"]
                    r.wait(V, "t")
                    for rb in range(8):
                        r.op(V, lambda E, rb=rb: E.tensor_copy(prodscr[:, rb, :], bank(rb)), inc="v")
                    r.wait(Sc, "v")
                    for rb in range(8):
                        if rb >= 2:
                            r.wait(Sc, "t")
                        r.op(Sc, lambda E, rb=rb: E.activation(sqscr[:, rb % 2, :], prodscr[:, rb, :],
                                                               AF.Square), inc="s")
                        r.wait("tensor", "s")
                        r.wait("tensor", "v")
                        r.op("tensor", lambda E, rb=rb: E.matmul(
                            bank(0)[0:1, :], ones_col[:], sqscr[:, rb % 2, :],
                            start=(rb == 0), stop=(rb == 7)), inc="t")
                    r.wait(Sc, "t")
                    r.op(Sc, lambda E: E.activation(rows_rms[:, 0:512], bank(0)[0:1, :], AF.Ln,
                                                    scale=1.0 / 1024.0, bias=eps11[:]), inc="s")
                    r.op(Sc, lambda E: E.activation(rows_rms[:, 0:512], rows_rms[:, 0:512], AF.Exp,
                                                    scale=-0.5), inc="s")
                    r.wait("tensor", "s")
                    r.op("tensor", lambda E: E.matmul(bank(1), ones_row[:], rows_rms[:, 0:512],
                                                      start=True, stop=True), inc="t")
                    r.wait(V, "t")
                    r.op(V, lambda E: E.tensor_copy(rbref[:], bank(1)))
                    for rb in range(8):
                        r.op(V, lambda E, rb=rb: E.tensor_mul(prodscr[:, rb, :], prodscr[:, rb, :],
                                                              rbref[:]))
                        r.op(V, lambda E, rb=rb, tgq=tgq: E.tensor_scalar_mul(
                            qc_bf[:, rb, tgq * 512:(tgq + 1) * 512], prodscr[:, rb, :],
                            qnw_t[:, rb:rb + 1]), inc="v")

                r.wait("tensor", "v")
                for g in range(8):
                    r.op("sync", lambda E, g=g: E.dma_start(
                        out=pos_sl[:], in_=pos[:, g * 512:(g + 1) * 512]), inc="qs", k=16)
                    r.wait("tensor", "qs")
                    r.op("tensor", lambda E, g=g: E.matmul(
                        bank(2 + g % 2)[0:32, :], invf[:], pos_sl[:], start=True, stop=True), inc="t")
                    r.wait(V, "t")
                    r.op(V, lambda E, g=g: E.tensor_copy(ftab[:, g * 512:(g + 1) * 512],
                                                         bank(2 + g % 2)[0:32, :]), inc="v")
                    r.wait("sync", "v")
                    r.wait("tensor", "v")
                for qtr, dst in ((0.0, sink2), (0.25, cosk2)):
                    for q4 in range(4):
                        sl = slice(q4 * 1024, (q4 + 1) * 1024)
                        r.wait(V, "s")
                        r.op(V, lambda E, sl=sl: E.tensor_scalar_mul(trig_s1[:], ftab[:, sl], inv2pi))
                        if qtr:
                            r.op(V, lambda E: E.tensor_scalar_add(trig_s1[:], trig_s1[:], 0.25))
                        r.op(V, lambda E: E.tensor_copy(trig_i[:], trig_s1[:]))
                        r.op(V, lambda E: E.tensor_copy(trig_s2[:], trig_i[:]))
                        r.op(V, lambda E: E.tensor_sub(trig_s1[:], trig_s1[:], trig_s2[:]))
                        r.op(V, lambda E: E.tensor_scalar_mul(trig_s1[:], trig_s1[:], twopi), inc="v")
                        r.wait(Sc, "v")
                        r.op(Sc, lambda E, dst=dst, sl=sl: E.activation(
                            dst[0:32, sl], trig_s1[:], AF.Sin), inc="s")
                r.wait("gpsimd", "s")
                r.op("gpsimd", lambda E: E.dma_start(out=sink2[32:64, :], in_=sink2[0:32, :]),
                     inc="qg", k=16)
                r.op("gpsimd", lambda E: E.dma_start(out=cosk2[32:64, :], in_=cosk2[0:32, :]),
                     inc="qg", k=16)

                r.wait("tensor", "qg")
                r.wait(V, "qg")
                r.wait("tensor", "v")
                for g in range(8):
                    sl = slice(g * 512, (g + 1) * 512)
                    r.op("tensor", lambda E, sl=sl: E.matmul(
                        bank(2)[0:64, :], r64b[:], kro_raw[:, sl], start=True, stop=True), inc="t")
                    r.wait(V, "t")
                    r.op(V, lambda E, sl=sl: E.tensor_tensor(qpe_f[:, 0, :], kro_raw[:, sl],
                                                             cosk2[:, sl], ALU.mult))
                    r.op(V, lambda E, sl=sl: E.tensor_tensor(qpe_f[:, 1, :], bank(2)[0:64, :],
                                                             sink2[:, sl], ALU.mult))
                    r.op(V, lambda E, sl=sl: E.tensor_add(krope2[:, sl], qpe_f[:, 0, :],
                                                          qpe_f[:, 1, :]), inc="v")
                    r.wait("tensor", "v")

                wq3 = wquT.rearrange("(c p) f -> c p f", p=128)
                wqr3 = wqrT.rearrange("(c p) f -> c p f", p=128)
                wkvu3 = wkvuT.rearrange("(c p) f -> c p f", p=128)
                head_ms = {}
                for h in range(NH):
                    if h >= 1:
                        r.wait("gpsimd", "t", head_ms.get(h - 1, 0))
                    for c in range(8):
                        r.op("gpsimd", lambda E, h=h, c=c: E.dma_start(
                            out=wup_h[:, h % 2, c, :], in_=wq3[c, :, h * 128:(h + 1) * 128]),
                            inc="qg", k=16)
                        r.op("gpsimd", lambda E, h=h, c=c: E.dma_start(
                            out=wqr_h[:, h % 2, c, :], in_=wqr3[c, :, h * 64:(h + 1) * 64]),
                            inc="qg", k=16)
                    for c in range(4):
                        r.op("gpsimd", lambda E, h=h, c=c: E.dma_start(
                            out=wkvu_h[:, h % 2, c, :], in_=wkvu3[c, :, h * 256:(h + 1) * 256]),
                            inc="qg", k=16)
                    r.wait("tensor", "qg")
                    r.wait("tensor", "v")
                    r.wait("tensor", "so")
                    kn_cp = {}
                    for g in range(8):
                        if g >= 2:
                            r.wait("tensor", "v", kn_cp[g - 2])
                        for rc in range(4):
                            r.op("tensor", lambda E, h=h, g=g, rc=rc: E.matmul(
                                bank(g % 2), wkvu_h[:, h % 2, rc, 0:128],
                                kvc_bf[:, rc, g * 512:(g + 1) * 512],
                                start=(rc == 0), stop=(rc == 3)), inc=("t" if rc == 3 else None))
                        r.wait(V, "t")
                        r.op(V, lambda E, g=g: E.tensor_copy(knope[:, g * 512:(g + 1) * 512],
                                                             bank(g % 2)), inc="v")
                        kn_cp[g] = r.cnt["v"]
                    r.op(V, lambda E: E.memset(vext[:, :, 64:65], 1.0))
                    r.op(V, lambda E: E.memset(vext[:, :, 129:130], 1.0), inc="v")
                    vx_cp = {}
                    for kt in range(NKT):
                        if kt >= 2:
                            r.wait("tensor", "v", vx_cp[kt - 2])
                        for rc in range(4):
                            r.op("tensor", lambda E, h=h, kt=kt, rc=rc: E.matmul(
                                bank(2 + kt % 2)[:, 0:128], kvc_bf[:, rc, kt * 128:(kt + 1) * 128],
                                wkvu_h[:, h % 2, rc, 128:256],
                                start=(rc == 0), stop=(rc == 3)), inc=("t" if rc == 3 else None))
                        r.wait(V, "t")
                        r.op(V, lambda E, kt=kt: E.tensor_copy(vext[:, kt, 0:64],
                                                               bank(2 + kt % 2)[:, 0:64]))
                        r.op(V, lambda E, kt=kt: E.tensor_copy(vext[:, kt, 65:129],
                                                               bank(2 + kt % 2)[:, 64:128]), inc="v")
                        vx_cp[kt] = r.cnt["v"]
                    qn_cp = {}
                    for g in range(4):
                        if g >= 2:
                            r.wait("tensor", "v", qn_cp[g - 2])
                        for rc in range(8):
                            r.op("tensor", lambda E, h=h, g=g, rc=rc: E.matmul(
                                bank(4 + g % 2), wup_h[:, h % 2, rc, :],
                                qc_bf[:, rc, g * 512:(g + 1) * 512],
                                start=(rc == 0), stop=(rc == 7)), inc=("t" if rc == 7 else None))
                        r.wait(V, "t")
                        r.op(V, lambda E, g=g: E.tensor_copy(qnope[:, g * 512:(g + 1) * 512],
                                                             bank(4 + g % 2)), inc="v")
                        qn_cp[g] = r.cnt["v"]
                    for g in range(4):
                        for rc in range(8):
                            r.op("tensor", lambda E, h=h, g=g, rc=rc: E.matmul(
                                bank(6)[0:64, :], wqr_h[:, h % 2, rc, :],
                                qc_bf[:, rc, g * 512:(g + 1) * 512],
                                start=(rc == 0), stop=(rc == 7)), inc=("t" if rc == 7 else None))
                        r.wait(V, "t")
                        r.op(V, lambda E, g=g: E.tensor_copy(qpe_f[:, 0, :], bank(6)[0:64, :]), inc="v")
                        r.wait("tensor", "v")
                        r.op("tensor", lambda E, g=g: E.matmul(
                            bank(7)[0:64, :], r64[:], qpe_f[:, 0, :], start=True, stop=True), inc="t")
                        qsl = slice(TOK + g * 512, TOK + (g + 1) * 512)
                        r.wait(V, "t")
                        r.op(V, lambda E, g=g, qsl=qsl: E.tensor_tensor(
                            qpe_f[:, 1, :], qpe_f[:, 0, :], cosk2[:, qsl], ALU.mult))
                        r.op(V, lambda E, g=g, qsl=qsl: E.tensor_tensor(
                            qpe_f[:, 0, :], bank(7)[0:64, :], sink2[:, qsl], ALU.mult))
                        r.op(V, lambda E, g=g: E.tensor_add(
                            qpe[:, g * 512:(g + 1) * 512], qpe_f[:, 0, :], qpe_f[:, 1, :]), inc="v")
                        r.wait("tensor", "v")
                    r.op(V, lambda E: E.reduce_max(mxq[:, 0:1], knope[:], axis=AX.X,
                                                   apply_absolute_value=True))
                    r.op(V, lambda E: E.reduce_max(mxq[0:64, 2:3], krope2[:], axis=AX.X,
                                                   apply_absolute_value=True), inc="v")
                    for j in (0, 2):
                        r.wait("tensor", "v")
                        r.op("tensor", lambda E, j=j: E.transpose(
                            bank(0)[0:1, 0:128] if j == 0 else bank(0)[0:1, 0:64],
                            mxq[:, j:j + 1] if j == 0 else mxq[0:64, j:j + 1],
                            ident[:] if j == 0 else ident[0:64, 0:64]), inc="t")
                        r.wait(V, "t")
                        r.op(V, lambda E, j=j: E.reduce_max(
                            msc[:, j:j + 1], bank(0)[0:1, 0:128] if j == 0 else bank(0)[0:1, 0:64],
                            axis=AX.X), inc="v")
                    r.op(V, lambda E: E.memset(msc[:, 7:8], 0.0), inc="v")
                    for g in range(4):
                        gsl = slice(g * 512, (g + 1) * 512)
                        r.wait(Sc, "v")
                        r.wait(Sc, "t")
                        r.op(Sc, lambda E, gsl=gsl: E.activation(sqscr[:, 0, :], qnope[:, gsl],
                                                                 AF.Square), inc="s")
                        r.op(Sc, lambda E, gsl=gsl: E.activation(sqscr[0:64, 1, :], qpe[:, gsl],
                                                                 AF.Square), inc="s")
                        r.wait("tensor", "s")
                        r.wait("tensor", "v")
                        r.op("tensor", lambda E: E.matmul(bank(1)[0:1, :], ones_col[:],
                                                          sqscr[:, 0, :], start=True, stop=False))
                        r.op("tensor", lambda E: E.matmul(bank(1)[0:1, :], ones_col[0:64, :],
                                                          sqscr[0:64, 1, :], start=False, stop=True),
                             inc="t")
                        r.wait(V, "t")
                        r.op(V, lambda E: E.reduce_max(msc[:, 6:7], bank(1)[0:1, :], axis=AX.X))
                        r.op(V, lambda E: E.tensor_max(msc[:, 7:8], msc[:, 7:8], msc[:, 6:7]), inc="v")
                    r.op(V, lambda E: E.tensor_mul(msc[:, 4:5], msc[:, 0:1], msc[:, 0:1]))
                    r.op(V, lambda E: E.tensor_scalar_mul(msc[:, 4:5], msc[:, 4:5], 128.0))
                    r.op(V, lambda E: E.tensor_mul(msc[:, 5:6], msc[:, 2:3], msc[:, 2:3]))
                    r.op(V, lambda E: E.tensor_scalar_mul(msc[:, 5:6], msc[:, 5:6], 64.0))
                    r.op(V, lambda E: E.tensor_add(msc[:, 4:5], msc[:, 4:5], msc[:, 5:6]))
                    r.op(V, lambda E: E.tensor_mul(msc[:, 4:5], msc[:, 4:5], msc[:, 7:8]), inc="v")
                    r.wait(Sc, "v")
                    r.op(Sc, lambda E: E.activation(msc[:, 4:5], msc[:, 4:5], AF.Sqrt), inc="s")
                    r.wait(V, "s")
                    r.op(V, lambda E: E.tensor_single_scalar(biasr[:], maskrow[:], 0.0, ALU.add))
                    r.op(V, lambda E: E.tensor_scalar_mul(msc[:, 4:5], msc[:, 4:5], -1.0))
                    r.op(V, lambda E: E.tensor_scalar_add(biasr[:], biasr[:], msc[:, 4:5]), inc="v")
                    r.wait("tensor", "v")
                    r.op("tensor", lambda E: E.matmul(bank(0)[:, 0:128], ones_row[:], biasr[:],
                                                      start=True, stop=True), inc="t")
                    r.wait(V, "t")
                    r.op(V, lambda E: E.tensor_copy(bias_h[:], bank(0)[:, 0:128]), inc="v")

                    r.wait(Sc, "v")
                    r.wait("tensor", "v")
                    exp_gp = {}

                    av_gp = {}
                    P = 0
                    ev_v = r.cnt["v"]
                    for j in range(4):
                        nkt = 20 + 4 * j
                        npair = nkt // 2
                        qsl = slice(j * 512, (j + 1) * 512)
                        mask_v = {}

                        def emit_av(Pa, pa):
                            pb_ = Pa % 2
                            kt0_ = 2 * pa
                            if pa == 0:
                                r.wait("tensor", "v", ev_v)
                            if Pa in mask_v:
                                r.wait("tensor", "v", mask_v[Pa])
                            else:
                                r.wait("tensor", "s", exp_gp[Pa])
                            for q in (0, 1):
                                kt = kt0_ + q
                                psl = slice(pb_ * 1024 + q * 512, pb_ * 1024 + (q + 1) * 512)
                                r.op("tensor", lambda E, kt=kt, nkt=nkt, psl=psl: E.matmul(
                                    bank(4)[0:65, :], vext[:, kt, 0:65], ptile[:, psl],
                                    start=(kt == 0), stop=(kt == nkt - 1)))
                                r.op("tensor", lambda E, kt=kt, nkt=nkt, psl=psl: E.matmul(
                                    bank(5)[0:65, :], vext[:, kt, 65:130], ptile[:, psl],
                                    start=(kt == 0), stop=(kt == nkt - 1)),
                                    inc=("t" if q == 1 else None))
                            av_gp[Pa] = r.cnt["t"]

                        for p in range(npair):
                            kt0 = 2 * p
                            pb = P % 2
                            if P >= 2:
                                r.wait("tensor", "s", exp_gp[P - 2])
                            for q in (0, 1):
                                kt = kt0 + q
                                sb_ = bank(2 * pb + q)
                                r.op("tensor", lambda E, kt=kt, qsl=qsl, sb_=sb_: E.matmul(
                                    sb_, knope[:, kt * 128:(kt + 1) * 128], qnope[:, qsl],
                                    start=True, stop=False))
                                r.op("tensor", lambda E, kt=kt, qsl=qsl, sb_=sb_: E.matmul(
                                    sb_, krope2[:, kt * 128:(kt + 1) * 128], qpe[:, qsl],
                                    start=False, stop=True), inc=("t" if q == 1 else None))
                            smm = r.cnt["t"]
                            r.wait(Sc, "t", smm)
                            if P >= 2:
                                r.wait(Sc, "t", av_gp[P - 2])
                            bcol = j * 32 + kt0
                            r.op(Sc, lambda E, pb=pb, bcol=bcol: E.activation(
                                ptile[:, pb * 1024:(pb + 1) * 1024], PS[:, pb * 1024:(pb + 1) * 1024],
                                AF.Exp, bias=bias_h[:, bcol:bcol + 1]), inc="s")
                            exp_gp[P] = r.cnt["s"]
                            dd0 = kt0 - (16 + 4 * j)
                            if dd0 >= 0:
                                r.wait(V, "s", exp_gp[P])
                                for q in (0, 1):
                                    r.op(V, lambda E, pb=pb, q=q, dd0=dd0: E.tensor_tensor(
                                        ptile[:, pb * 1024 + q * 512:pb * 1024 + (q + 1) * 512],
                                        ptile[:, pb * 1024 + q * 512:pb * 1024 + (q + 1) * 512],
                                        m01[:, dd0 + q, :], ALU.mult),
                                        inc=("v" if q == 1 else None))
                                mask_v[P] = r.cnt["v"]
                            if p > 0:
                                emit_av(P - 1, p - 1)
                            P += 1
                        emit_av(P - 1, npair - 1)
                        r.wait(V, "t", av_gp[P - 1])
                        r.op(V, lambda E: E.tensor_copy(rows_rms[:, 512:1024], bank(4)[64:65, :]))
                        r.op(V, lambda E: E.tensor_scalar_add(rows_rms[:, 512:1024],
                                                              rows_rms[:, 512:1024], 1e-38))
                        r.wait(V, "v")
                        r.op(V, lambda E: E.reciprocal(rows_rms[:, 512:1024], rows_rms[:, 512:1024]),
                             inc="v")
                        r.wait("sync", "v")
                        r.op("sync", lambda E: E.dma_start(out=dsc, in_=rows_rms[0:1, 512:1024]),
                             inc="qs", k=16)
                        r.wait("sync", "qs")
                        r.op("sync", lambda E: E.dma_start(out=rbref[:],
                                                           in_=dsc.broadcast_to((128, 512))),
                             inc="qs", k=16)
                        r.wait(V, "qs")
                        r.wait(V, "so")
                        r.op(V, lambda E, j=j: E.scalar_tensor_tensor(
                            att_t[0:64, 0, :], rbref[0:64, :], 1.0, bank(4)[0:64, :],
                            ALU.mult, ALU.mult))
                        r.op(V, lambda E, j=j: E.scalar_tensor_tensor(
                            att_t[64:128, 0, :], rbref[64:128, :], 1.0, bank(5)[0:64, :],
                            ALU.mult, ALU.mult), inc="v")
                        ev_v = r.cnt["v"]
                        r.wait("sync", "v")
                        r.op("sync", lambda E, h=h, j=j: E.dma_start(
                            out=asp3[h, :, j * 512:(j + 1) * 512], in_=att_t[:, 0, :]),
                            inc="so", k=16)
                    head_ms[h] = r.cnt["t"]

                woT3 = woT.rearrange("(c p) d -> c p d", p=128)
                for qg in range(4):
                    r.wait("gpsimd", "so")
                    r.wait("gpsimd", "t")
                    for hc in range(NCH):
                        r.op("gpsimd", lambda E, qg=qg, hc=hc: E.dma_start(
                            out=att_qg[:, hc, :], in_=asp3[hc, :, qg * 512:(qg + 1) * 512]),
                            inc="qg", k=16)
                    for dh in range(2):
                        r.wait("gpsimd", "t")
                        for hc in range(NCH):
                            r.op("gpsimd", lambda E, qg=qg, dh=dh, hc=hc: E.dma_start(
                                out=wo_s[:, hc, :], in_=woT3[hc, :, dh * 1024:(dh + 1) * 1024]),
                                inc="qg", k=16)
                        r.wait("tensor", "qg")
                        if qg > 0 or dh > 0:
                            r.wait("tensor", "v")
                        for dblk in range(8):
                            for hc in range(NCH):
                                r.op("tensor", lambda E, dblk=dblk, hc=hc: E.matmul(
                                    bank(dblk), wo_s[:, hc, dblk * 128:(dblk + 1) * 128],
                                    att_qg[:, hc, :], start=(hc == 0), stop=(hc == NCH - 1)),
                                    inc=("t" if hc == NCH - 1 else None))
                        for dblk in range(8):
                            c = dh * 8 + dblk
                            if dblk >= 2 or dh > 0 or qg > 0:
                                r.wait("sync", "v")
                            r.op("sync", lambda E, c=c, qg=qg, dblk=dblk: E.dma_start(
                                out=x32k[:, dblk % 2, :],
                                in_=xkv3[c, :, TOK + qg * 512:TOK + (qg + 1) * 512]), inc="qs", k=16)
                            r.wait(V, "qs")
                            r.wait(V, "t")
                            r.op(V, lambda E, c=c, dblk=dblk: E.scalar_tensor_tensor(
                                pre[:, c, :], x32k[:, dblk % 2, :], 1.0, bank(dblk),
                                ALU.mult, ALU.add), inc="v")
                    r.wait("scalar", "v")
                    _ln_emit(r, pre, NCH, sqscr, PS[0:1, 2048:2560], PS[0:1, 2560:3072],
                             ones_col, ones_row, rowscr_f, abc, bbc,
                             [PS[:, 3072:3584], PS[:, 3584:4096]], 512)
                    r.wait("sync", "v")
                    for c in range(NCH):
                        r.op("sync", lambda E, c=c, qg=qg: E.dma_start(
                            out=x33[c, :, qg * 512:(qg + 1) * 512], in_=pre[:, c, :]), inc="so", k=16)
                r.wait("sync", "so")

        def _phase_D():
            with ExitStack() as ctx:
                sb = lambda n, s, d=F32: ctx.enter_context(nc.sbuf_tensor("D_" + n, s, d))
                xbf = sb("xbf", [128, NCH, TOK], BF16)
                sgn = sb("sgn", [128, NCH, D], BF16)
                wld = sb("wld", [128, 2, D])
                alpha = sb("alpha", [128, NCH])
                bias_t = sb("bias_t", [128, NCH])
                x32 = sb("x32", [128, 2, 512])
                scr = sb("scr", [128, 2048])
                pre = sb("pre", [128, NCH, 512])
                sqscr = sb("sqscr", [128, 2, 512])
                rowscr = sb("rowscr", [1, 2048])
                abc = sb("abc", [128, 512])
                bbc = sb("bbc", [128, 512])
                ones_col = sb("ones_col", [128, 1])
                ones_row = sb("ones_row", [1, 128])

                bank = lambda i: PS[:, 512 * i:512 * (i + 1)]

                V, Sc = "vector", "scalar"

                r.op("sync", lambda E: E.dma_start(out=bias_t[:], in_=bcol), inc="qs", k=16)
                r.wait(V, "qs")
                r.op(V, lambda E: E.memset(ones_col[:], 1.0))
                r.op(V, lambda E: E.memset(ones_row[:], 1.0))

                ld_ms = {}
                for c in range(NCH):
                    if c >= 2:
                        r.wait("sync", "v", ld_ms[c - 2])
                        r.wait("gpsimd", "v", ld_ms[c - 2])
                    r.op("sync", lambda E, c=c: E.dma_start(out=wld[:, c % 2, :], in_=wo3[c]),
                         inc="qs", k=16)
                    r.op("gpsimd", lambda E, c=c: E.dma_start(out=xbf[:, c, :], in_=x33[c]),
                         inc="qg", k=16)
                    r.wait(V, "qs")
                    r.op(V, lambda E, c=c: E.tensor_reduce(alpha[:, c:c + 1], wld[:, c % 2, :],
                                                           AX.X, ALU.add, apply_absolute_value=True),
                         inc="v")
                    ld_ms[c] = r.cnt["v"]
                for c in range(NCH):
                    r.wait("sync", "v")
                    r.op("sync", lambda E, c=c: E.dma_start(out=wld[:, c % 2, :], in_=wT3[c]),
                         inc="qs", k=16)
                    r.wait(V, "qs")
                    r.op(V, lambda E, c=c: E.tensor_single_scalar(scr[:], wld[:, c % 2, :],
                                                                  0.0, ALU.is_ge))
                    r.op(V, lambda E, c=c: E.tensor_scalar(sgn[:, c, :], scr[:], 2.0, -1.0,
                                                           ALU.mult, ALU.add), inc="v")
                r.op(V, lambda E: E.tensor_scalar_mul(alpha[:], alpha[:], 1.0 / float(D)), inc="v")

                r.wait("tensor", "qg")
                r.wait("tensor", "v")
                for tg in range(4):
                    for dblk in range(NCH):
                        if tg + dblk > 0:
                            r.wait("tensor", "v")
                        for c in range(NCH):
                            r.op("tensor", lambda E, c=c, dblk=dblk, tg=tg: E.matmul(
                                bank(dblk % 2), sgn[:, c, dblk * 128:(dblk + 1) * 128],
                                xbf[:, c, tg * 512:(tg + 1) * 512],
                                start=(c == 0), stop=(c == NCH - 1)), inc=("t" if c == NCH - 1 else None))
                        if dblk >= 2:
                            r.wait("sync", "v")
                        r.op("sync", lambda E, dblk=dblk, tg=tg: E.dma_start(
                            out=x32[:, dblk % 2, :], in_=x33[dblk, :, tg * 512:(tg + 1) * 512]),
                            inc="qs", k=16)
                        r.wait(V, "qs")
                        r.wait(V, "t")
                        r.op(V, lambda E, dblk=dblk: E.tensor_scalar(
                            pre[:, dblk, :], bank(dblk % 2), alpha[:, dblk:dblk + 1],
                            bias_t[:, dblk:dblk + 1], ALU.mult, ALU.add))
                        r.op(V, lambda E, dblk=dblk: E.tensor_add(
                            pre[:, dblk, :], pre[:, dblk, :], x32[:, dblk % 2, :]), inc="v")
                    r.wait("scalar", "v")
                    _ln_emit(r, pre, NCH, sqscr, PS[0:1, 2048:2560], PS[0:1, 2560:3072],
                             ones_col, ones_row, rowscr, abc, bbc,
                             [PS[:, 3072:3584], PS[:, 3584:4096]], 512)
                    r.wait("sync", "v")
                    if tg > 0:
                        pass
                    for c in range(NCH):
                        r.op("sync", lambda E, c=c, tg=tg: E.dma_start(
                            out=x43[c, :, tg * 512:(tg + 1) * 512], in_=pre[:, c, :]), inc="so", k=16)
                    r.wait("vector", "so")
                r.wait("sync", "so")

        _phase_C()
        barrier()
        _phase_D()
        r.wait("sync", "so")
        r.play(block)
    return nc


def build_fused_single():
    nc = bass.Bass("TRN2", target_bir_lowering=False, debug=False)
    dbg = False
    u_pref = nc.dram_tensor("u_pref", [D, S], F32, kind="ExternalInput").ap()
    lam_ur = nc.dram_tensor("lam_ur", [1, 32], F32, kind="ExternalInput").ap()
    lam_im = nc.dram_tensor("lam_im", [1, 32], F32, kind="ExternalInput").ap()
    logdt = nc.dram_tensor("logdt", [1, 64], F32, kind="ExternalInput").ap()
    bre = nc.dram_tensor("bre", [32, D], F32, kind="ExternalInput").ap()
    bim = nc.dram_tensor("bim", [32, D], F32, kind="ExternalInput").ap()
    cret = nc.dram_tensor("cret", [32, D], F32, kind="ExternalInput").ap()
    cimt = nc.dram_tensor("cimt", [32, D], F32, kind="ExternalInput").ap()
    dcol = nc.dram_tensor("dcol", [128, NCH], F32, kind="ExternalInput").ap()
    x1 = nc.dram_tensor("x1", [D, TOK], F32, kind="Internal").ap()
    rwt = nc.dram_tensor("rwt", [D, 8], F32, kind="ExternalInput").ap()
    w1 = nc.dram_tensor("w1", [8, D, 1024], F32, kind="ExternalInput").ap()
    w2 = nc.dram_tensor("w2", [8, D, 1024], F32, kind="ExternalInput").ap()
    w3 = nc.dram_tensor("w3", [8, 1024, D], F32, kind="ExternalInput").ap()
    hsp = nc.dram_tensor("hsp", [8, 1024, TOK], BF16, kind="Internal").ap()
    x2pre = nc.dram_tensor("x2pre", [D, TOK], F32, kind="Internal").ap()
    xkv = nc.dram_tensor("xkv", [D, S], F32, kind="Internal").ap()
    pos = nc.dram_tensor("pos", [1, S], F32, kind="ExternalInput").ap()
    maskrow_d = nc.dram_tensor("maskrow", [1, 128], F32, kind="ExternalInput").ap()
    wqdT = nc.dram_tensor("wqdT", [D, 1024], F32, kind="ExternalInput").ap()
    qnw = nc.dram_tensor("qnw", [128, 8], F32, kind="ExternalInput").ap()
    wquT = nc.dram_tensor("wquT", [1024, 2048], F32, kind="ExternalInput").ap()
    wqrT = nc.dram_tensor("wqrT", [1024, 1024], F32, kind="ExternalInput").ap()
    wkvdT = nc.dram_tensor("wkvdT", [D, 512], F32, kind="ExternalInput").ap()
    kvnw = nc.dram_tensor("kvnw", [128, 4], F32, kind="ExternalInput").ap()
    wkvuT = nc.dram_tensor("wkvuT", [512, 4096], F32, kind="ExternalInput").ap()
    wkrT = nc.dram_tensor("wkrT", [D, 64], F32, kind="ExternalInput").ap()
    woT = nc.dram_tensor("woT", [D, D], F32, kind="ExternalInput").ap()
    x3 = nc.dram_tensor("x3", [D, TOK], F32, kind="Internal").ap()
    aspill = nc.dram_tensor("aspill", [D, TOK], BF16, kind="Internal").ap()
    w_o = nc.dram_tensor("w_o", [D, D], F32, kind="ExternalInput").ap()
    wT = nc.dram_tensor("wT", [D, D], F32, kind="ExternalInput").ap()
    bcol = nc.dram_tensor("bcol", [128, NCH], F32, kind="ExternalInput").ap()
    x4 = nc.dram_tensor("x4", [D, TOK], F32, kind="ExternalOutput").ap()
    hflag = nc.dram_tensor("hflag", [1, 4], I32, kind="ExternalInput").ap()
    nonce = nc.dram_tensor("nonce", [1, 4], I32, kind="ExternalInput").ap()
    shr = nc.dram_tensor("shr", [D, TOK], F32, kind="Internal",
                         addr_space="Shared").ap()
    shrf = nc.dram_tensor("shrf", [1, 4], I32, kind="Internal",
                          addr_space="Shared").ap()

    NT = TOK // 512
    NFB = 8
    NH = 16
    NKT = S // 128
    SCALE = 1.0 / float(np.sqrt(192.0))
    inv2pi = 1.0 / (2.0 * np.pi)
    twopi = 2.0 * np.pi
    u3 = u_pref.rearrange("(c p) t -> c p t", p=128)
    x13 = x1.rearrange("(c p) t -> c p t", p=128)
    x2p3 = x2pre.rearrange("(c p) t -> c p t", p=128)
    rwt3 = rwt.rearrange("(c p) e -> c p e", p=128)
    xkv3 = xkv.rearrange("(c p) t -> c p t", p=128)
    x33 = x3.rearrange("(c p) t -> c p t", p=128)
    asp3 = aspill.rearrange("(c p) t -> c p t", p=128)
    x43 = x4.rearrange("(c p) t -> c p t", p=128)
    wo3 = w_o.rearrange("(c p) i -> c p i", p=128)
    wT3 = wT.rearrange("(c p) o -> c p o", p=128)
    shr3 = shr.rearrange("(c p) t -> c p t", p=128)

    with ExitStack() as topctx:
        r = Rec(nc, topctx)
        for s_ in ("qs", "qg", "v", "s", "t", "so", "z"):
            r.sem(s_)
        PS = topctx.enter_context(nc.psum_tensor("PS", [128, 4096], F32))
        hfs = topctx.enter_context(nc.sbuf_tensor("hfs", [1, 4], I32))
        nons = topctx.enter_context(nc.sbuf_tensor("nons", [1, 4], I32))
        fls = topctx.enter_context(nc.sbuf_tensor("fls", [1, 4], I32))
        xe = topctx.enter_context(nc.semaphore("xe"))
        jk = topctx.enter_context(nc.semaphore("jk"))
        block = topctx.enter_context(nc.Block())

        def barrier():
            for eng in ENGINES:
                for snm in ("qs", "qg", "v", "s", "t", "so", "z"):
                    r.wait(eng, snm)

        def _prologue():
            with ExitStack() as ctx:
                zt = ctx.enter_context(nc.sbuf_tensor("zt", [128, 512], F32))
                r.op("sync", lambda E: E.dma_start(out=hfs[:], in_=hflag),
                     inc="z", k=16)
                r.op("sync", lambda E: E.dma_start(out=nons[:], in_=nonce),
                     inc="z", k=16)
                r.op("vector", lambda E: E.memset(zt[:], 0.0), inc="z", k=1)
                r.wait("sync", "z")
                for c in range(NCH):
                    for t in range(4):
                        r.op("sync", lambda E, c=c, t=t: E.dma_start(
                            out=xkv3[c, :, t * 512:(t + 1) * 512], in_=zt[:]),
                            inc="z", k=16)
                r.wait("sync", "z")
                r.wait("gpsimd", "z")

        def _exchange():
            def emit(E):
                hv = E.value_load(hfs[0:1, 0:1])
                nv = E.value_load(nons[0:1, 0:1])
                with E.register("d_ex") as d:
                    with E.If(hv):
                        def cond():
                            E.dma_start(out=fls[:], in_=shrf).then_inc(jk, 16)
                            for _ in range(16):
                                E.reg_alu(d, nv, 0, AOP.add)
                            fv = E.value_load(fls[0:1, 0:1])
                            E.reg_alu(d, fv, nv, AOP.subtract)
                            return d
                        with E.While(cond):
                            E.reg_alu(d, nv, 0, AOP.add)
                        for c in range(NCH):
                            E.dma_start(out=xkv3[c, :, 0:TOK],
                                        in_=shr3[c]).then_inc(xe, 16)
                        E.wait_ge(xe, 16 * NCH)
                    with E.Else():
                        for c in range(NCH):
                            E.dma_start(out=shr3[c],
                                        in_=xkv3[c, :, TOK:2 * TOK]).then_inc(xe, 16)
                        E.wait_ge(xe, 16 * NCH)
                        E.dma_start(out=shrf, in_=nons[:]).then_inc(xe, 16)
                        E.wait_ge(xe, 16 * NCH + 16)
            r.op("sync", emit)
            r.op("sync", lambda E: E.dma_start(out=fls[:], in_=shrf),
                 inc="z", k=16)

        def _phase_A():
            with ExitStack() as ctx:
                sb = lambda n, s, d=F32: ctx.enter_context(nc.sbuf_tensor("A_" + n, s, d))
                row_par = sb("row_par", [1, 32 * 24])
                logdt_t = sb("logdt_t", [1, 64])
                lam_ur_row = sb("lam_ur_t", [1, 32])
                lam_im_row = sb("lam_im_t", [1, 32])
                ki32 = sb("ki32", [1, 64], I32)
                dcol_t = sb("dcol_t", [128, NCH])
                d1col = sb("d1col", [128, NCH])
                idv = sb("idv", [128, 128])
                ident = sb("ident", [128, 128])
                ones_col = sb("ones_col", [128, 1])
                ones_row = sb("ones_row", [1, 128])
                scal6 = sb("scal6", [32, 6])
                a128c = sb("a128c", [32, 2])
                tp = sb("tp", [32, 2, 128])
                tm = sb("tm", [32, 2, 128])
                tcol = sb("tcol", [32, 2])
                wcol = sb("wcol", [32, 2])
                ccol = sb("ccol", [32, 4])
                ee = sb("ee", [32, 2, 32])
                bpack = sb("bpack", [32, 4, D])
                bre_t = bpack[:, 0, :]
                bim_t = bpack[:, 1, :]
                cret_t = bpack[:, 2, :]
                cimt_t = bpack[:, 3, :]
                d0s = bpack[:, 0:2, :].rearrange("n a b -> n (a b)")
                sca = bpack[:, 2, :]
                scb = bpack[:, 3, :]
                bbarp = sb("bbarp", [32, 2, D])
                bbre = bbarp[:, 0, :]
                bbim = bbarp[:, 1, :]
                scr2 = sb("scr2", [32, D])
                xs_bf = bbarp[:].rearrange("n a b -> n (a b)").bitcast(BF16).rearrange("n (a b) -> n a b", a=2)
                bbar_Tre = sb("bbar_Tre", [128, NCH, 32], BF16)
                bbar_Tim = sb("bbar_Tim", [128, NCH, 32], BF16)
                cpack_re = sb("cpack_re", [32, NCH, 128], BF16)
                cpack_im = sb("cpack_im", [32, NCH, 128], BF16)
                u_bf = sb("u_bf", [128, NCH, 512], BF16)
                bu = sb("bu", [32, 2, S])
                cum = sb("cum", [32, 2, S])
                u32b = sb("u32b", [128, 2, 512])
                pre = sb("pre", [128, NCH, 512])
                sqscr = sb("sqscr", [128, 2, 512])
                rowscr = sb("rowscr", [1, 2048])
                abc = sb("abc", [128, 512])
                bbc = sb("bbc", [128, 512])

                ps_bu_re = PS[0:32, 0:512]
                ps_bu_im = PS[0:32, 512:1024]
                ps_y = [PS[:, 0:512], PS[:, 512:1024]]
                ps_tr = PS[:, 1024:1056]
                ps_6 = PS[0:32, 1536:1542]
                ps_sum = PS[0:1, 2048:2560]
                ps_sumsq = PS[0:1, 2560:3072]
                ps_bc = [PS[:, 3072:3584], PS[:, 3584:4096]]

                ROWS = {}
                for i, nm in enumerate(
                    ["delta", "ab", "en", "ln1", "mx", "sp", "lamre", "rr", "th", "y",
                     "kf", "fr", "ang", "yc", "frc", "angc", "sinth", "costh", "er",
                     "are", "aim", "asq", "den", "tmp"]):
                    ROWS[nm] = row_par[:, 32 * i:32 * i + 32]


                for (dst, src) in [(lam_ur_row[:], lam_ur), (lam_im_row[:], lam_im),
                                   (logdt_t[:], logdt), (bre_t, bre), (bim_t, bim),
                                   (cret_t, cret), (cimt_t, cimt), (dcol_t[:], dcol)]:
                    r.op("sync", lambda E, dst=dst, src=src: E.dma_start(out=dst, in_=src), inc="qs", k=16)
                r.op("gpsimd", lambda E: E.iota(idv[:], pattern=[[-1, 128]], channel_multiplier=1,
                                                allow_small_or_imprecise_dtypes=True), inc="qg", k=1)

                r.wait("vector", "qs")
                r.wait("vector", "qg")
                r.wait("scalar", "qs")
                V, Sc = "vector", "scalar"
                R = ROWS
                r.op(V, lambda E: E.tensor_single_scalar(ident[:], idv[:], 0.0, ALU.is_equal), inc="v")
                r.op(V, lambda E: E.memset(ones_col[:], 1.0), inc="v")
                r.op(V, lambda E: E.memset(ones_row[:], 1.0), inc="v")
                r.op(V, lambda E: E.tensor_scalar_add(d1col[:], dcol_t[:], 1.0), inc="v")
                r.op(Sc, lambda E: E.activation(R["delta"], logdt_t[:, 0:32], AF.Exp), inc="s")
                r.op(Sc, lambda E: E.activation(R["ab"], lam_ur_row[:], AF.Abs), inc="s")
                r.op(Sc, lambda E: E.activation(R["en"], R["ab"], AF.Exp, scale=-1.0), inc="s")
                r.op(Sc, lambda E: E.activation(R["ln1"], R["en"], AF.Ln, bias=1.0), inc="s")
                r.wait(V, "s")
                r.op(V, lambda E: E.tensor_scalar_max(R["mx"], lam_ur_row[:], 0.0), inc="v")
                r.op(V, lambda E: E.tensor_add(R["sp"], R["mx"], R["ln1"]), inc="v")
                r.wait(Sc, "v")
                r.op(Sc, lambda E: E.activation(R["lamre"], R["sp"], AF.Copy, scale=-1.0), inc="s")
                r.wait(V, "s")
                r.op(V, lambda E: E.tensor_scalar_add(R["lamre"], R["lamre"], -0.0101), inc="v")
                r.op(V, lambda E: E.tensor_mul(R["rr"], R["lamre"], R["delta"]), inc="v")
                r.op(V, lambda E: E.tensor_mul(R["th"], lam_im_row[:], R["delta"]), inc="v")
                inv2pi = 1.0 / (2.0 * np.pi)
                twopi = 2.0 * np.pi
                r.op(V, lambda E: E.tensor_scalar_mul(R["y"], R["th"], inv2pi), inc="v")
                r.op(V, lambda E: E.tensor_copy(ki32[:, 0:32], R["y"]), inc="v")
                r.wait(V, "v")
                r.op(V, lambda E: E.tensor_copy(R["kf"], ki32[:, 0:32]), inc="v")
                r.wait(V, "v")
                r.op(V, lambda E: E.tensor_sub(R["fr"], R["y"], R["kf"]), inc="v")
                r.op(V, lambda E: E.tensor_scalar_mul(R["ang"], R["fr"], twopi), inc="v")
                r.op(V, lambda E: E.tensor_scalar_add(R["yc"], R["y"], 0.25), inc="v")
                r.op(V, lambda E: E.tensor_copy(ki32[:, 32:64], R["yc"]), inc="v")
                r.wait(V, "v")
                r.op(V, lambda E: E.tensor_copy(R["kf"], ki32[:, 32:64]), inc="v")
                r.wait(V, "v")
                r.op(V, lambda E: E.tensor_sub(R["frc"], R["yc"], R["kf"]), inc="v")
                r.op(V, lambda E: E.tensor_scalar_mul(R["angc"], R["frc"], twopi), inc="v")
                r.wait(Sc, "v")
                r.op(Sc, lambda E: E.activation(R["sinth"], R["ang"], AF.Sin), inc="s")
                r.op(Sc, lambda E: E.activation(R["costh"], R["angc"], AF.Sin), inc="s")
                r.op(Sc, lambda E: E.activation(R["er"], R["rr"], AF.Exp), inc="s")
                r.wait(V, "s")
                r.op(V, lambda E: E.tensor_mul(R["are"], R["er"], R["costh"]), inc="v")
                r.op(V, lambda E: E.tensor_mul(R["aim"], R["er"], R["sinth"]), inc="v")
                r.op(V, lambda E: E.tensor_mul(R["asq"], R["are"], R["are"]), inc="v")
                r.op(V, lambda E: E.tensor_mul(R["tmp"], R["aim"], R["aim"]), inc="v")
                r.op(V, lambda E: E.tensor_add(R["asq"], R["asq"], R["tmp"]), inc="v")
                r.op(V, lambda E: E.reciprocal(R["asq"], R["asq"]), inc="v")
                r.wait(V, "v")
                r.op(V, lambda E: E.tensor_mul(R["en"], R["are"], R["asq"]), inc="v")
                r.op(V, lambda E: E.tensor_mul(R["ln1"], R["aim"], R["asq"]), inc="v")
                r.op(V, lambda E: E.tensor_scalar_mul(R["ln1"], R["ln1"], -1.0), inc="v")
                r.op(V, lambda E: E.tensor_mul(R["den"], R["lamre"], R["lamre"]), inc="v")
                r.op(V, lambda E: E.tensor_mul(R["tmp"], lam_im_row[:], lam_im_row[:]), inc="v")
                r.op(V, lambda E: E.tensor_add(R["den"], R["den"], R["tmp"]), inc="v")
                r.op(V, lambda E: E.reciprocal(R["den"], R["den"]), inc="v")
                r.wait(V, "v")
                r.op(V, lambda E: E.tensor_scalar_add(R["mx"], R["are"], -1.0), inc="v")
                r.op(V, lambda E: E.tensor_mul(R["sp"], R["mx"], R["lamre"]), inc="v")
                r.op(V, lambda E: E.tensor_mul(R["tmp"], R["aim"], lam_im_row[:]), inc="v")
                r.op(V, lambda E: E.tensor_add(R["sp"], R["sp"], R["tmp"]), inc="v")
                r.op(V, lambda E: E.tensor_mul(R["sp"], R["sp"], R["den"]), inc="v")
                r.op(V, lambda E: E.tensor_mul(R["ab"], R["aim"], R["lamre"]), inc="v")
                r.op(V, lambda E: E.tensor_mul(R["tmp"], R["mx"], lam_im_row[:]), inc="v")
                r.op(V, lambda E: E.tensor_sub(R["ab"], R["ab"], R["tmp"]), inc="v")
                r.op(V, lambda E: E.tensor_mul(R["ab"], R["ab"], R["den"]), inc="v")
                ones11 = ones_row[:, 0:1]
                r.wait("tensor", "v")
                for j, nm in enumerate(["sp", "ab", "are", "aim", "en", "ln1"]):
                    r.op("tensor", lambda E, j=j, nm=nm: E.matmul(
                        ps_6[:, j:j + 1], R[nm], ones11, start=True, stop=True),
                        inc=("t" if j == 5 else None))
                r.wait(V, "t")
                r.op(V, lambda E: E.tensor_copy(scal6[:], ps_6), inc="v")
                s_re_c = scal6[:, 0:1]
                s_im_c = scal6[:, 1:2]
                a_re_c = scal6[:, 2:3]
                a_im_c = scal6[:, 3:4]
                ainv_re_c = scal6[:, 4:5]
                ainv_im_c = scal6[:, 5:6]

                def build_table(T, base_re_c, base_im_c):
                    Tre = T[:, 0, :]
                    Tim = T[:, 1, :]
                    r.op(V, lambda E: E.memset(T[:, :, 0:1], 0.0), inc="v")
                    r.op(V, lambda E: E.memset(Tre[:, 0:1], 1.0), inc="v")
                    r.op(V, lambda E: E.tensor_copy(Tre[:, 1:2], base_re_c), inc="v")
                    r.op(V, lambda E: E.tensor_copy(Tim[:, 1:2], base_im_c), inc="v")
                    k = 2
                    while k <= 64:
                        r.op(V, lambda E, k=k: E.tensor_mul(tcol[:, 0:1], Tre[:, k - 1:k], base_re_c))
                        r.op(V, lambda E, k=k: E.tensor_mul(wcol[:, 0:1], Tim[:, k - 1:k], base_im_c))
                        r.op(V, lambda E: E.tensor_sub(tcol[:, 0:1], tcol[:, 0:1], wcol[:, 0:1]))
                        r.op(V, lambda E, k=k: E.tensor_mul(tcol[:, 1:2], Tre[:, k - 1:k], base_im_c))
                        r.op(V, lambda E, k=k: E.tensor_mul(wcol[:, 1:2], Tim[:, k - 1:k], base_re_c))
                        r.op(V, lambda E: E.tensor_add(tcol[:, 1:2], tcol[:, 1:2], wcol[:, 1:2]))
                        r.op(V, lambda E, k=k: E.tensor_scalar_mul(Tre[:, k:2 * k], Tre[:, 0:k], tcol[:, 0:1]))
                        r.op(V, lambda E, k=k: E.tensor_scalar_mul(sca[:, 0:k], Tim[:, 0:k], tcol[:, 1:2]))
                        r.op(V, lambda E, k=k: E.tensor_sub(Tre[:, k:2 * k], Tre[:, k:2 * k], sca[:, 0:k]))
                        r.op(V, lambda E, k=k: E.tensor_scalar_mul(Tim[:, k:2 * k], Tim[:, 0:k], tcol[:, 0:1]))
                        r.op(V, lambda E, k=k: E.tensor_scalar_mul(sca[:, 0:k], Tre[:, 0:k], tcol[:, 1:2]))
                        r.op(V, lambda E, k=k: E.tensor_add(Tim[:, k:2 * k], Tim[:, k:2 * k], sca[:, 0:k]),
                             inc="v")
                        k *= 2

                build_table(tp, a_re_c, a_im_c)
                build_table(tm, ainv_re_c, ainv_im_c)
                r.op(V, lambda E: E.tensor_mul(a128c[:, 0:1], tp[:, 0, 127:128], a_re_c))
                r.op(V, lambda E: E.tensor_mul(wcol[:, 0:1], tp[:, 1, 127:128], a_im_c))
                r.op(V, lambda E: E.tensor_sub(a128c[:, 0:1], a128c[:, 0:1], wcol[:, 0:1]))
                r.op(V, lambda E: E.tensor_mul(a128c[:, 1:2], tp[:, 0, 127:128], a_im_c))
                r.op(V, lambda E: E.tensor_mul(wcol[:, 1:2], tp[:, 1, 127:128], a_re_c))
                r.op(V, lambda E: E.tensor_add(a128c[:, 1:2], a128c[:, 1:2], wcol[:, 1:2]), inc="v")

                r.op(V, lambda E: E.tensor_scalar_mul(bbre, bre_t, s_re_c))
                r.op(V, lambda E: E.tensor_scalar_mul(scr2[:], bim_t, s_im_c))
                r.op(V, lambda E: E.tensor_sub(bbre, bbre, scr2[:]))
                r.op(V, lambda E: E.tensor_scalar_mul(bbim, bre_t, s_im_c))
                r.op(V, lambda E: E.tensor_scalar_mul(scr2[:], bim_t, s_re_c))
                r.op(V, lambda E: E.tensor_add(bbim, bbim, scr2[:]), inc="v")
                id32 = ident[0:32, 0:32]
                for c in range(NCH):
                    for srcb, dstb in ((bbre, bbar_Tre), (bbim, bbar_Tim)):
                        r.wait("tensor", "v")
                        r.op("tensor", lambda E, c=c, srcb=srcb: E.transpose(
                            ps_tr, srcb[:, c * 128:(c + 1) * 128], id32), inc="t")
                        r.wait(V, "t")
                        r.op(V, lambda E, c=c, dstb=dstb: E.tensor_copy(
                            dstb[:, c, :], ps_tr[:, 0:32]), inc="v")
                for c in range(NCH):
                    r.op(Sc, lambda E, c=c: E.activation(cpack_re[:, c, :], cret_t[:, c * 128:(c + 1) * 128],
                                                         AF.Copy, scale=2.0), inc="s")
                    r.op(Sc, lambda E, c=c: E.activation(cpack_im[:, c, :], cimt_t[:, c * 128:(c + 1) * 128],
                                                         AF.Copy, scale=-2.0), inc="s")

                NG = S // 512
                t_group_end = {}
                for g in range(NG):
                    if g >= 1:
                        r.wait("gpsimd", "t", t_group_end[g - 1])
                    for c in range(NCH):
                        r.op("gpsimd", lambda E, c=c, g=g: E.dma_start(
                            out=u_bf[:, c, :],
                            in_=u3[c, :, g * 512:(g + 1) * 512]), inc="qg", k=16)
                    r.wait("tensor", "qg")
                    r.wait("tensor", "v")
                    for c in range(NCH):
                        r.op("tensor", lambda E, c=c: E.matmul(
                            ps_bu_re, bbar_Tre[:, c, :], u_bf[:, c, :],
                            start=(c == 0), stop=(c == NCH - 1)))
                        r.op("tensor", lambda E, c=c: E.matmul(
                            ps_bu_im, bbar_Tim[:, c, :], u_bf[:, c, :],
                            start=(c == 0), stop=(c == NCH - 1)), inc="t")
                    t_group_end[g] = r.cnt["t"]
                    r.wait(V, "t")
                    r.op(V, lambda E, g=g: E.tensor_copy(bu[:, 0, g * 512:(g + 1) * 512], ps_bu_re))
                    r.op(V, lambda E, g=g: E.tensor_copy(bu[:, 1, g * 512:(g + 1) * 512], ps_bu_im), inc="v")

                d03 = d0s.rearrange("n (c t) -> n c t", t=128)
                r.wait("gpsimd", "s")
                r.wait("gpsimd", "v")
                r.op("gpsimd", lambda E: E.iota(d03, pattern=[[0, S // 128], [1, 128]],
                                                channel_multiplier=0,
                                                allow_small_or_imprecise_dtypes=True), inc="qg", k=1)
                r.wait(V, "qg")
                r.wait(V, "s")
                r.op(V, lambda E: E.tensor_single_scalar(d0s, d0s, 0.5, ALU.is_ge), inc="v")
                NCH_S = S // 128
                HCH = NCH_S // 2

                def cplx_combine(dst_sel, in_re, in_im, tab, ta, tb, alu):
                    for half in range(2):
                        lo = half * (S // 2)
                        in_a = in_re[:, lo:lo + S // 2].rearrange("n (c t) -> n c t", t=128)
                        in_b = in_im[:, lo:lo + S // 2].rearrange("n (c t) -> n c t", t=128)
                        tb_a = tab[:, ta, :].unsqueeze(1).broadcast_to([32, HCH, 128])
                        tb_b = tab[:, tb, :].unsqueeze(1).broadcast_to([32, HCH, 128])
                        sca3 = sca.rearrange("n (c t) -> n c t", t=128)
                        scb3 = scb.rearrange("n (c t) -> n c t", t=128)
                        r.op(V, lambda E, in_a=in_a, tb_a=tb_a, sca3=sca3: E.tensor_tensor(
                            sca3, in_a, tb_a, ALU.mult))
                        r.op(V, lambda E, in_b=in_b, tb_b=tb_b, scb3=scb3: E.tensor_tensor(
                            scb3, in_b, tb_b, ALU.mult))
                        dst = dst_sel(lo).rearrange("n (c t) -> n c t", t=128)
                        r.op(V, lambda E, dst=dst, sca3=sca3, scb3=scb3, alu=alu: E.tensor_tensor(
                            dst, sca3, scb3, alu), inc="v")

                cplx_combine(lambda lo: cum[:, 0, lo:lo + S // 2], bu[:, 0, :], bu[:, 1, :],
                             tm, 1, 0, ALU.add)
                cplx_combine(lambda lo: bu[:, 0, lo:lo + S // 2], bu[:, 0, :], bu[:, 1, :],
                             tm, 0, 1, ALU.subtract)
                r.op(V, lambda E: E.tensor_copy(bu[:, 1, :], cum[:, 0, :]), inc="v")

                r.op(V, lambda E: E.tensor_tensor_scan(cum[:, 0, :], d0s, bu[:, 0, :], 0.0,
                                                       ALU.mult, ALU.add), inc="v")
                r.op(V, lambda E: E.tensor_tensor_scan(cum[:, 1, :], d0s, bu[:, 1, :], 0.0,
                                                       ALU.mult, ALU.add), inc="v")
                cum3r = cum[:, 0, :].rearrange("n (c t) -> n c t", t=128)
                cum3i = cum[:, 1, :].rearrange("n (c t) -> n c t", t=128)
                r.op(V, lambda E: E.tensor_copy(ee[:, 0, :], cum3r[:, :, 127]))
                r.op(V, lambda E: E.tensor_copy(ee[:, 1, :], cum3i[:, :, 127]), inc="v")
                r.op(V, lambda E: E.memset(ccol[:, 0:2], 0.0), inc="v")
                for p in range(1, NCH_S):
                    ccr = ccol[:, 0:1]
                    cci = ccol[:, 1:2]
                    ctr = ccol[:, 2:3]
                    cti = ccol[:, 3:4]
                    r.op(V, lambda E, p=p: E.tensor_add(ccol[:, 2:4], ccol[:, 0:2], ee[:, :, p - 1]))
                    r.op(V, lambda E: E.tensor_mul(ccr, ctr, a128c[:, 0:1]))
                    r.op(V, lambda E: E.tensor_mul(wcol[:, 0:1], cti, a128c[:, 1:2]))
                    r.op(V, lambda E: E.tensor_sub(ccr, ccr, wcol[:, 0:1]))
                    r.op(V, lambda E: E.tensor_mul(cci, ctr, a128c[:, 1:2]))
                    r.op(V, lambda E: E.tensor_mul(wcol[:, 1:2], cti, a128c[:, 0:1]))
                    r.op(V, lambda E: E.tensor_add(cci, cci, wcol[:, 1:2]))
                    r.op(V, lambda E, p=p: E.tensor_scalar_add(
                        cum[:, 0, p * 128:(p + 1) * 128], cum[:, 0, p * 128:(p + 1) * 128], ccr))
                    r.op(V, lambda E, p=p: E.tensor_scalar_add(
                        cum[:, 1, p * 128:(p + 1) * 128], cum[:, 1, p * 128:(p + 1) * 128], cci),
                        inc="v")
                cplx_combine(lambda lo: xs_bf[:, 0, lo:lo + S // 2], cum[:, 0, :], cum[:, 1, :],
                             tp, 0, 1, ALU.subtract)
                cplx_combine(lambda lo: xs_bf[:, 1, lo:lo + S // 2], cum[:, 0, :], cum[:, 1, :],
                             tp, 1, 0, ALU.add)

                x13 = x1.rearrange("(c p) t -> c p t", p=128)
                OWN0 = TOK
                r.wait("tensor", "s")
                for qg in range(4):
                    for c in range(NCH):
                        if qg * NCH + c >= 2:
                            r.wait("sync", "v")
                        r.op("sync", lambda E, qg=qg, c=c: E.dma_start(
                            out=u32b[:, (qg * NCH + c) % 2, :],
                            in_=u3[c, :, OWN0 + qg * 512:OWN0 + (qg + 1) * 512]), inc="qs", k=16)
                        r.wait("vector", "qs")
                        r.wait("tensor", "v")
                        if qg > 0 and c == 0:
                            r.wait("vector", "so")
                        r.op("tensor", lambda E, qg=qg, c=c: E.matmul(
                            ps_y[c % 2], cpack_re[:, c, :],
                            xs_bf[:, 0, OWN0 + qg * 512:OWN0 + (qg + 1) * 512],
                            start=True, stop=False))
                        r.op("tensor", lambda E, qg=qg, c=c: E.matmul(
                            ps_y[c % 2], cpack_im[:, c, :],
                            xs_bf[:, 1, OWN0 + qg * 512:OWN0 + (qg + 1) * 512],
                            start=False, stop=True), inc="t")
                        r.wait(V, "t")
                        r.op(V, lambda E, qg=qg, c=c: E.scalar_tensor_tensor(
                            pre[:, c, :], u32b[:, (qg * NCH + c) % 2, :], d1col[:, c:c + 1],
                            ps_y[c % 2], ALU.mult, ALU.add), inc="v")
                    r.wait("scalar", "v")
                    _ln_emit(r, pre, NCH, sqscr, ps_sum, ps_sumsq, ones_col, ones_row,
                             rowscr, abc, bbc, ps_bc, 512)
                    r.wait("sync", "v")
                    for c in range(NCH):
                        r.op("sync", lambda E, qg=qg, c=c: E.dma_start(
                            out=x13[c, :, qg * 512:(qg + 1) * 512], in_=pre[:, c, :]), inc="so", k=16)
                r.wait("sync", "so")

        def _phase_B():
            with ExitStack() as ctx:
                sb = lambda n, s, d=F32: ctx.enter_context(nc.sbuf_tensor("B_" + n, s, d))
                xbf = sb("xbf", [128, NCH, TOK], BF16)
                w3sl4 = xbf[:].rearrange("p c t -> p (c t)").rearrange(
                    "p (e fc dq) -> p e fc dq", e=8, fc=8)
                x32 = sb("x32", [128, 4, 512])
                rwt_t = sb("rwt_t", [128, NCH, 8])
                pack8 = sb("pack8", [64, TOK])
                lg_fm = pack8[0:8, :]
                wrow_fm = pack8[32:40, :]
                lg_tm = sb("lg_tm", [128, NCH, 8])
                mx_t = sb("mx_t", [128, 8])
                idx_t = sb("idx_t", [128, 8], U32)
                idxf = sb("idxf", [128, 2])
                wcol2 = sb("wcol2", [128, 4])
                iota8 = sb("iota8", [128, 8])
                wmap = sb("wmap", [128, 8])
                wrow_e = sb("wrow_e", [1, TOK])
                wbc = sb("wbc", [128, TOK])
                w12s = sb("w12s", [128, 2, NCH, 512], BF16)
                pre = w12s[:].rearrange("p a c t -> p (a c t)").bitcast(F32).rearrange(
                    "p (c t) -> p c t", c=NCH)
                gsil = sb("gsil", [128, 512])
                vw = sb("vw", [128, 512])
                hprime = sb("hprime", [128, NFB, TOK], BF16)
                hb3 = hprime[:].rearrange("p a t -> p (a t)").rearrange(
                    "p (b fc t) -> p b fc t", b=4, fc=8)
                sqscr = sb("sqscr", [128, 2, 512])
                rowscr = sb("rowscr", [1, 2048])
                abc = sb("abc", [128, 512])
                bbc = sb("bbc", [128, 512])
                idv = sb("idv", [128, 128])
                ident = sb("ident", [128, 128])
                ones_col = sb("ones_col", [128, 1])
                ones_row = sb("ones_row", [1, 128])

                bank = lambda i: PS[:, 512 * i:512 * (i + 1)]
                ps_lg = PS[0:8, 0:512]
                ps_tr = PS[:, 512:1024]

                V, Sc = "vector", "scalar"

                r.op("sync", lambda E: E.dma_start(out=rwt_t[:], in_=rwt3.rearrange("c p e -> p c e")),
                     inc="qs", k=16)
                r.op("gpsimd", lambda E: E.iota(idv[:], pattern=[[-1, 128]], channel_multiplier=1,
                                                allow_small_or_imprecise_dtypes=True), inc="qg", k=1)
                r.op("gpsimd", lambda E: E.iota(iota8[:], pattern=[[1, 8]], channel_multiplier=0,
                                                allow_small_or_imprecise_dtypes=True), inc="qg", k=1)
                r.wait(V, "qg")
                r.op(V, lambda E: E.tensor_single_scalar(ident[:], idv[:], 0.0, ALU.is_equal))
                r.op(V, lambda E: E.memset(ones_col[:], 1.0))
                r.op(V, lambda E: E.memset(ones_row[:], 1.0))

                r.wait("tensor", "qs")
                ld_done = {}
                for tg in range(NT):
                    for c in range(NCH):
                        i = tg * NCH + c
                        if i >= 2:
                            r.wait("sync", "t", ld_done[i - 2][0])
                            r.wait("sync", "v", ld_done[i - 2][1])
                        r.op("sync", lambda E, tg=tg, c=c, i=i: E.dma_start(
                            out=x32[:, i % 4, :], in_=x13[c, :, tg * 512:(tg + 1) * 512]), inc="qs", k=16)
                        r.wait("tensor", "qs")
                        r.op("tensor", lambda E, tg=tg, c=c, i=i: E.matmul(
                            ps_lg, rwt_t[:, c, :], x32[:, i % 4, :],
                            start=(c == 0), stop=(c == NCH - 1)), inc="t")
                        r.wait(V, "qs")
                        r.op(V, lambda E, tg=tg, c=c, i=i: E.tensor_copy(
                            xbf[:, c, tg * 512:(tg + 1) * 512], x32[:, i % 4, :]))
                        ld_done[i] = (r.cnt["t"], r.cnt["v"])
                    r.wait(V, "t")
                    r.op(V, lambda E, tg=tg: E.tensor_copy(lg_fm[:, tg * 512:(tg + 1) * 512], ps_lg))
                    r.wait("tensor", "v")

                ident8 = ident[0:8, 0:8]
                for i in range(NCH):
                    r.wait("tensor", "v")
                    r.op("tensor", lambda E, i=i: E.transpose(
                        ps_tr[0:128, 0:8], lg_fm[:, i * 128:(i + 1) * 128], ident8), inc="t")
                    r.wait(V, "t")
                    r.op(V, lambda E, i=i: E.tensor_copy(lg_tm[:, i, :], ps_tr[0:128, 0:8]))
                    r.op(V, lambda E, i=i: E.max(mx_t[:], lg_tm[:, i, :]))
                    r.op(V, lambda E, i=i: E.max_index(idx_t[:], mx_t[:], lg_tm[:, i, :]))
                    r.op(V, lambda E: E.tensor_copy(idxf[:], idx_t[:, 0:2]))
                    r.op(V, lambda E: E.tensor_sub(wcol2[:, 0:1], mx_t[:, 1:2], mx_t[:, 0:1]))
                    r.wait(Sc, "v")
                    r.op(Sc, lambda E: E.activation(wcol2[:, 1:2], wcol2[:, 0:1], AF.Exp))
                    r.wait(V, "s")
                    r.op(V, lambda E: E.tensor_scalar_add(wcol2[:, 2:3], wcol2[:, 1:2], 1.0))
                    r.op(V, lambda E: E.reciprocal(wcol2[:, 2:3], wcol2[:, 2:3]))
                    r.wait(V, "v")
                    r.op(V, lambda E: E.tensor_mul(wcol2[:, 3:4], wcol2[:, 1:2], wcol2[:, 2:3]))
                    r.op(V, lambda E: E.tensor_scalar(wmap[:], iota8[:], idxf[:, 0:1], wcol2[:, 2:3],
                                                      ALU.is_equal, ALU.mult))
                    r.op(V, lambda E: E.tensor_scalar(gsil[:, 0:8], iota8[:], idxf[:, 1:2], wcol2[:, 3:4],
                                                      ALU.is_equal, ALU.mult))
                    r.op(V, lambda E: E.tensor_add(wmap[:], wmap[:], gsil[:, 0:8]))
                    r.wait("tensor", "v")
                    r.op("tensor", lambda E, i=i: E.transpose(
                        ps_tr[0:8, 0:128], wmap[:], ident[:]), inc="t")
                    r.wait(V, "t")
                    r.op(V, lambda E, i=i: E.tensor_copy(wrow_fm[:, i * 128:(i + 1) * 128],
                                                         ps_tr[0:8, 0:128]))

                for e in range(8):
                    r.wait("gpsimd", "v")
                    r.op("gpsimd", lambda E, e=e: E.dma_start(out=wrow_e[:], in_=pack8[32 + e:33 + e, :]),
                         inc="qg", k=16)
                    r.wait("tensor", "qg")
                    for tg in range(NT):
                        r.op("tensor", lambda E, tg=tg: E.matmul(
                            bank(2 + tg % 2), ones_row[:], wrow_e[:, tg * 512:(tg + 1) * 512],
                            start=True, stop=True), inc="t")
                        r.wait(V, "t")
                        r.op(V, lambda E, tg=tg: E.tensor_copy(wbc[:, tg * 512:(tg + 1) * 512],
                                                               bank(2 + tg % 2)))
                        r.wait("tensor", "v")
                    for fh in range(2):
                        r.wait("gpsimd", "t")
                        for c in range(NCH):
                            r.op("gpsimd", lambda E, e=e, fh=fh, c=c: E.dma_start(
                                out=w12s[:, 0, c, :], in_=w1[e, :, :].rearrange(
                                    "(c p) f -> c p f", p=128)[c, :, fh * 512:(fh + 1) * 512]),
                                inc="qg", k=16)
                            r.op("gpsimd", lambda E, e=e, fh=fh, c=c: E.dma_start(
                                out=w12s[:, 1, c, :], in_=w2[e, :, :].rearrange(
                                    "(c p) f -> c p f", p=128)[c, :, fh * 512:(fh + 1) * 512]),
                                inc="qg", k=16)
                        r.wait("tensor", "qg")
                        for fb in range(4):
                            fblk = fh * 4 + fb
                            for c in range(NCH):
                                for tg in range(NT):
                                    r.op("tensor", lambda E, c=c, tg=tg, fb=fb: E.matmul(
                                        bank(tg), w12s[:, 0, c, fb * 128:(fb + 1) * 128],
                                        xbf[:, c, tg * 512:(tg + 1) * 512],
                                        start=(c == 0), stop=(c == NCH - 1)))
                                for tg in range(NT):
                                    r.op("tensor", lambda E, c=c, tg=tg, fb=fb: E.matmul(
                                        bank(4 + tg), w12s[:, 1, c, fb * 128:(fb + 1) * 128],
                                        xbf[:, c, tg * 512:(tg + 1) * 512],
                                        start=(c == 0), stop=(c == NCH - 1)),
                                        inc=("t" if c == NCH - 1 and tg == NT - 1 else None))
                            for tg in range(NT):
                                r.wait(Sc, "t")
                                r.wait(Sc, "v")
                                r.op(Sc, lambda E, tg=tg: E.activation(gsil[:], bank(tg), AF.Silu))
                                r.wait(V, "s")
                                r.op(V, lambda E, tg=tg: E.tensor_tensor(
                                    vw[:], bank(4 + tg), wbc[:, tg * 512:(tg + 1) * 512], ALU.mult))
                                r.op(V, lambda E, tg=tg, fblk=fblk: E.tensor_tensor(
                                    hprime[:, fblk, tg * 512:(tg + 1) * 512], gsil[:], vw[:], ALU.mult))
                            r.wait("tensor", "v")
                    r.wait("sync", "v")
                    for fblk in range(NFB):
                        r.op("sync", lambda E, e=e, fblk=fblk: E.dma_start(
                            out=hsp[e, fblk * 128:(fblk + 1) * 128, :], in_=hprime[:, fblk, :]),
                            inc="so", k=16)
                    r.wait("gpsimd", "so")
                    r.wait("vector", "so")

                t_b3 = {}
                r.wait("gpsimd", "so")
                for dq in range(4):
                    r.wait("gpsimd", "t")
                    for e in range(8):
                        for fc in range(8):
                            r.op("gpsimd", lambda E, dq=dq, e=e, fc=fc: E.dma_start(
                                out=w3sl4[:, e, fc, :],
                                in_=w3[e, :, :].rearrange("(fc p) d -> fc p d", p=128)[
                                    fc, :, dq * 512:(dq + 1) * 512]), inc="qg", k=16)
                    r.wait("tensor", "qg")
                    for tg in range(NT):
                        for e in range(8):
                            i = tg * 8 + e
                            if i >= 2:
                                r.wait("gpsimd", "t", t_b3[(dq, i - 2)])
                            for fc in range(8):
                                r.op("gpsimd", lambda E, e=e, fc=fc, tg=tg, i=i: E.dma_start(
                                    out=hb3[:, i % 4, fc, :],
                                    in_=hsp[e, fc * 128:(fc + 1) * 128, tg * 512:(tg + 1) * 512]),
                                    inc="qg", k=16)
                            r.wait("tensor", "qg")
                            if e == 0:
                                r.wait("tensor", "v")
                            for dblk in range(4):
                                for fc in range(8):
                                    r.op("tensor", lambda E, e=e, fc=fc, dblk=dblk, i=i: E.matmul(
                                        bank(dblk), w3sl4[:, e, fc, dblk * 128:(dblk + 1) * 128],
                                        hb3[:, i % 4, fc, :],
                                        start=(e == 0 and fc == 0), stop=(e == 7 and fc == 7)),
                                        inc=("t" if fc == 7 and dblk == 3 else None))
                            t_b3[(dq, i)] = r.cnt["t"]
                        for dblk in range(4):
                            c = dq * 4 + dblk
                            r.op("sync", lambda E, c=c, tg=tg, dblk=dblk: E.dma_start(
                                out=x32[:, dblk % 4, :], in_=x13[c, :, tg * 512:(tg + 1) * 512]),
                                inc="qs", k=16)
                            r.wait(V, "qs")
                            r.wait(V, "t")
                            r.op(V, lambda E, c=c, tg=tg, dblk=dblk: E.scalar_tensor_tensor(
                                gsil[:], x32[:, dblk % 4, :], 1.0, bank(dblk), ALU.mult, ALU.add))
                            r.wait("sync", "v")
                            r.op("sync", lambda E, c=c, tg=tg: E.dma_start(
                                out=x2p3[c, :, tg * 512:(tg + 1) * 512], in_=gsil[:]), inc="so", k=16)
                            r.wait(V, "so")

                for tg in range(NT):
                    r.wait("sync", "so")
                    r.wait("sync", "v")
                    for c in range(NCH):
                        r.op("sync", lambda E, c=c, tg=tg: E.dma_start(
                            out=pre[:, c, :], in_=x2p3[c, :, tg * 512:(tg + 1) * 512]), inc="qs", k=16)
                    r.wait("vector", "qs")
                    r.wait("scalar", "qs")
                    r.wait("tensor", "qs")
                    _ln_emit(r, pre, NCH, sqscr, PS[0:1, 2048:2560], PS[0:1, 2560:3072],
                             ones_col, ones_row, rowscr, abc, bbc,
                             [PS[:, 3072:3584], PS[:, 3584:4096]], 512)
                    r.wait("sync", "v")
                    for c in range(NCH):
                        r.op("sync", lambda E, c=c, tg=tg: E.dma_start(
                            out=xkv3[c, :, TOK + tg * 512:TOK + (tg + 1) * 512], in_=pre[:, c, :]), inc="so", k=16)
                r.wait("sync", "so")

        def _phase_C():
            with ExitStack() as ctx:
                sb = lambda n, s, d=F32: ctx.enter_context(nc.sbuf_tensor("C_" + n, s, d))
                x32k = sb("x32k", [128, 2, 512])
                xbf_s = sb("xbf_s", [128, 2, 512], BF16)
                qc_bf = sb("qc_bf", [128, 8, TOK], BF16)
                pre = qc_bf[:].rearrange("p a t -> p (a t)").bitcast(F32).rearrange(
                    "p (c t) -> p c t", c=NCH)
                kvc_bf = sb("kvc_bf", [128, 4, S], BF16)
                wo_s = kvc_bf[:].rearrange("p a t -> p (a t)").rearrange(
                    "p (c t) -> p c t", c=NCH)
                kro_raw = sb("kro_raw", [64, S], BF16)
                krope2 = sb("krope2", [64, S], BF16)
                cosk2 = sb("cosk2", [64, S], BF16)
                sink2 = sb("sink2", [64, S], BF16)
                wkr_bf = sb("wkr_bf", [128, NCH, 64], BF16)
                wkvd_s = sb("wkvd_s", [128, 2, 512], BF16)
                wqd_s = sb("wqd_s", [128, 2, 1024], BF16)
                qnw_t = sb("qnw_t", [128, 8])
                kvnw_t = sb("kvnw_t", [128, 4])
                prodscr = sb("prodscr", [128, 8, 512])
                prodflat = prodscr[:].rearrange("p a t -> p (a t)")
                ftab = prodflat[0:32, 0:4096]
                rowscr_f = prodflat[0:1, 0:2048]
                sqscr = sb("sqscr", [128, 2, 512])
                trig_s1 = sb("trig_s1", [32, 1024])
                trig_s2 = sqscr[:].rearrange("p a t -> p (a t)")[0:32, 0:1024]
                trig_i = sb("trig_i", [32, 1024], I32)
                rows_rms = sb("rows_rms", [1, 1024])
                rbref = sb("rbref", [128, 512])
                wup_h = sb("wup_h", [128, 2, 8, 128], BF16)
                wqr_h = sb("wqr_h", [128, 2, 8, 64], BF16)
                wkvu_h = sb("wkvu_h", [128, 2, 4, 256], BF16)
                knope = sb("knope", [128, S], BF16)
                vext = sb("vext", [128, NKT, 130], BF16)
                qnope = sb("qnope", [128, TOK], BF16)
                qpe = sb("qpe", [64, TOK], BF16)
                qpe_f = prodflat[0:64, 2048:3072].rearrange("p (a t) -> p a t", a=2)
                ptile = sb("ptile", [128, 2048], BF16)
                att_t = sb("att_t", [128, 1, 512], BF16)
                att_qg = sb("att_qg", [128, NCH, 512], BF16)
                m01 = sb("m01", [128, 4, 512], BF16)
                ivt = None
                biasr = rows_rms[0:1, 0:128]
                bias_h = sb("bias_h", [128, 128])
                maskrow = sb("maskrow_t", [1, 128])
                pos_sl = rows_rms[0:1, 512:1024]
                invf = sb("invf", [1, 32])
                r64 = sb("r64", [64, 64])
                r64b = sb("r64b", [64, 64], BF16)
                d64 = sb("d64", [64, 64])
                pcol = sb("pcol", [64, 4])
                mxq = sb("mxq", [128, 4])
                msc = sb("msc", [1, 8])
                ident = sb("identc", [128, 128])
                idv = ident
                ones_col = sb("ones_colc", [128, 1])
                ones_row = sb("ones_rowc", [1, 128])
                eps11 = sb("eps11", [1, 1])
                abc = prodflat[:, 3072:3584]
                bbc = prodflat[:, 3584:4096]

                bank = lambda i: PS[:, 512 * i:512 * (i + 1)]

                V, Sc = "vector", "scalar"

                r.op("sync", lambda E: E.dma_start(out=maskrow[:], in_=maskrow_d), inc="qs", k=16)
                r.op("sync", lambda E: E.dma_start(out=qnw_t[:], in_=qnw), inc="qs", k=16)
                r.op("sync", lambda E: E.dma_start(out=kvnw_t[:], in_=kvnw), inc="qs", k=16)
                r.op("gpsimd", lambda E: E.dma_start(
                    out=wkr_bf[:], in_=wkrT.rearrange("(c p) f -> p c f", p=128)), inc="qg", k=16)
                r.op("gpsimd", lambda E: E.iota(idv[:], pattern=[[-1, 128]], channel_multiplier=1,
                                                allow_small_or_imprecise_dtypes=True), inc="qg", k=1)

                xkvbf = nc.dram_tensor("xkvbf", [D, S], BF16, kind="Internal").ap()
                xkvbf3 = xkvbf.rearrange("(c p) t -> c p t", p=128)
                p0_st = {}
                for i0 in range(NCH * 8):
                    c0, tg0 = i0 // 8, i0 % 8
                    if i0 >= 2:
                        r.wait("gpsimd", "qs", p0_st[i0 - 2])
                    r.op("gpsimd", lambda E, c0=c0, tg0=tg0, i0=i0: E.dma_start(
                        out=xbf_s[:, i0 % 2, :], in_=xkv3[c0, :, tg0 * 512:(tg0 + 1) * 512]),
                        inc="qg", k=16)
                    r.wait("sync", "qg")
                    r.op("sync", lambda E, c0=c0, tg0=tg0, i0=i0: E.dma_start(
                        out=xkvbf3[c0, :, tg0 * 512:(tg0 + 1) * 512], in_=xbf_s[:, i0 % 2, :]),
                        inc="qs", k=16)
                    p0_st[i0] = r.cnt["qs"]
                p0_done = r.cnt["qs"]

                r.op("gpsimd", lambda E: E.iota(sqscr[:, 0, :], pattern=[[1, 512]], channel_multiplier=-1,
                                                allow_small_or_imprecise_dtypes=True), inc="qg", k=1)
                r.op("gpsimd", lambda E: E.iota(d64[:], pattern=[[-1, 64]], channel_multiplier=1,
                                                allow_small_or_imprecise_dtypes=True), inc="qg", k=1)
                r.op("gpsimd", lambda E: E.iota(pcol[:, 0:1], pattern=[[0, 1]], channel_multiplier=1,
                                                allow_small_or_imprecise_dtypes=True), inc="qg", k=1)
                r.op("gpsimd", lambda E: E.iota(invf[:], pattern=[[1, 32]], channel_multiplier=0,
                                                allow_small_or_imprecise_dtypes=True), inc="qg", k=1)
                r.wait(V, "qg")
                r.wait(V, "qs")
                r.op(V, lambda E: E.tensor_single_scalar(ident[:], idv[:], 0.0, ALU.is_equal))
                r.op(V, lambda E: E.memset(ones_col[:], 1.0))
                r.op(V, lambda E: E.memset(ones_row[:], 1.0))
                r.op(V, lambda E: E.memset(eps11[:], 1e-6))
                for dd in range(4):
                    r.op(V, lambda E, dd=dd: E.tensor_single_scalar(
                        m01[:, dd, :], sqscr[:, 0, :], dd * 128 - 0.5, ALU.is_ge))
                r.op(V, lambda E: E.tensor_single_scalar(pcol[:, 1:2], pcol[:, 0:1], 31.5, ALU.is_ge))
                r.op(V, lambda E: E.tensor_single_scalar(pcol[:, 2:3], pcol[:, 1:2], 0.5, ALU.is_lt))
                r.op(V, lambda E: E.tensor_single_scalar(r64[:], d64[:], 32.0, ALU.is_equal))
                r.op(V, lambda E: E.tensor_tensor(r64[:], r64[:],
                                                  pcol[:, 1:2].broadcast_to([64, 64]), ALU.mult))
                r.op(V, lambda E: E.tensor_single_scalar(d64[:], d64[:], -32.0, ALU.is_equal))
                r.op(V, lambda E: E.tensor_tensor(d64[:], d64[:],
                                                  pcol[:, 2:3].broadcast_to([64, 64]), ALU.mult))
                r.op(V, lambda E: E.tensor_sub(r64[:], d64[:], r64[:]))
                r.op(V, lambda E: E.tensor_copy(r64b[:], r64[:]))
                r.op(V, lambda E: E.tensor_scalar_mul(invf[:], invf[:], -float(np.log(10000.0)) / 32.0))
                r.wait(Sc, "v")
                r.op(Sc, lambda E: E.activation(invf[:], invf[:], AF.Exp))
                r.wait(V, "s")
                r.op(V, lambda E: E.tensor_scalar_mul(qnw_t[:], qnw_t[:], SCALE))

                dsc = nc.dram_tensor("dsc", [1, 512], F32, kind="Internal").ap()
                NTK = S // 512
                p1_ld = {}
                r.wait("sync", "qs", p0_done)
                for tgk in range(NTK):
                    for c in range(NCH):
                        i = tgk * NCH + c
                        if i >= 2:
                            r.wait("gpsimd", "t", p1_ld[i - 2])
                            r.wait("sync", "t", p1_ld[i - 2])
                        r.op("sync", lambda E, c=c, tgk=tgk, i=i: E.dma_start(
                            out=xbf_s[:, i % 2, :], in_=xkvbf3[c, :, tgk * 512:(tgk + 1) * 512]),
                            inc="qs", k=16)
                        r.op("gpsimd", lambda E, c=c, i=i: E.dma_start(
                            out=wkvd_s[:, i % 2, :], in_=wkvdT.rearrange(
                                "(c p) f -> c p f", p=128)[c, :, :]), inc="qg", k=16)
                        r.wait("tensor", "qg")
                        r.wait("tensor", "qs")
                        if c == 0:
                            r.wait("tensor", "v")
                        for rb in range(4):
                            r.op("tensor", lambda E, c=c, rb=rb, i=i: E.matmul(
                                bank(rb), wkvd_s[:, i % 2, rb * 128:(rb + 1) * 128],
                                xbf_s[:, i % 2, :], start=(c == 0), stop=(c == NCH - 1)))
                        r.op("tensor", lambda E, c=c, i=i: E.matmul(
                            bank(4)[0:64, :], wkr_bf[:, c, :], xbf_s[:, i % 2, :],
                            start=(c == 0), stop=(c == NCH - 1)), inc="t")
                        p1_ld[i] = r.cnt["t"]
                    r.wait(V, "t")
                    for rb in range(4):
                        r.op(V, lambda E, rb=rb: E.tensor_copy(prodscr[:, rb, :], bank(rb)))
                    r.op(V, lambda E, tgk=tgk: E.tensor_copy(
                        kro_raw[:, tgk * 512:(tgk + 1) * 512], bank(4)[0:64, :]), inc="v")
                    r.wait(Sc, "v")
                    for rb in range(4):
                        if rb >= 2:
                            r.wait(Sc, "t")
                        r.op(Sc, lambda E, rb=rb: E.activation(sqscr[:, rb % 2, :], prodscr[:, rb, :],
                                                               AF.Square), inc="s")
                        r.wait("tensor", "s")
                        r.op("tensor", lambda E, rb=rb: E.matmul(
                            bank(6)[0:1, :], ones_col[:], sqscr[:, rb % 2, :],
                            start=(rb == 0), stop=(rb == 3)), inc="t")
                    r.wait(Sc, "t")
                    r.op(Sc, lambda E: E.activation(rows_rms[:, 0:512], bank(6)[0:1, :], AF.Ln,
                                                    scale=1.0 / 512.0, bias=eps11[:]), inc="s")
                    r.op(Sc, lambda E: E.activation(rows_rms[:, 0:512], rows_rms[:, 0:512], AF.Exp,
                                                    scale=-0.5), inc="s")
                    r.wait("tensor", "s")
                    r.op("tensor", lambda E: E.matmul(bank(7), ones_row[:], rows_rms[:, 0:512],
                                                      start=True, stop=True), inc="t")
                    r.wait(V, "t")
                    r.op(V, lambda E: E.tensor_copy(rbref[:], bank(7)))
                    for rb in range(4):
                        r.op(V, lambda E, rb=rb: E.tensor_mul(prodscr[:, rb, :], prodscr[:, rb, :],
                                                              rbref[:]))
                        r.op(V, lambda E, rb=rb, tgk=tgk: E.tensor_scalar_mul(
                            kvc_bf[:, rb, tgk * 512:(tgk + 1) * 512], prodscr[:, rb, :],
                            kvnw_t[:, rb:rb + 1]), inc="v")

                NTQ = TOK // 512
                p2_ld = {}
                for tgq in range(NTQ):
                    for c in range(NCH):
                        i = tgq * NCH + c
                        if i >= 2:
                            r.wait("gpsimd", "t", p2_ld[i - 2])
                            r.wait("sync", "t", p2_ld[i - 2])
                        else:
                            r.wait("sync", "t", p1_ld[NTK * NCH - 2 + i])
                        r.op("sync", lambda E, c=c, tgq=tgq, i=i: E.dma_start(
                            out=xbf_s[:, i % 2, :], in_=xkvbf3[c, :, TOK + tgq * 512:TOK + (tgq + 1) * 512]),
                            inc="qs", k=16)
                        r.op("gpsimd", lambda E, c=c, i=i: E.dma_start(
                            out=wqd_s[:, i % 2, :], in_=wqdT.rearrange(
                                "(c p) f -> c p f", p=128)[c, :, :]), inc="qg", k=16)
                        r.wait("tensor", "qg")
                        r.wait("tensor", "qs")
                        if c == 0:
                            r.wait("tensor", "v")
                        for rb in range(8):
                            r.op("tensor", lambda E, c=c, rb=rb, i=i: E.matmul(
                                bank(rb), wqd_s[:, i % 2, rb * 128:(rb + 1) * 128],
                                xbf_s[:, i % 2, :], start=(c == 0), stop=(c == NCH - 1)),
                                inc=("t" if rb == 7 else None))
                        p2_ld[i] = r.cnt["t"]
                    r.wait(V, "t")
                    for rb in range(8):
                        r.op(V, lambda E, rb=rb: E.tensor_copy(prodscr[:, rb, :], bank(rb)), inc="v")
                    r.wait(Sc, "v")
                    for rb in range(8):
                        if rb >= 2:
                            r.wait(Sc, "t")
                        r.op(Sc, lambda E, rb=rb: E.activation(sqscr[:, rb % 2, :], prodscr[:, rb, :],
                                                               AF.Square), inc="s")
                        r.wait("tensor", "s")
                        r.wait("tensor", "v")
                        r.op("tensor", lambda E, rb=rb: E.matmul(
                            bank(0)[0:1, :], ones_col[:], sqscr[:, rb % 2, :],
                            start=(rb == 0), stop=(rb == 7)), inc="t")
                    r.wait(Sc, "t")
                    r.op(Sc, lambda E: E.activation(rows_rms[:, 0:512], bank(0)[0:1, :], AF.Ln,
                                                    scale=1.0 / 1024.0, bias=eps11[:]), inc="s")
                    r.op(Sc, lambda E: E.activation(rows_rms[:, 0:512], rows_rms[:, 0:512], AF.Exp,
                                                    scale=-0.5), inc="s")
                    r.wait("tensor", "s")
                    r.op("tensor", lambda E: E.matmul(bank(1), ones_row[:], rows_rms[:, 0:512],
                                                      start=True, stop=True), inc="t")
                    r.wait(V, "t")
                    r.op(V, lambda E: E.tensor_copy(rbref[:], bank(1)))
                    for rb in range(8):
                        r.op(V, lambda E, rb=rb: E.tensor_mul(prodscr[:, rb, :], prodscr[:, rb, :],
                                                              rbref[:]))
                        r.op(V, lambda E, rb=rb, tgq=tgq: E.tensor_scalar_mul(
                            qc_bf[:, rb, tgq * 512:(tgq + 1) * 512], prodscr[:, rb, :],
                            qnw_t[:, rb:rb + 1]), inc="v")

                r.wait("tensor", "v")
                for g in range(8):
                    r.op("sync", lambda E, g=g: E.dma_start(
                        out=pos_sl[:], in_=pos[:, g * 512:(g + 1) * 512]), inc="qs", k=16)
                    r.wait("tensor", "qs")
                    r.op("tensor", lambda E, g=g: E.matmul(
                        bank(2 + g % 2)[0:32, :], invf[:], pos_sl[:], start=True, stop=True), inc="t")
                    r.wait(V, "t")
                    r.op(V, lambda E, g=g: E.tensor_copy(ftab[:, g * 512:(g + 1) * 512],
                                                         bank(2 + g % 2)[0:32, :]), inc="v")
                    r.wait("sync", "v")
                    r.wait("tensor", "v")
                for qtr, dst in ((0.0, sink2), (0.25, cosk2)):
                    for q4 in range(4):
                        sl = slice(q4 * 1024, (q4 + 1) * 1024)
                        r.wait(V, "s")
                        r.op(V, lambda E, sl=sl: E.tensor_scalar_mul(trig_s1[:], ftab[:, sl], inv2pi))
                        if qtr:
                            r.op(V, lambda E: E.tensor_scalar_add(trig_s1[:], trig_s1[:], 0.25))
                        r.op(V, lambda E: E.tensor_copy(trig_i[:], trig_s1[:]))
                        r.op(V, lambda E: E.tensor_copy(trig_s2[:], trig_i[:]))
                        r.op(V, lambda E: E.tensor_sub(trig_s1[:], trig_s1[:], trig_s2[:]))
                        r.op(V, lambda E: E.tensor_scalar_mul(trig_s1[:], trig_s1[:], twopi), inc="v")
                        r.wait(Sc, "v")
                        r.op(Sc, lambda E, dst=dst, sl=sl: E.activation(
                            dst[0:32, sl], trig_s1[:], AF.Sin), inc="s")
                r.wait("gpsimd", "s")
                r.op("gpsimd", lambda E: E.dma_start(out=sink2[32:64, :], in_=sink2[0:32, :]),
                     inc="qg", k=16)
                r.op("gpsimd", lambda E: E.dma_start(out=cosk2[32:64, :], in_=cosk2[0:32, :]),
                     inc="qg", k=16)

                r.wait("tensor", "qg")
                r.wait(V, "qg")
                r.wait("tensor", "v")
                for g in range(8):
                    sl = slice(g * 512, (g + 1) * 512)
                    r.op("tensor", lambda E, sl=sl: E.matmul(
                        bank(2)[0:64, :], r64b[:], kro_raw[:, sl], start=True, stop=True), inc="t")
                    r.wait(V, "t")
                    r.op(V, lambda E, sl=sl: E.tensor_tensor(qpe_f[:, 0, :], kro_raw[:, sl],
                                                             cosk2[:, sl], ALU.mult))
                    r.op(V, lambda E, sl=sl: E.tensor_tensor(qpe_f[:, 1, :], bank(2)[0:64, :],
                                                             sink2[:, sl], ALU.mult))
                    r.op(V, lambda E, sl=sl: E.tensor_add(krope2[:, sl], qpe_f[:, 0, :],
                                                          qpe_f[:, 1, :]), inc="v")
                    r.wait("tensor", "v")

                wq3 = wquT.rearrange("(c p) f -> c p f", p=128)
                wqr3 = wqrT.rearrange("(c p) f -> c p f", p=128)
                wkvu3 = wkvuT.rearrange("(c p) f -> c p f", p=128)
                head_ms = {}
                for h in range(NH):
                    if h >= 1:
                        r.wait("gpsimd", "t", head_ms.get(h - 1, 0))
                    for c in range(8):
                        r.op("gpsimd", lambda E, h=h, c=c: E.dma_start(
                            out=wup_h[:, h % 2, c, :], in_=wq3[c, :, h * 128:(h + 1) * 128]),
                            inc="qg", k=16)
                        r.op("gpsimd", lambda E, h=h, c=c: E.dma_start(
                            out=wqr_h[:, h % 2, c, :], in_=wqr3[c, :, h * 64:(h + 1) * 64]),
                            inc="qg", k=16)
                    for c in range(4):
                        r.op("gpsimd", lambda E, h=h, c=c: E.dma_start(
                            out=wkvu_h[:, h % 2, c, :], in_=wkvu3[c, :, h * 256:(h + 1) * 256]),
                            inc="qg", k=16)
                    r.wait("tensor", "qg")
                    r.wait("tensor", "v")
                    r.wait("tensor", "so")
                    kn_cp = {}
                    for g in range(8):
                        if g >= 2:
                            r.wait("tensor", "v", kn_cp[g - 2])
                        for rc in range(4):
                            r.op("tensor", lambda E, h=h, g=g, rc=rc: E.matmul(
                                bank(g % 2), wkvu_h[:, h % 2, rc, 0:128],
                                kvc_bf[:, rc, g * 512:(g + 1) * 512],
                                start=(rc == 0), stop=(rc == 3)), inc=("t" if rc == 3 else None))
                        r.wait(V, "t")
                        r.op(V, lambda E, g=g: E.tensor_copy(knope[:, g * 512:(g + 1) * 512],
                                                             bank(g % 2)), inc="v")
                        kn_cp[g] = r.cnt["v"]
                    r.op(V, lambda E: E.memset(vext[:, :, 64:65], 1.0))
                    r.op(V, lambda E: E.memset(vext[:, :, 129:130], 1.0), inc="v")
                    vx_cp = {}
                    for kt in range(NKT):
                        if kt >= 2:
                            r.wait("tensor", "v", vx_cp[kt - 2])
                        for rc in range(4):
                            r.op("tensor", lambda E, h=h, kt=kt, rc=rc: E.matmul(
                                bank(2 + kt % 2)[:, 0:128], kvc_bf[:, rc, kt * 128:(kt + 1) * 128],
                                wkvu_h[:, h % 2, rc, 128:256],
                                start=(rc == 0), stop=(rc == 3)), inc=("t" if rc == 3 else None))
                        r.wait(V, "t")
                        r.op(V, lambda E, kt=kt: E.tensor_copy(vext[:, kt, 0:64],
                                                               bank(2 + kt % 2)[:, 0:64]))
                        r.op(V, lambda E, kt=kt: E.tensor_copy(vext[:, kt, 65:129],
                                                               bank(2 + kt % 2)[:, 64:128]), inc="v")
                        vx_cp[kt] = r.cnt["v"]
                    qn_cp = {}
                    for g in range(4):
                        if g >= 2:
                            r.wait("tensor", "v", qn_cp[g - 2])
                        for rc in range(8):
                            r.op("tensor", lambda E, h=h, g=g, rc=rc: E.matmul(
                                bank(4 + g % 2), wup_h[:, h % 2, rc, :],
                                qc_bf[:, rc, g * 512:(g + 1) * 512],
                                start=(rc == 0), stop=(rc == 7)), inc=("t" if rc == 7 else None))
                        r.wait(V, "t")
                        r.op(V, lambda E, g=g: E.tensor_copy(qnope[:, g * 512:(g + 1) * 512],
                                                             bank(4 + g % 2)), inc="v")
                        qn_cp[g] = r.cnt["v"]
                    for g in range(4):
                        for rc in range(8):
                            r.op("tensor", lambda E, h=h, g=g, rc=rc: E.matmul(
                                bank(6)[0:64, :], wqr_h[:, h % 2, rc, :],
                                qc_bf[:, rc, g * 512:(g + 1) * 512],
                                start=(rc == 0), stop=(rc == 7)), inc=("t" if rc == 7 else None))
                        r.wait(V, "t")
                        r.op(V, lambda E, g=g: E.tensor_copy(qpe_f[:, 0, :], bank(6)[0:64, :]), inc="v")
                        r.wait("tensor", "v")
                        r.op("tensor", lambda E, g=g: E.matmul(
                            bank(7)[0:64, :], r64[:], qpe_f[:, 0, :], start=True, stop=True), inc="t")
                        qsl = slice(TOK + g * 512, TOK + (g + 1) * 512)
                        r.wait(V, "t")
                        r.op(V, lambda E, g=g, qsl=qsl: E.tensor_tensor(
                            qpe_f[:, 1, :], qpe_f[:, 0, :], cosk2[:, qsl], ALU.mult))
                        r.op(V, lambda E, g=g, qsl=qsl: E.tensor_tensor(
                            qpe_f[:, 0, :], bank(7)[0:64, :], sink2[:, qsl], ALU.mult))
                        r.op(V, lambda E, g=g: E.tensor_add(
                            qpe[:, g * 512:(g + 1) * 512], qpe_f[:, 0, :], qpe_f[:, 1, :]), inc="v")
                        r.wait("tensor", "v")
                    r.op(V, lambda E: E.reduce_max(mxq[:, 0:1], knope[:], axis=AX.X,
                                                   apply_absolute_value=True))
                    r.op(V, lambda E: E.reduce_max(mxq[0:64, 2:3], krope2[:], axis=AX.X,
                                                   apply_absolute_value=True), inc="v")
                    for j in (0, 2):
                        r.wait("tensor", "v")
                        r.op("tensor", lambda E, j=j: E.transpose(
                            bank(0)[0:1, 0:128] if j == 0 else bank(0)[0:1, 0:64],
                            mxq[:, j:j + 1] if j == 0 else mxq[0:64, j:j + 1],
                            ident[:] if j == 0 else ident[0:64, 0:64]), inc="t")
                        r.wait(V, "t")
                        r.op(V, lambda E, j=j: E.reduce_max(
                            msc[:, j:j + 1], bank(0)[0:1, 0:128] if j == 0 else bank(0)[0:1, 0:64],
                            axis=AX.X), inc="v")
                    r.op(V, lambda E: E.memset(msc[:, 7:8], 0.0), inc="v")
                    for g in range(4):
                        gsl = slice(g * 512, (g + 1) * 512)
                        r.wait(Sc, "v")
                        r.wait(Sc, "t")
                        r.op(Sc, lambda E, gsl=gsl: E.activation(sqscr[:, 0, :], qnope[:, gsl],
                                                                 AF.Square), inc="s")
                        r.op(Sc, lambda E, gsl=gsl: E.activation(sqscr[0:64, 1, :], qpe[:, gsl],
                                                                 AF.Square), inc="s")
                        r.wait("tensor", "s")
                        r.wait("tensor", "v")
                        r.op("tensor", lambda E: E.matmul(bank(1)[0:1, :], ones_col[:],
                                                          sqscr[:, 0, :], start=True, stop=False))
                        r.op("tensor", lambda E: E.matmul(bank(1)[0:1, :], ones_col[0:64, :],
                                                          sqscr[0:64, 1, :], start=False, stop=True),
                             inc="t")
                        r.wait(V, "t")
                        r.op(V, lambda E: E.reduce_max(msc[:, 6:7], bank(1)[0:1, :], axis=AX.X))
                        r.op(V, lambda E: E.tensor_max(msc[:, 7:8], msc[:, 7:8], msc[:, 6:7]), inc="v")
                    r.op(V, lambda E: E.tensor_mul(msc[:, 4:5], msc[:, 0:1], msc[:, 0:1]))
                    r.op(V, lambda E: E.tensor_scalar_mul(msc[:, 4:5], msc[:, 4:5], 128.0))
                    r.op(V, lambda E: E.tensor_mul(msc[:, 5:6], msc[:, 2:3], msc[:, 2:3]))
                    r.op(V, lambda E: E.tensor_scalar_mul(msc[:, 5:6], msc[:, 5:6], 64.0))
                    r.op(V, lambda E: E.tensor_add(msc[:, 4:5], msc[:, 4:5], msc[:, 5:6]))
                    r.op(V, lambda E: E.tensor_mul(msc[:, 4:5], msc[:, 4:5], msc[:, 7:8]), inc="v")
                    r.wait(Sc, "v")
                    r.op(Sc, lambda E: E.activation(msc[:, 4:5], msc[:, 4:5], AF.Sqrt), inc="s")
                    r.wait(V, "s")
                    r.op(V, lambda E: E.tensor_single_scalar(biasr[:], maskrow[:], 0.0, ALU.add))
                    r.op(V, lambda E: E.tensor_scalar_mul(msc[:, 4:5], msc[:, 4:5], -1.0))
                    r.op(V, lambda E: E.tensor_scalar_add(biasr[:], biasr[:], msc[:, 4:5]), inc="v")
                    r.wait("tensor", "v")
                    r.op("tensor", lambda E: E.matmul(bank(0)[:, 0:128], ones_row[:], biasr[:],
                                                      start=True, stop=True), inc="t")
                    r.wait(V, "t")
                    r.op(V, lambda E: E.tensor_copy(bias_h[:], bank(0)[:, 0:128]), inc="v")

                    r.wait(Sc, "v")
                    r.wait("tensor", "v")
                    exp_gp = {}

                    av_gp = {}
                    P = 0
                    ev_v = r.cnt["v"]
                    for j in range(4):
                        nkt = 20 + 4 * j
                        npair = nkt // 2
                        qsl = slice(j * 512, (j + 1) * 512)
                        mask_v = {}

                        def emit_av(Pa, pa):
                            pb_ = Pa % 2
                            kt0_ = 2 * pa
                            if pa == 0:
                                r.wait("tensor", "v", ev_v)
                            if Pa in mask_v:
                                r.wait("tensor", "v", mask_v[Pa])
                            else:
                                r.wait("tensor", "s", exp_gp[Pa])
                            for q in (0, 1):
                                kt = kt0_ + q
                                psl = slice(pb_ * 1024 + q * 512, pb_ * 1024 + (q + 1) * 512)
                                r.op("tensor", lambda E, kt=kt, nkt=nkt, psl=psl: E.matmul(
                                    bank(4)[0:65, :], vext[:, kt, 0:65], ptile[:, psl],
                                    start=(kt == 0), stop=(kt == nkt - 1)))
                                r.op("tensor", lambda E, kt=kt, nkt=nkt, psl=psl: E.matmul(
                                    bank(5)[0:65, :], vext[:, kt, 65:130], ptile[:, psl],
                                    start=(kt == 0), stop=(kt == nkt - 1)),
                                    inc=("t" if q == 1 else None))
                            av_gp[Pa] = r.cnt["t"]

                        for p in range(npair):
                            kt0 = 2 * p
                            pb = P % 2
                            if P >= 2:
                                r.wait("tensor", "s", exp_gp[P - 2])
                            for q in (0, 1):
                                kt = kt0 + q
                                sb_ = bank(2 * pb + q)
                                r.op("tensor", lambda E, kt=kt, qsl=qsl, sb_=sb_: E.matmul(
                                    sb_, knope[:, kt * 128:(kt + 1) * 128], qnope[:, qsl],
                                    start=True, stop=False))
                                r.op("tensor", lambda E, kt=kt, qsl=qsl, sb_=sb_: E.matmul(
                                    sb_, krope2[:, kt * 128:(kt + 1) * 128], qpe[:, qsl],
                                    start=False, stop=True), inc=("t" if q == 1 else None))
                            smm = r.cnt["t"]
                            r.wait(Sc, "t", smm)
                            if P >= 2:
                                r.wait(Sc, "t", av_gp[P - 2])
                            bcol = j * 32 + kt0
                            r.op(Sc, lambda E, pb=pb, bcol=bcol: E.activation(
                                ptile[:, pb * 1024:(pb + 1) * 1024], PS[:, pb * 1024:(pb + 1) * 1024],
                                AF.Exp, bias=bias_h[:, bcol:bcol + 1]), inc="s")
                            exp_gp[P] = r.cnt["s"]
                            dd0 = kt0 - (16 + 4 * j)
                            if dd0 >= 0:
                                r.wait(V, "s", exp_gp[P])
                                for q in (0, 1):
                                    r.op(V, lambda E, pb=pb, q=q, dd0=dd0: E.tensor_tensor(
                                        ptile[:, pb * 1024 + q * 512:pb * 1024 + (q + 1) * 512],
                                        ptile[:, pb * 1024 + q * 512:pb * 1024 + (q + 1) * 512],
                                        m01[:, dd0 + q, :], ALU.mult),
                                        inc=("v" if q == 1 else None))
                                mask_v[P] = r.cnt["v"]
                            if p > 0:
                                emit_av(P - 1, p - 1)
                            P += 1
                        emit_av(P - 1, npair - 1)
                        r.wait(V, "t", av_gp[P - 1])
                        r.op(V, lambda E: E.tensor_copy(rows_rms[:, 512:1024], bank(4)[64:65, :]))
                        r.op(V, lambda E: E.tensor_scalar_add(rows_rms[:, 512:1024],
                                                              rows_rms[:, 512:1024], 1e-38))
                        r.wait(V, "v")
                        r.op(V, lambda E: E.reciprocal(rows_rms[:, 512:1024], rows_rms[:, 512:1024]),
                             inc="v")
                        r.wait("sync", "v")
                        r.op("sync", lambda E: E.dma_start(out=dsc, in_=rows_rms[0:1, 512:1024]),
                             inc="qs", k=16)
                        r.wait("sync", "qs")
                        r.op("sync", lambda E: E.dma_start(out=rbref[:],
                                                           in_=dsc.broadcast_to((128, 512))),
                             inc="qs", k=16)
                        r.wait(V, "qs")
                        r.wait(V, "so")
                        r.op(V, lambda E, j=j: E.scalar_tensor_tensor(
                            att_t[0:64, 0, :], rbref[0:64, :], 1.0, bank(4)[0:64, :],
                            ALU.mult, ALU.mult))
                        r.op(V, lambda E, j=j: E.scalar_tensor_tensor(
                            att_t[64:128, 0, :], rbref[64:128, :], 1.0, bank(5)[0:64, :],
                            ALU.mult, ALU.mult), inc="v")
                        ev_v = r.cnt["v"]
                        r.wait("sync", "v")
                        r.op("sync", lambda E, h=h, j=j: E.dma_start(
                            out=asp3[h, :, j * 512:(j + 1) * 512], in_=att_t[:, 0, :]),
                            inc="so", k=16)
                    head_ms[h] = r.cnt["t"]

                woT3 = woT.rearrange("(c p) d -> c p d", p=128)
                for qg in range(4):
                    r.wait("gpsimd", "so")
                    r.wait("gpsimd", "t")
                    for hc in range(NCH):
                        r.op("gpsimd", lambda E, qg=qg, hc=hc: E.dma_start(
                            out=att_qg[:, hc, :], in_=asp3[hc, :, qg * 512:(qg + 1) * 512]),
                            inc="qg", k=16)
                    for dh in range(2):
                        r.wait("gpsimd", "t")
                        for hc in range(NCH):
                            r.op("gpsimd", lambda E, qg=qg, dh=dh, hc=hc: E.dma_start(
                                out=wo_s[:, hc, :], in_=woT3[hc, :, dh * 1024:(dh + 1) * 1024]),
                                inc="qg", k=16)
                        r.wait("tensor", "qg")
                        if qg > 0 or dh > 0:
                            r.wait("tensor", "v")
                        for dblk in range(8):
                            for hc in range(NCH):
                                r.op("tensor", lambda E, dblk=dblk, hc=hc: E.matmul(
                                    bank(dblk), wo_s[:, hc, dblk * 128:(dblk + 1) * 128],
                                    att_qg[:, hc, :], start=(hc == 0), stop=(hc == NCH - 1)),
                                    inc=("t" if hc == NCH - 1 else None))
                        for dblk in range(8):
                            c = dh * 8 + dblk
                            if dblk >= 2 or dh > 0 or qg > 0:
                                r.wait("sync", "v")
                            r.op("sync", lambda E, c=c, qg=qg, dblk=dblk: E.dma_start(
                                out=x32k[:, dblk % 2, :],
                                in_=xkv3[c, :, TOK + qg * 512:TOK + (qg + 1) * 512]), inc="qs", k=16)
                            r.wait(V, "qs")
                            r.wait(V, "t")
                            r.op(V, lambda E, c=c, dblk=dblk: E.scalar_tensor_tensor(
                                pre[:, c, :], x32k[:, dblk % 2, :], 1.0, bank(dblk),
                                ALU.mult, ALU.add), inc="v")
                    r.wait("scalar", "v")
                    _ln_emit(r, pre, NCH, sqscr, PS[0:1, 2048:2560], PS[0:1, 2560:3072],
                             ones_col, ones_row, rowscr_f, abc, bbc,
                             [PS[:, 3072:3584], PS[:, 3584:4096]], 512)
                    r.wait("sync", "v")
                    for c in range(NCH):
                        r.op("sync", lambda E, c=c, qg=qg: E.dma_start(
                            out=x33[c, :, qg * 512:(qg + 1) * 512], in_=pre[:, c, :]), inc="so", k=16)
                r.wait("sync", "so")

        def _phase_D():
            with ExitStack() as ctx:
                sb = lambda n, s, d=F32: ctx.enter_context(nc.sbuf_tensor("D_" + n, s, d))
                xbf = sb("xbf", [128, NCH, TOK], BF16)
                sgn = sb("sgn", [128, NCH, D], BF16)
                wld = sb("wld", [128, 2, D])
                alpha = sb("alpha", [128, NCH])
                bias_t = sb("bias_t", [128, NCH])
                x32 = sb("x32", [128, 2, 512])
                scr = sb("scr", [128, 2048])
                pre = sb("pre", [128, NCH, 512])
                sqscr = sb("sqscr", [128, 2, 512])
                rowscr = sb("rowscr", [1, 2048])
                abc = sb("abc", [128, 512])
                bbc = sb("bbc", [128, 512])
                ones_col = sb("ones_col", [128, 1])
                ones_row = sb("ones_row", [1, 128])

                bank = lambda i: PS[:, 512 * i:512 * (i + 1)]

                V, Sc = "vector", "scalar"

                r.op("sync", lambda E: E.dma_start(out=bias_t[:], in_=bcol), inc="qs", k=16)
                r.wait(V, "qs")
                r.op(V, lambda E: E.memset(ones_col[:], 1.0))
                r.op(V, lambda E: E.memset(ones_row[:], 1.0))

                ld_ms = {}
                for c in range(NCH):
                    if c >= 2:
                        r.wait("sync", "v", ld_ms[c - 2])
                        r.wait("gpsimd", "v", ld_ms[c - 2])
                    r.op("sync", lambda E, c=c: E.dma_start(out=wld[:, c % 2, :], in_=wo3[c]),
                         inc="qs", k=16)
                    r.op("gpsimd", lambda E, c=c: E.dma_start(out=xbf[:, c, :], in_=x33[c]),
                         inc="qg", k=16)
                    r.wait(V, "qs")
                    r.op(V, lambda E, c=c: E.tensor_reduce(alpha[:, c:c + 1], wld[:, c % 2, :],
                                                           AX.X, ALU.add, apply_absolute_value=True),
                         inc="v")
                    ld_ms[c] = r.cnt["v"]
                for c in range(NCH):
                    r.wait("sync", "v")
                    r.op("sync", lambda E, c=c: E.dma_start(out=wld[:, c % 2, :], in_=wT3[c]),
                         inc="qs", k=16)
                    r.wait(V, "qs")
                    r.op(V, lambda E, c=c: E.tensor_single_scalar(scr[:], wld[:, c % 2, :],
                                                                  0.0, ALU.is_ge))
                    r.op(V, lambda E, c=c: E.tensor_scalar(sgn[:, c, :], scr[:], 2.0, -1.0,
                                                           ALU.mult, ALU.add), inc="v")
                r.op(V, lambda E: E.tensor_scalar_mul(alpha[:], alpha[:], 1.0 / float(D)), inc="v")

                r.wait("tensor", "qg")
                r.wait("tensor", "v")
                for tg in range(4):
                    for dblk in range(NCH):
                        if tg + dblk > 0:
                            r.wait("tensor", "v")
                        for c in range(NCH):
                            r.op("tensor", lambda E, c=c, dblk=dblk, tg=tg: E.matmul(
                                bank(dblk % 2), sgn[:, c, dblk * 128:(dblk + 1) * 128],
                                xbf[:, c, tg * 512:(tg + 1) * 512],
                                start=(c == 0), stop=(c == NCH - 1)), inc=("t" if c == NCH - 1 else None))
                        if dblk >= 2:
                            r.wait("sync", "v")
                        r.op("sync", lambda E, dblk=dblk, tg=tg: E.dma_start(
                            out=x32[:, dblk % 2, :], in_=x33[dblk, :, tg * 512:(tg + 1) * 512]),
                            inc="qs", k=16)
                        r.wait(V, "qs")
                        r.wait(V, "t")
                        r.op(V, lambda E, dblk=dblk: E.tensor_scalar(
                            pre[:, dblk, :], bank(dblk % 2), alpha[:, dblk:dblk + 1],
                            bias_t[:, dblk:dblk + 1], ALU.mult, ALU.add))
                        r.op(V, lambda E, dblk=dblk: E.tensor_add(
                            pre[:, dblk, :], pre[:, dblk, :], x32[:, dblk % 2, :]), inc="v")
                    r.wait("scalar", "v")
                    _ln_emit(r, pre, NCH, sqscr, PS[0:1, 2048:2560], PS[0:1, 2560:3072],
                             ones_col, ones_row, rowscr, abc, bbc,
                             [PS[:, 3072:3584], PS[:, 3584:4096]], 512)
                    r.wait("sync", "v")
                    if tg > 0:
                        pass
                    for c in range(NCH):
                        r.op("sync", lambda E, c=c, tg=tg: E.dma_start(
                            out=x43[c, :, tg * 512:(tg + 1) * 512], in_=pre[:, c, :]), inc="so", k=16)
                    r.wait("vector", "so")
                r.wait("sync", "so")

        _prologue()
        _phase_A()
        barrier()
        _phase_B()
        barrier()
        _exchange()
        barrier()
        _phase_C()
        barrier()
        _phase_D()
        r.wait("sync", "so")
        r.play(block)
    return nc



def _prep_launch_A(inputs):
    u = np.asarray(inputs["L0_attn_u"], dtype=np.float32)
    small = {
        "lam_ur": np.ascontiguousarray(np.asarray(inputs["L0_attn_Lambda_unconstrained_re"],
                                                  np.float32).reshape(1, 32)),
        "lam_im": np.ascontiguousarray(np.asarray(inputs["L0_attn_Lambda_im"],
                                                  np.float32).reshape(1, 32)),
        "logdt": np.ascontiguousarray(np.asarray(inputs["L0_attn_log_Delta"],
                                                 np.float32).reshape(1, 64)),
        "bre": np.ascontiguousarray(np.asarray(inputs["L0_attn_B_re"], np.float32)),
        "bim": np.ascontiguousarray(np.asarray(inputs["L0_attn_B_im"], np.float32)),
        "cret": np.ascontiguousarray(np.asarray(inputs["L0_attn_C_re"], np.float32).T),
        "cimt": np.ascontiguousarray(np.asarray(inputs["L0_attn_C_im"], np.float32).T),
        "dcol": np.ascontiguousarray(np.asarray(inputs["L0_attn_D"], np.float32)
                                     .reshape(NCH, 128).T),
    }
    in_maps = []
    for cid in range(NCORE):
        b, h = cid // 2, cid % 2
        ut = np.ascontiguousarray(u[b].T)
        if h == 0:
            u_pref = np.concatenate([np.zeros((D, TOK), np.float32), ut[:, 0:TOK]], axis=1)
        else:
            u_pref = ut
        m = dict(small)
        m["u_pref"] = np.ascontiguousarray(u_pref)
        in_maps.append(m)
    return in_maps


def run_launch_A(inputs, dbg=False):
    nc = build_launch_A(dbg)
    in_maps = _prep_launch_A(inputs)
    res = run_bass_kernel_spmd(nc, in_maps, core_ids=list(range(NCORE)))
    if dbg:
        return res.results
    return [res.results[cid]["x1"] for cid in range(NCORE)]


def _prep_launch_B(inputs, x1_shards):
    base = {
        "rwt": np.ascontiguousarray(np.asarray(inputs["L0_ffn_router_w"], np.float32).T),
        "w1": np.ascontiguousarray(np.asarray(inputs["L0_ffn_w1"], np.float32)),
        "w2": np.ascontiguousarray(np.asarray(inputs["L0_ffn_w2"], np.float32)),
        "w3": np.ascontiguousarray(np.asarray(inputs["L0_ffn_w3"], np.float32)),
    }
    in_maps = []
    for cid in range(NCORE):
        m = dict(base)
        m["x1"] = np.ascontiguousarray(x1_shards[cid])
        in_maps.append(m)
    return in_maps


def run_launch_B(inputs, x1_shards, dbg=False):
    nc = build_launch_B(dbg)
    in_maps = _prep_launch_B(inputs, x1_shards)
    res = run_bass_kernel_spmd(nc, in_maps, core_ids=list(range(NCORE)))
    if dbg:
        return res.results
    return [res.results[cid]["x2"] for cid in range(NCORE)]


def _prep_launch_C(inputs, x2_shards):
    H, ROPE = 16, 64
    perm64 = np.concatenate([np.arange(0, 64, 2), np.arange(1, 64, 2)])
    wq_rope = np.asarray(inputs["L1_attn_wq_rope"], np.float32)
    perm_rows_q = np.concatenate([h * 64 + perm64 for h in range(H)])
    wk_rope = np.asarray(inputs["L1_attn_wk_rope"], np.float32)[perm64, :]
    base = {
        "wqdT": np.ascontiguousarray(np.asarray(inputs["L1_attn_wq_down"], np.float32).T),
        "qnw": np.ascontiguousarray(np.asarray(inputs["L1_attn_q_norm_w"], np.float32)
                                    .reshape(8, 128).T),
        "wquT": np.ascontiguousarray(np.asarray(inputs["L1_attn_wq_up"], np.float32).T),
        "wqrT": np.ascontiguousarray(wq_rope[perm_rows_q, :].T),
        "wkvdT": np.ascontiguousarray(np.asarray(inputs["L1_attn_wkv_down"], np.float32).T),
        "kvnw": np.ascontiguousarray(np.asarray(inputs["L1_attn_kv_norm_w"], np.float32)
                                     .reshape(4, 128).T),
        "wkvuT": np.ascontiguousarray(np.asarray(inputs["L1_attn_wkv_up"], np.float32).T),
        "wkrT": np.ascontiguousarray(wk_rope.T),
        "woT": np.ascontiguousarray(np.asarray(inputs["L1_attn_wo"], np.float32).T),
    }
    in_maps = []
    for cid in range(NCORE):
        b, h = cid // 2, cid % 2
        m = dict(base)
        own = x2_shards[2 * b + h]
        if h == 0:
            xkv = np.concatenate([np.zeros((D, TOK), np.float32), own], axis=1)
            posv = np.concatenate([np.zeros(TOK, np.float32),
                                   np.arange(TOK, dtype=np.float32)])
            maskrow = np.zeros((1, 128), np.float32)
            for j in range(4):
                maskrow[0, j * 32:j * 32 + 16] = BIGNEG
        else:
            first = x2_shards[2 * b + 0]
            xkv = np.concatenate([first, own], axis=1)
            posv = np.arange(S, dtype=np.float32)
            maskrow = np.zeros((1, 128), np.float32)
        m["xkv"] = np.ascontiguousarray(xkv)
        m["pos"] = np.ascontiguousarray(posv.reshape(1, S))
        m["maskrow"] = maskrow
        in_maps.append(m)
    return in_maps


def run_launch_C(inputs, x2_shards, dbg=False):
    nc = build_launch_C(dbg)
    in_maps = _prep_launch_C(inputs, x2_shards)
    res = run_bass_kernel_spmd(nc, in_maps, core_ids=list(range(NCORE)))
    if dbg:
        return res.results
    return [res.results[cid]["x3"] for cid in range(NCORE)]


def _weights_B(inputs):
    return {
        "rwt": np.ascontiguousarray(np.asarray(inputs["L0_ffn_router_w"], np.float32).T),
        "w1": np.ascontiguousarray(np.asarray(inputs["L0_ffn_w1"], np.float32)),
        "w2": np.ascontiguousarray(np.asarray(inputs["L0_ffn_w2"], np.float32)),
        "w3": np.ascontiguousarray(np.asarray(inputs["L0_ffn_w3"], np.float32)),
    }


def _weights_D(inputs):
    w = np.asarray(inputs["L1_ffn_weight"], np.float32)
    return {
        "w_o": np.ascontiguousarray(w),
        "wT": np.ascontiguousarray(w.T),
        "bcol": np.ascontiguousarray(np.asarray(inputs["L1_ffn_bias"], np.float32)
                                     .reshape(NCH, 128).T),
    }


def _prep_single(inputs):
    in_maps = _prep_launch_A(inputs)
    wb = _weights_B(inputs)
    wd = _weights_D(inputs)
    H, ROPE = 16, 64
    perm64 = np.concatenate([np.arange(0, 64, 2), np.arange(1, 64, 2)])
    wq_rope = np.asarray(inputs["L1_attn_wq_rope"], np.float32)
    perm_rows_q = np.concatenate([h * 64 + perm64 for h in range(H)])
    wk_rope = np.asarray(inputs["L1_attn_wk_rope"], np.float32)[perm64, :]
    wc = {
        "wqdT": np.ascontiguousarray(np.asarray(inputs["L1_attn_wq_down"], np.float32).T),
        "qnw": np.ascontiguousarray(np.asarray(inputs["L1_attn_q_norm_w"], np.float32)
                                    .reshape(8, 128).T),
        "wquT": np.ascontiguousarray(np.asarray(inputs["L1_attn_wq_up"], np.float32).T),
        "wqrT": np.ascontiguousarray(wq_rope[perm_rows_q, :].T),
        "wkvdT": np.ascontiguousarray(np.asarray(inputs["L1_attn_wkv_down"], np.float32).T),
        "kvnw": np.ascontiguousarray(np.asarray(inputs["L1_attn_kv_norm_w"], np.float32)
                                     .reshape(4, 128).T),
        "wkvuT": np.ascontiguousarray(np.asarray(inputs["L1_attn_wkv_up"], np.float32).T),
        "wkrT": np.ascontiguousarray(wk_rope.T),
        "woT": np.ascontiguousarray(np.asarray(inputs["L1_attn_wo"], np.float32).T),
    }
    nval = int(np.random.default_rng().integers(1 << 20, 1 << 30))
    for cid, m in enumerate(in_maps):
        h = cid % 2
        m.update(wb)
        m.update(wc)
        m.update(wd)
        if h == 0:
            posv = np.concatenate([np.zeros(TOK, np.float32),
                                   np.arange(TOK, dtype=np.float32)])
            maskrow = np.zeros((1, 128), np.float32)
            for j in range(4):
                maskrow[0, j * 32:j * 32 + 16] = BIGNEG
        else:
            posv = np.arange(S, dtype=np.float32)
            maskrow = np.zeros((1, 128), np.float32)
        m["pos"] = np.ascontiguousarray(posv.reshape(1, S))
        m["maskrow"] = maskrow
        m["hflag"] = np.full((1, 4), h, np.int32)
        m["nonce"] = np.full((1, 4), nval, np.int32)
    return in_maps


def kernel(**inputs):
    in_maps = _prep_single(inputs)
    res = run_bass_kernel_spmd(build_fused_single(), in_maps,
                               core_ids=list(range(NCORE)))
    x4s = [res.results[cid]["x4"] for cid in range(NCORE)]
    out = np.empty((B, S, D), np.float32)
    for cid in range(NCORE):
        b, h = cid // 2, cid % 2
        out[b, h * TOK:(h + 1) * TOK, :] = x4s[cid].T
    return out


def kernel_2launch(**inputs):
    in1 = _prep_launch_A(inputs)
    wb = _weights_B(inputs)
    for m in in1:
        m.update(wb)
    res1 = run_bass_kernel_spmd(build_fused_AB(), in1, core_ids=list(range(NCORE)))
    x2s = [res1.results[cid]["x2"] for cid in range(NCORE)]
    in2 = _prep_launch_C(inputs, x2s)
    wd = _weights_D(inputs)
    for m in in2:
        m.update(wd)
    res2 = run_bass_kernel_spmd(build_fused_CD(), in2, core_ids=list(range(NCORE)))
    x4s = [res2.results[cid]["x4"] for cid in range(NCORE)]
    out = np.empty((B, S, D), np.float32)
    for cid in range(NCORE):
        b, h = cid // 2, cid % 2
        out[b, h * TOK:(h + 1) * TOK, :] = x4s[cid].T
    return out


def kernel_multilaunch(**inputs):
    x1s = run_launch_A(inputs)
    x2s = run_launch_B(inputs, x1s)
    x3s = run_launch_C(inputs, x2s)
    x4s = run_launch_D(inputs, x3s)
    out = np.empty((B, S, D), np.float32)
    for cid in range(NCORE):
        b, h = cid // 2, cid % 2
        out[b, h * TOK:(h + 1) * TOK, :] = x4s[cid].T
    return out
